# version 11
# baseline (speedup 1.0000x reference)
import numpy as np
import ml_dtypes
import concourse.bass as bass
import concourse.mybir as mybir
from concourse.bass_utils import run_bass_kernel_spmd

F32 = mybir.dt.float32
BF16 = mybir.dt.bfloat16
AF = mybir.ActivationFunctionType
ALU = mybir.AluOpType
AX = mybir.AxisListType
BF = ml_dtypes.bfloat16

B, S, D = 4, 4096, 2048
NH, DQK, DPE, DV = 16, 128, 64, 128
QHD = DQK + DPE
LATD = 512
NE, HID = 8, 1024
NKV, HD = 4, 128
FF = 8192
TT = 512
NTT = S // TT
NTO = 2048 // TT
EPS = 1e-5
SC0 = float(1.0 / np.sqrt(QHD))
SC1 = float(1.0 / np.sqrt(HD))
NEGBIG = -1.0e30

ALL_PHASES = ("p0", "p2", "p3", "p4", "p4b", "p5", "p9", "p7", "p8", "p10")

DRAM_SPECS = {
    "Q0":  ([NH * QHD, S], BF16, "p0", ("p2",)),
    "LAT": ([LATD + DPE, S], BF16, "p0", ("p2",)),
    "QS0": ([1, 128], F32, "p0", ("p2",)),
    "CT0": ([D, S], BF16, "p2", ("p3",)),
    "X1":  ([D, S], BF16, "p3", ("p4", "p4b")),
    "WG8": ([8, S], BF16, "p3", ("p4",)),
    "ACC": ([D, S], F32, "p4", ("p4b",)),
    "X2":  ([D, S], BF16, "p4b", ("p5", "p8")),
    "Q1":  ([D, 2048], BF16, "p5", ("p7",)),
    "K1":  ([NKV * HD, S], BF16, "p5", ("p7",)),
    "V1":  ([S, NKV * HD], BF16, "p5", ("p7",)),
    "QS1": ([1, 96], F32, "p5", ("p7",)),
    "CT1": ([D, 2048], BF16, "p7", ("p8",)),
    "X3":  ([D, 2048], BF16, "p8", ("p10",)),
    "FS":  ([FF, D], BF16, "p9", ()),
    "W1B": ([D, FF], BF16, "p9", ("p10",)),
    "W3B": ([D, FF], BF16, "p9", ("p10",)),
    "W2X": ([16, 64, 128, 128], BF16, "p9", ("p10",)),
    "OUT": ([D, 2048], F32, "p10", ()),
}

IN_SPECS = {
    "xT":   ([D, S], F32, ("p0", "p3")),
    "qwT":  ([D, NH * DQK], F32, ("p0",)),
    "qwTpe": ([D, NH * DPE], F32, ("p0",)),
    "kvdT": ([D, LATD + DPE], F32, ("p0",)),
    "lng":  ([128, 4], F32, ("p0",)),
    "lnb":  ([128, 4], F32, ("p0",)),
    "r0k":  ([32, 2 * S], F32, ("p0",)),
    "r0q":  ([64, 2 * S], F32, ("p0",)),
    "kvuT": ([LATD, NH * (DQK + DV)], F32, ("p2",)),
    "owT":  ([D, D], F32, ("p3",)),
    "gT":   ([D, 8], F32, ("p3",)),
    "ebc":  ([128, 8], F32, ("p3",)),
    "w1T":  ([NE, D, HID], F32, ("p4",)),
    "w2T":  ([NE, HID, D], F32, ("p4",)),
    "w3T":  ([NE, D, HID], F32, ("p4",)),
    "sw1T": ([D, HID], F32, ("p4",)),
    "sw2T": ([HID, D], F32, ("p4",)),
    "sw3T": ([D, HID], F32, ("p4",)),
    "wqT":  ([D, D], F32, ("p5",)),
    "wkT":  ([D, NKV * HD], F32, ("p5",)),
    "wvT":  ([D, NKV * HD], F32, ("p5",)),
    "r1k":  ([128, 4 * S], F32, ("p5",)),
    "r1q":  ([128, 4 * 2048], F32, ("p5",)),
    "woT":  ([D, D], F32, ("p8",)),
    "W2T":  ([FF, D], F32, ("p9",)),
    "W1T":  ([D, FF], F32, ("p9",)),
    "W3T":  ([D, FF], F32, ("p9",)),
}


def _hadamard(n):
    h = np.array([[1.0]], dtype=np.float32)
    while h.shape[0] < n:
        h = np.block([[h, h], [h, -h]])
    return h


class Sem:
    __slots__ = ("h", "v")

    def __init__(self, nc, name):
        self.h = nc.alloc_semaphore(name=name)
        self.v = 0


class Tok:
    __slots__ = ("sem", "val")

    def __init__(self, sem, val):
        self.sem = sem
        self.val = val


class Sched:

    ENGINES = ("gpsimd", "sync", "tensor", "scalar", "vector")

    def __init__(self, kb):
        self.kb = kb
        self.qs = {e: [] for e in self.ENGINES}
        self.seq = 0

    def op(self, engine, fn, wait=(), inc=None, n=1, ni=1):
        tok = None
        if inc is not None:
            inc.v += n * ni
            tok = Tok(inc, inc.v)
        waits = [(w.sem, w.val) for w in wait if w is not None]
        self.qs[engine].append((fn, waits, inc, n, ni, self.seq))
        self.seq += 1
        return tok

    def merge(self, other):
        ta, tb = max(self.seq, 1), max(other.seq, 1)
        for e in self.ENGINES:
            merged = sorted(
                [(op[-1] / ta, 0, k) for k, op in enumerate(self.qs[e])]
                + [(op[-1] / tb, 1, k) for k, op in enumerate(other.qs[e])])
            self.qs[e] = [(self.qs[e] if ph == 0 else other.qs[e])[k]
                          for _, ph, k in merged]

    def emit(self, blk):
        for name in self.ENGINES:
            ops = self.qs[name]
            if not ops:
                continue
            dec = getattr(blk, name)

            def run(eng, ops=ops):
                seen = {}
                for fn, waits, inc, n, ni, _ in ops:
                    for sem, val in waits:
                        if seen.get(id(sem), -1) >= val:
                            continue
                        eng.wait_ge(sem.h, val)
                        seen[id(sem)] = val
                    inst = fn(eng) if fn is not None else None
                    if inc is not None:
                        insts = inst if isinstance(inst, (list, tuple)) else [inst]
                        assert len(insts) == ni and all(i is not None for i in insts)
                        for i in insts:
                            i.then_inc(inc.h, n)

            dec(run)


class KB:

    def __init__(self, phases):
        self.phases = list(phases)
        nc = bass.Bass("TRN2", target_bir_lowering=False)
        self.nc = nc
        pset = set(self.phases)
        self.d = {}
        self.in_names = []
        self.out_names = []
        for name, (shape, dt, consumers) in IN_SPECS.items():
            if pset & set(consumers):
                self.d[name] = nc.dram_tensor(name, shape, dt, kind="ExternalInput").ap()
                self.in_names.append(name)
        for name, (shape, dt, prod, cons) in DRAM_SPECS.items():
            inside_p = prod in pset
            needed = pset & set(cons)
            later = [c for c in cons if c not in pset]
            if inside_p and (name == "OUT" or later):
                self.d[name] = nc.dram_tensor(name, shape, dt, kind="ExternalOutput").ap()
                self.out_names.append(name)
            elif inside_p:
                self.d[name] = nc.dram_tensor(name, shape, dt, kind="Internal").ap()
            elif needed:
                self.d[name] = nc.dram_tensor(name, shape, dt, kind="ExternalInput").ap()
                self.in_names.append(name)
        self.ones_bf = nc.inline_tensor(np.ones((128, 1), dtype=BF), name="c_ones").ap()
        self.ones_row = nc.inline_tensor(np.ones((1, 128), dtype=BF), name="c_onesr").ap()
        sel = np.zeros((8, 8 * 128), dtype=BF)
        for e in range(8):
            sel[e, e * 128:(e + 1) * 128] = 1
        self.sel8 = nc.inline_tensor(sel, name="c_sel8").ap()
        self.ident = nc.inline_tensor(np.eye(128, dtype=BF), name="c_ident").ap()
        self.h64 = nc.inline_tensor(_hadamard(64).astype(BF), name="c_h64").ap()
        self.epsc = nc.inline_tensor(np.full((1, 1), EPS, np.float32), name="c_eps").ap()
        self.h128 = nc.inline_tensor(_hadamard(128).astype(BF), name="c_h128").ap()
        self._sems = {}

    def sem(self, name):
        if name not in self._sems:
            self._sems[name] = Sem(self.nc, name)
        return self._sems[name]


def chunked(dram_ap, cols=None):
    ap = dram_ap if cols is None else dram_ap[:, cols[0]:cols[1]]
    return ap.rearrange("(c p) o -> p c o", p=128)



def emit_hi_lo(v, src_f32, hi_bf, lo_bf):
    v.tensor_copy(out=hi_bf, in_=src_f32)
    v.drain()
    return v.tensor_tensor(out=lo_bf, in0=src_f32, in1=hi_bf, op=ALU.subtract)


def ln_tail(sc, kb, *, nfeat, PMp, PSp, st_f, hl_b, onesr_sb, PBC, mbc_sb, rbc_sb,
            stat_tok, bc_war_tok, MM, EV, AC, eps_ap):
    inv = 1.0 / float(nfeat)

    def stats(v):
        v.tensor_scalar(out=st_f[:, 0], in0=PMp[:], scalar1=inv, scalar2=None, op0=ALU.mult)
        v.tensor_scalar(out=st_f[:, 1], in0=PSp[:], scalar1=inv, scalar2=None, op0=ALU.mult)
        v.drain()
        v.tensor_tensor(out=st_f[:, 2], in0=st_f[:, 0], in1=st_f[:, 0], op=ALU.mult)
        v.drain()
        return v.tensor_tensor(out=st_f[:, 3], in0=st_f[:, 1], in1=st_f[:, 2], op=ALU.subtract)

    t_stats = sc.op("vector", stats, wait=[stat_tok], inc=EV)
    t_sqrt = sc.op("scalar",
                   lambda s: s.activation(out=st_f[:, 1], in_=st_f[:, 3], func=AF.Sqrt, bias=eps_ap),
                   wait=[t_stats], inc=AC)
    t_rec = sc.op("vector", lambda v: v.reciprocal(out=st_f[:, 2], in_=st_f[:, 1]),
                  wait=[t_sqrt], inc=EV)

    def hilo(v):
        emit_hi_lo(v, st_f[:, 0], hl_b[:, 0], hl_b[:, 1])
        v.drain()
        return emit_hi_lo(v, st_f[:, 2], hl_b[:, 2], hl_b[:, 3])

    t_hl = sc.op("vector", hilo, wait=[t_rec], inc=EV)

    def bc1(t):
        t.matmul(PBC[:], onesr_sb[:], hl_b[:, 0], start=True, stop=False)
        return t.matmul(PBC[:], onesr_sb[:], hl_b[:, 1], start=False, stop=True)

    t_b1 = sc.op("tensor", bc1, wait=[t_hl, bc_war_tok], inc=MM)
    t_e1 = sc.op("vector", lambda v: v.tensor_copy(out=mbc_sb[:], in_=PBC[:]),
                 wait=[t_b1], inc=EV)

    def bc2(t):
        t.matmul(PBC[:], onesr_sb[:], hl_b[:, 2], start=True, stop=False)
        return t.matmul(PBC[:], onesr_sb[:], hl_b[:, 3], start=False, stop=True)

    t_b2 = sc.op("tensor", bc2, wait=[t_hl, t_e1], inc=MM)
    t_e2 = sc.op("vector", lambda v: v.tensor_copy(out=rbc_sb[:], in_=PBC[:]),
                 wait=[t_b2], inc=EV)
    return t_e2, t_stats



def p0(kb, stack):
    nc = kb.nc
    sc = Sched(kb)
    MM, EV, AC = (kb.sem(x) for x in ("p0mm", "p0ev", "p0ac"))
    LDW, LDX, LDT = (kb.sem(x) for x in ("p0ldw", "p0ldx", "p0ldt"))
    STL, STQ, STS = (kb.sem(x) for x in ("p0stl", "p0stq", "p0sts"))

    ctx = stack.enter_context
    qw_sb = ctx(nc.sbuf_tensor("p0_qw", [128, 16, NH * DQK], BF16))
    qwp_sb = ctx(nc.sbuf_tensor("p0_qwp", [128, 16, NH * DPE], BF16))
    kvd_sb = ctx(nc.sbuf_tensor("p0_kvd", [128, 16, LATD + DPE], BF16))
    lng_sb = ctx(nc.sbuf_tensor("p0_lng", [128, 4], F32))
    lnb_sb = ctx(nc.sbuf_tensor("p0_lnb", [128, 4], F32))
    ones_sb = ctx(nc.sbuf_tensor("p0_ones", [128, 1], BF16))
    eps_sb = ctx(nc.sbuf_tensor("p0_eps", [1, 1], F32))
    onesr_sb = ctx(nc.sbuf_tensor("p0_onesr", [1, 128], BF16))
    rtk_sb = ctx(nc.sbuf_tensor("p0_rtk", [32, 2, TT], BF16))
    rtq_sb = ctx(nc.sbuf_tensor("p0_rtq", [64, 2, TT], BF16))
    xt_sb = ctx(nc.sbuf_tensor("p0_xt", [128, 16, TT], BF16))
    lat_f = ctx(nc.sbuf_tensor("p0_latf", [128, 4, TT], F32))
    lat_b = ctx(nc.sbuf_tensor("p0_latb", [128, 4, TT], BF16))
    sq_b = ctx(nc.sbuf_tensor("p0_sqb", [128, 4, TT], BF16))
    st_f = ctx(nc.sbuf_tensor("p0_stf", [1, 4, TT], F32))
    hl_b = ctx(nc.sbuf_tensor("p0_hlb", [1, 4, TT], BF16))
    mbc_sb = ctx(nc.sbuf_tensor("p0_mbc", [128, TT], F32))
    rbc_sb = ctx(nc.sbuf_tensor("p0_rbc", [128, TT], F32))
    t1_f = ctx(nc.sbuf_tensor("p0_t1f", [128, TT], F32))
    latout = ctx(nc.sbuf_tensor("p0_latout", [128, 4, TT], BF16))
    peout = ctx(nc.sbuf_tensor("p0_peout", [32, 2, TT], BF16))
    pesc = ctx(nc.sbuf_tensor("p0_pesc", [64, 2, TT], F32))
    qn_b = ctx(nc.sbuf_tensor("p0_qnb", [128, 2, TT], BF16))
    qp_b = ctx(nc.sbuf_tensor("p0_qpb", [64, 2, TT], BF16))
    qsq_b = ctx(nc.sbuf_tensor("p0_qsqb", [128, 2, TT], BF16))
    qpsq_b = ctx(nc.sbuf_tensor("p0_qpsqb", [64, 2, TT], BF16))
    qs_f = ctx(nc.sbuf_tensor("p0_qsf", [1, 128], F32))
    PA = ctx(nc.psum_tensor("p0_PA", [128, TT], F32))
    PB = ctx(nc.psum_tensor("p0_PB", [128, TT], F32))
    PPA = ctx(nc.psum_tensor("p0_PPA", [64, TT], F32))
    PPB = ctx(nc.psum_tensor("p0_PPB", [64, TT], F32))
    PM = ctx(nc.psum_tensor("p0_PM", [1, TT], F32))
    PS = ctx(nc.psum_tensor("p0_PS", [1, TT], F32))
    PQ2 = ctx(nc.psum_tensor("p0_PQ2", [1, TT], F32))
    PBC = ctx(nc.psum_tensor("p0_PBC", [128, TT], F32))

    for dst, srcd in ((qw_sb, chunked(kb.d["qwT"])), (qwp_sb, chunked(kb.d["qwTpe"])),
                      (kvd_sb, chunked(kb.d["kvdT"])),
                      (lng_sb, kb.d["lng"]), (lnb_sb, kb.d["lnb"]),
                      (ones_sb, kb.ones_bf), (onesr_sb, kb.ones_row),
                      (eps_sb, kb.epsc)):
        sc.op("gpsimd", lambda g, dst=dst, srcd=srcd: g.dma_start(out=dst[:], in_=srcd),
              inc=LDW, n=16)
    t_w = Tok(LDW, LDW.v)

    prev = {
        "evA": None, "evB": None, "evPA": None, "evPB": None,
        "bc_done": None, "apply": None, "qnorm": None,
        "q_store": None, "lat_store": None, "pm_free": None,
        "tab_used": None, "tile_done": None,
    }

    for tt in range(NTT):
        cs = (tt * TT, (tt + 1) * TT)
        x_tok = sc.op("gpsimd",
                      lambda g, cs=cs: g.dma_start(out=xt_sb[:], in_=chunked(kb.d["xT"], cs)),
                      wait=[t_w, prev["tile_done"]], inc=LDX, n=16)
        t_tab = sc.op("gpsimd",
                      lambda g, cs=cs: [
                          g.dma_start(out=rtk_sb[:, 0], in_=kb.d["r0k"][:, cs[0]:cs[1]]),
                          g.dma_start(out=rtk_sb[:, 1], in_=kb.d["r0k"][:, S + cs[0]:S + cs[1]]),
                          g.dma_start(out=rtq_sb[:, 0], in_=kb.d["r0q"][:, cs[0]:cs[1]]),
                          g.dma_start(out=rtq_sb[:, 1], in_=kb.d["r0q"][:, S + cs[0]:S + cs[1]])],
                      wait=[prev["tab_used"], prev["tile_done"]], inc=LDT, n=16, ni=4)
        xb = xt_sb

        ev_lat = [None] * 4
        for oc in range(4):
            tgt = PA if oc % 2 == 0 else PB
            warp = prev["evA"] if oc % 2 == 0 else prev["evB"]

            def mms(t, tgt=tgt, oc=oc):
                for fc in range(16):
                    r = t.matmul(tgt[:], kvd_sb[:, fc, oc * 128:(oc + 1) * 128],
                                 xb[:, fc], start=(fc == 0), stop=(fc == 15))
                return r

            t_mm = sc.op("tensor", mms, wait=[x_tok, warp], inc=MM)

            def evac(v, oc=oc, tgt=tgt):
                v.tensor_copy(out=lat_f[:, oc], in_=tgt[:])
                return v.tensor_copy(out=lat_b[:, oc], in_=tgt[:])

            tok = sc.op("vector", evac, wait=[t_mm, prev["apply"]], inc=EV)
            ev_lat[oc] = tok
            if oc % 2 == 0:
                prev["evA"] = tok
            else:
                prev["evB"] = tok

        def pemm(t):
            for fc in range(16):
                t.matmul(PPA[0:32, :], kvd_sb[:, fc, 512:544], xb[:, fc],
                         start=(fc == 0), stop=(fc == 15))
            for fc in range(16):
                r = t.matmul(PPB[0:32, :], kvd_sb[:, fc, 544:576], xb[:, fc],
                             start=(fc == 0), stop=(fc == 15))
            return r

        t_pemm = sc.op("tensor", pemm, wait=[x_tok, prev["evPA"], prev["evPB"]], inc=MM)

        def ropepe(v):
            a = PPA[0:32, :]
            b = PPB[0:32, :]
            c = rtk_sb[:, 0]
            s_ = rtk_sb[:, 1]
            v.tensor_tensor(out=pesc[0:32, 0], in0=a, in1=c, op=ALU.mult)
            v.tensor_tensor(out=pesc[0:32, 1], in0=b, in1=s_, op=ALU.mult)
            v.drain()
            v.tensor_tensor(out=peout[:, 0], in0=pesc[0:32, 0], in1=pesc[0:32, 1],
                            op=ALU.subtract)
            v.drain()
            v.tensor_tensor(out=pesc[0:32, 0], in0=a, in1=s_, op=ALU.mult)
            v.tensor_tensor(out=pesc[0:32, 1], in0=b, in1=c, op=ALU.mult)
            v.drain()
            return v.tensor_tensor(out=peout[:, 1], in0=pesc[0:32, 1], in1=pesc[0:32, 0],
                                   op=ALU.add)

        t_rope = sc.op("vector", ropepe, wait=[t_pemm, t_tab, prev["lat_store"]], inc=EV)
        prev["evPA"] = t_rope
        prev["evPB"] = t_rope

        def sqs(s):
            for c in range(4):
                r = s.activation(out=sq_b[:, c], in_=lat_b[:, c], func=AF.Square)
            return r

        t_sq = sc.op("scalar", sqs, wait=ev_lat, inc=AC)

        def statmm(t):
            for c in range(4):
                t.matmul(PM[:], ones_sb[:], lat_b[:, c], start=(c == 0), stop=(c == 3))
            for c in range(4):
                r = t.matmul(PS[:], ones_sb[:], sq_b[:, c], start=(c == 0), stop=(c == 3))
            return r

        t_stat = sc.op("tensor", statmm, wait=ev_lat + [t_sq, prev["pm_free"]], inc=MM)

        t_bc, t_statsdone = ln_tail(
            sc, kb, nfeat=LATD, PMp=PM, PSp=PS, st_f=st_f, hl_b=hl_b,
            onesr_sb=onesr_sb, PBC=PBC, mbc_sb=mbc_sb, rbc_sb=rbc_sb,
            stat_tok=t_stat, bc_war_tok=prev["apply"], MM=MM, EV=EV, AC=AC,
            eps_ap=eps_sb[:, :])
        prev["pm_free"] = t_statsdone
        prev["bc_done"] = t_bc

        def apply(v):
            for oc in range(4):
                v.tensor_tensor(out=t1_f[:], in0=lat_f[:, oc], in1=mbc_sb[:], op=ALU.subtract)
                v.drain()
                v.tensor_tensor(out=t1_f[:], in0=t1_f[:], in1=rbc_sb[:], op=ALU.mult)
                v.drain()
                r = v.tensor_scalar(out=latout[:, oc], in0=t1_f[:],
                                    scalar1=lng_sb[:, oc:oc + 1],
                                    scalar2=lnb_sb[:, oc:oc + 1],
                                    op0=ALU.mult, op1=ALU.add)
            return r

        t_apply = sc.op("vector", apply, wait=[t_bc, prev["lat_store"]] + ev_lat, inc=EV)
        prev["apply"] = t_apply

        def latst(e, cs=cs):
            return [
                e.dma_start(out=kb.d["LAT"][0:LATD, cs[0]:cs[1]].rearrange("(c p) t -> p c t", p=128),
                            in_=latout[:]),
                e.dma_start(out=kb.d["LAT"][LATD:LATD + 32, cs[0]:cs[1]], in_=peout[:, 0]),
                e.dma_start(out=kb.d["LAT"][LATD + 32:LATD + 64, cs[0]:cs[1]], in_=peout[:, 1])]

        prev["lat_store"] = sc.op("sync", latst, wait=[t_apply, t_rope], inc=STL, n=16, ni=3)

        for p in range(NH // 2):
            h0, h1 = 2 * p, 2 * p + 1

            def qmm(t, p=p, h0=h0, h1=h1):
                for fc in range(16):
                    t.matmul(PA[:], qw_sb[:, fc, h0 * DQK:(h0 + 1) * DQK], xb[:, fc],
                             start=(fc == 0), stop=(fc == 15))
                for fc in range(16):
                    t.matmul(PB[:], qw_sb[:, fc, h1 * DQK:(h1 + 1) * DQK], xb[:, fc],
                             start=(fc == 0), stop=(fc == 15))
                for fc in range(16):
                    t.matmul(PPA[:], qwp_sb[:, fc, p * 128:p * 128 + 64], xb[:, fc],
                             start=(fc == 0), stop=(fc == 15))
                for fc in range(16):
                    r = t.matmul(PPB[:], qwp_sb[:, fc, p * 128 + 64:p * 128 + 128], xb[:, fc],
                                 start=(fc == 0), stop=(fc == 15))
                return r

            t_qmm = sc.op("tensor", qmm,
                          wait=[x_tok, prev["evA"], prev["evB"], prev["evPA"], prev["evPB"],
                                prev["apply"] if p == 0 else None],
                          inc=MM)

            def qevac(s):
                s.activation(out=qn_b[:, 0], in_=PA[:], func=AF.Copy, scale=SC0)
                s.activation(out=qsq_b[:, 0], in_=PA[:], func=AF.Square, scale=SC0)
                s.activation(out=qn_b[:, 1], in_=PB[:], func=AF.Copy, scale=SC0)
                return s.activation(out=qsq_b[:, 1], in_=PB[:], func=AF.Square, scale=SC0)

            t_qe = sc.op("scalar", qevac, wait=[t_qmm, prev["q_store"]], inc=AC)
            prev["evA"] = t_qe
            prev["evB"] = t_qe

            def qrope(v):
                c = rtq_sb[:, 0]
                s_ = rtq_sb[:, 1]
                v.tensor_tensor(out=pesc[:, 0], in0=PPA[:], in1=c, op=ALU.mult)
                v.tensor_tensor(out=pesc[:, 1], in0=PPB[:], in1=s_, op=ALU.mult)
                v.drain()
                v.tensor_tensor(out=qp_b[:, 0], in0=pesc[:, 0], in1=pesc[:, 1],
                                op=ALU.subtract)
                v.drain()
                v.tensor_tensor(out=pesc[:, 0], in0=PPA[:], in1=s_, op=ALU.mult)
                v.tensor_tensor(out=pesc[:, 1], in0=PPB[:], in1=c, op=ALU.mult)
                v.drain()
                v.tensor_tensor(out=qp_b[:, 1], in0=pesc[:, 1], in1=pesc[:, 0], op=ALU.add)
                v.drain()
                v.tensor_tensor(out=qpsq_b[:, 0], in0=qp_b[:, 0], in1=qp_b[:, 0], op=ALU.mult)
                return v.tensor_tensor(out=qpsq_b[:, 1], in0=qp_b[:, 1], in1=qp_b[:, 1],
                                       op=ALU.mult)

            t_qr = sc.op("vector", qrope, wait=[t_qmm, t_tab, prev["q_store"], t_rope], inc=EV)
            prev["evPA"] = t_qr
            prev["evPB"] = t_qr

            for hh, base in ((h0, 0), (h1, 32)):
                def qnmm(t, base=base, hh=hh):
                    t.matmul(PQ2[:], ones_sb[:], qsq_b[:, 0 if base == 0 else 1],
                             start=True, stop=False)
                    t.matmul(PQ2[:], ones_sb[base:base + 32, :], qpsq_b[base:base + 32, 0],
                             start=False, stop=False)
                    return t.matmul(PQ2[:], ones_sb[base:base + 32, :], qpsq_b[base:base + 32, 1],
                                    start=False, stop=True)

                t_qn = sc.op("tensor", qnmm, wait=[t_qe, t_qr, prev["qnorm"]], inc=MM)
                slot = hh * 8 + tt
                t_qx = sc.op("vector",
                             lambda v, slot=slot: v.tensor_reduce(
                                 out=qs_f[:, slot:slot + 1], in_=PQ2[:], axis=AX.X, op=ALU.max),
                             wait=[t_qn], inc=EV)
                prev["qnorm"] = t_qx

            def qst(e, h0=h0, h1=h1, cs=cs):
                return [
                    e.dma_start(out=kb.d["Q0"][h0 * QHD:h0 * QHD + 128, cs[0]:cs[1]], in_=qn_b[:, 0]),
                    e.dma_start(out=kb.d["Q0"][h1 * QHD:h1 * QHD + 128, cs[0]:cs[1]], in_=qn_b[:, 1]),
                    e.dma_start(out=kb.d["Q0"][h0 * QHD + 128:h0 * QHD + 160, cs[0]:cs[1]],
                                in_=qp_b[0:32, 0]),
                    e.dma_start(out=kb.d["Q0"][h0 * QHD + 160:h0 * QHD + 192, cs[0]:cs[1]],
                                in_=qp_b[0:32, 1]),
                    e.dma_start(out=kb.d["Q0"][h1 * QHD + 128:h1 * QHD + 160, cs[0]:cs[1]],
                                in_=qp_b[32:64, 0]),
                    e.dma_start(out=kb.d["Q0"][h1 * QHD + 160:h1 * QHD + 192, cs[0]:cs[1]],
                                in_=qp_b[32:64, 1])]

            t_qs = sc.op("sync", qst, wait=[t_qe, t_qr], inc=STQ, n=16, ni=6)
            prev["q_store"] = t_qs
        prev["tile_done"] = prev["qnorm"]
        prev["tab_used"] = prev["evPA"]

    sc.op("sync", lambda e: e.dma_start(out=kb.d["QS0"], in_=qs_f[:]),
          wait=[prev["qnorm"]], inc=STS, n=16)
    sc.op("sync", None, wait=[Tok(STL, STL.v), Tok(STQ, STQ.v), Tok(STS, STS.v)])

    with nc.Block() as blk:
        sc.emit(blk)



P2_NG = 4
P2_NQ = NTT
P2_SECT = 3
P2_SUB = 2


def p2(kb, stack, sc=None, precast=False):
    nc = kb.nc
    own = sc is None
    if own:
        sc = Sched(kb)
    t_cw = None
    pc_jobs = []
    if precast:
        CW = kb.sem("p9cw")
        for w in ("W1T", "W3T"):
            for ch in range(16):
                pc_jobs.append((kb.d[w.replace("T", "B")], kb.d[w], ch))

        def pc_emit(njobs):
            for _ in range(njobs):
                if not pc_jobs:
                    return
                dst, srcd, ch = pc_jobs.pop(0)
                sc.op("gpsimd",
                      lambda g, dst=dst, srcd=srcd, ch=ch: g.dma_start(
                          out=dst[ch * 128:(ch + 1) * 128, :],
                          in_=srcd[ch * 128:(ch + 1) * 128, :]),
                      inc=CW, n=16)
    MM, EV, AC = (kb.sem(x) for x in ("p2mm", "p2ev", "p2ac"))
    LDC, LDK = (kb.sem(x) for x in ("p2ldc", "p2ldk"))
    LQ = [kb.sem("p2lqa"), kb.sem("p2lqb")]
    STC = kb.sem("p2stc")
    ctx = stack.enter_context
    lat_sb = ctx(nc.sbuf_tensor("p2_lat", [128, 4, S], BF16))
    pe_sb = ctx(nc.sbuf_tensor("p2_pe", [64, S], BF16))
    qs_sb = ctx(nc.sbuf_tensor("p2_qs", [1, 128], F32))
    pex_f = ctx(nc.sbuf_tensor("p2_pex", [1, 9], F32))
    pesq_b = ctx(nc.sbuf_tensor("p2_pesq", [64, TT], BF16))
    kvuK_sb = ctx(nc.sbuf_tensor("p2_kvuK", [128, 4, 4, 128], BF16))
    kvuV_sb = ctx(nc.sbuf_tensor("p2_kvuV", [128, 4, 4, 128], BF16))
    K_sb = ctx(nc.sbuf_tensor("p2_K", [128, 4, S], BF16))
    V_sb = ctx(nc.sbuf_tensor("p2_V", [128, 32, 512], BF16))
    ksq_b = ctx(nc.sbuf_tensor("p2_ksq", [128, TT], BF16))
    k2_f = ctx(nc.sbuf_tensor("p2_k2", [1, 4, 8], F32))
    k2t_f = ctx(nc.sbuf_tensor("p2_k2t", [1, TT], F32))
    bs_f = ctx(nc.sbuf_tensor("p2_bs", [1, 4, 4], F32))
    bs_b = ctx(nc.sbuf_tensor("p2_bsb", [1, 4], BF16))
    bias_sb = ctx(nc.sbuf_tensor("p2_bias", [128, 4], F32))
    qn_sb = ctx(nc.sbuf_tensor("p2_qn", [128, 2, TT], BF16))
    qp_sb = ctx(nc.sbuf_tensor("p2_qp", [64, 2, TT], BF16))
    P_sb = ctx(nc.sbuf_tensor("p2_P", [128, 2, 8, TT], BF16))
    su_f = ctx(nc.sbuf_tensor("p2_su", [1, 2, TT], F32))
    hl_b = ctx(nc.sbuf_tensor("p2_hl", [1, 2, TT], BF16))
    bc_f = ctx(nc.sbuf_tensor("p2_bc", [128, TT], F32))
    ct_b = ctx(nc.sbuf_tensor("p2_ct", [128, TT], BF16))
    ones_sb = ctx(nc.sbuf_tensor("p2_ones", [128, 1], BF16))
    onesr_sb = ctx(nc.sbuf_tensor("p2_onesr", [1, 128], BF16))
    PSC = ctx(nc.psum_tensor("p2_PSC", [128, 4, TT], F32))
    PZ = ctx(nc.psum_tensor("p2_PZ", [128, TT], F32))
    PW = ctx(nc.psum_tensor("p2_PW", [1, TT], F32))
    PBC = ctx(nc.psum_tensor("p2_PBC", [128, TT], F32))
    PBI = ctx(nc.psum_tensor("p2_PBI", [128, 4], F32))

    sc.op("gpsimd", lambda g: g.dma_start(out=ones_sb[:], in_=kb.ones_bf), inc=LDC, n=16)
    sc.op("gpsimd", lambda g: g.dma_start(out=onesr_sb[:], in_=kb.ones_row), inc=LDC, n=16)
    sc.op("gpsimd", lambda g: g.dma_start(out=qs_sb[:], in_=kb.d["QS0"]), inc=LDC, n=16)
    sc.op("gpsimd",
          lambda g: g.dma_start(out=lat_sb[:],
                                in_=kb.d["LAT"][0:LATD, :].rearrange("(c p) t -> p c t", p=128)),
          inc=LDC, n=16)
    sc.op("gpsimd", lambda g: g.dma_start(out=pe_sb[:], in_=kb.d["LAT"][LATD:LATD + 64, :]),
          inc=LDC, n=16)
    t_res = Tok(LDC, LDC.v)

    t_prev = None
    for seg in range(NTT):
        ss = (seg * TT, (seg + 1) * TT)
        t_a = sc.op("scalar",
                    lambda s, ss=ss: s.activation(out=pesq_b[:], in_=pe_sb[:, ss[0]:ss[1]],
                                                  func=AF.Square),
                    wait=[t_res, t_prev], inc=AC)
        t_m = sc.op("tensor",
                    lambda t: t.matmul(PW[:], ones_sb[0:64, :], pesq_b[:], start=True, stop=True),
                    wait=[t_a, t_prev], inc=MM)
        t_prev = sc.op("vector",
                       lambda v, seg=seg: v.tensor_reduce(out=pex_f[:, seg:seg + 1], in_=PW[:],
                                                          axis=AX.X, op=ALU.max),
                       wait=[t_m], inc=EV)
    t_pe2 = sc.op("vector",
                  lambda v: v.tensor_reduce(out=pex_f[:, 8:9], in_=pex_f[:, 0:8],
                                            axis=AX.X, op=ALU.max),
                  wait=[t_prev], inc=EV)

    prev = {
        "grp_done": None,
        "kv_loads": None,
        "expA": [None], "expB": None,
        "ctx_done": None,
        "qload": [None, None],
        "qtok": [None, None],
        "cmm2": None, "cmm3": None,
        "pending_epi": None,
        "st": None,
    }

    for g in range(P2_NG):
        heads = [4 * g + i for i in range(4)]
        def kvld(gp, heads=heads):
            r = []
            for i, h in enumerate(heads):
                r.append(gp.dma_start(
                    out=kvuK_sb[:, :, i, :],
                    in_=kb.d["kvuT"][:, h * 256:h * 256 + 128].rearrange("(c p) o -> p c o", p=128)))
                r.append(gp.dma_start(
                    out=kvuV_sb[:, :, i, :],
                    in_=kb.d["kvuT"][:, h * 256 + 128:h * 256 + 256].rearrange("(c p) o -> p c o", p=128)))
            return r

        t_kv = sc.op("gpsimd", kvld, wait=[prev["grp_done"]], inc=LDK, n=16, ni=8)

        if P2_SECT < 1:
            continue
        kwar = {0: prev["grp_done"], 1: prev["grp_done"]}
        kprev_k2 = None
        kidx = 0
        for i in range(4 if P2_SUB >= 1 else 0):
            for seg in range(NTT):
                ss = (seg * TT, (seg + 1) * TT)
                par2 = kidx % 2
                tgt = PSC[:, par2]

                def kmm(t, i=i, ss=ss, tgt=tgt):
                    for lc in range(4):
                        r = t.matmul(tgt, kvuK_sb[:, lc, i, :], lat_sb[:, lc, ss[0]:ss[1]],
                                     start=(lc == 0), stop=(lc == 3))
                    return r

                t_kmm = sc.op("tensor", kmm,
                              wait=[t_kv, t_res, kwar[par2]],
                              inc=MM)

                def kev_f(v, i=i, ss=ss, tgt=tgt):
                    return v.tensor_copy(out=K_sb[:, i, ss[0]:ss[1]], in_=tgt)

                t_kev = sc.op("vector", kev_f, wait=[t_kmm], inc=EV)

                if P2_SUB < 2:
                    kwar[par2] = t_kev
                    kprev_k2 = t_kev
                    kidx += 1
                    continue
                t_ksq = sc.op("scalar",
                              lambda s, tgt=tgt: s.activation(out=ksq_b[:], in_=tgt,
                                                              func=AF.Square),
                              wait=[t_kmm, t_kev, kprev_k2], inc=AC)
                t_k2m = sc.op("tensor",
                              lambda t: t.matmul(PW[:], ones_sb[:], ksq_b[:],
                                                 start=True, stop=True),
                              wait=[t_ksq, t_pe2, kprev_k2, prev["ctx_done"]], inc=MM)

                def k2fin(v, i=i, seg=seg):
                    return v.tensor_reduce(out=k2_f[:, i, seg:seg + 1], in_=PW[:],
                                           axis=AX.X, op=ALU.max)

                t_k2 = sc.op("vector", k2fin, wait=[t_k2m], inc=EV)
                kwar[par2] = t_k2
                kprev_k2 = t_k2
                kidx += 1
        t_kdone = kprev_k2

        if P2_SECT < 2:
            prev["grp_done"] = t_kdone
            continue
        vev = []
        for kt in range(32):
            def vmm(t, kt=kt):
                for lc in range(4):
                    r = t.matmul(PZ[:], lat_sb[:, lc, kt * 128:(kt + 1) * 128],
                                 kvuV_sb[:, lc].rearrange("p a b -> p (a b)"),
                                 start=(lc == 0), stop=(lc == 3))
                return r

            t_vmm = sc.op("tensor", vmm,
                          wait=[t_kv, t_res, prev["ctx_done"],
                                vev[-1] if vev else prev["grp_done"]],
                          inc=MM)
            t_vev = sc.op("vector",
                          lambda v, kt=kt: v.tensor_copy(out=V_sb[:, kt, :], in_=PZ[:]),
                          wait=[t_vmm], inc=EV)
            vev.append(t_vev)
        t_vdone = vev[-1]

        def bmax(v, heads=heads):
            for i, h in enumerate(heads):
                v.tensor_reduce(out=bs_f[:, 0, i:i + 1], in_=qs_sb[:, h * 8:h * 8 + 8],
                                axis=AX.X, op=ALU.max)
                v.tensor_reduce(out=bs_f[:, 1, i:i + 1], in_=k2_f[:, i, :],
                                axis=AX.X, op=ALU.max)
            v.drain()
            v.tensor_scalar(out=bs_f[:, 1], in0=bs_f[:, 1], scalar1=pex_f[:, 8:9],
                            scalar2=None, op0=ALU.add)
            v.drain()
            return v.tensor_tensor(out=bs_f[:, 2], in0=bs_f[:, 0], in1=bs_f[:, 1],
                                   op=ALU.mult)

        t_bm = sc.op("vector", bmax, wait=[t_kdone, t_res, t_pe2], inc=EV)
        t_bsq = sc.op("scalar",
                      lambda s: s.activation(out=bs_f[:, 3], in_=bs_f[:, 2], func=AF.Sqrt),
                      wait=[t_bm], inc=AC)
        t_bneg = sc.op("vector",
                       lambda v: v.tensor_scalar(out=bs_b[:], in0=bs_f[:, 3],
                                                 scalar1=-1.0, scalar2=None, op0=ALU.mult),
                       wait=[t_bsq], inc=EV)

        def bimm(t):
            for i in range(4):
                r = t.matmul(PBI[:, i:i + 1], onesr_sb[:], bs_b[:, i:i + 1],
                             start=True, stop=True)
            return r

        t_bimm = sc.op("tensor", bimm, wait=[t_bneg, prev["grp_done"]], inc=MM)
        t_bias = sc.op("vector", lambda v: v.tensor_copy(out=bias_sb[:], in_=PBI[:]),
                       wait=[t_bimm], inc=EV)

        if P2_SECT < 3:
            prev["grp_done"] = t_bias
            continue
        units = [(i, qt) for i in range(4) for qt in range(P2_NQ)]

        def emit_qld(u, heads=heads):
            i_, qt_ = units[u]
            h_ = heads[i_]
            par_ = u % 2
            qs2 = (qt_ * TT, (qt_ + 1) * TT)

            def qld(gp, h_=h_, qs2=qs2, par_=par_):
                return [
                    gp.dma_start(out=qn_sb[:, par_],
                                 in_=kb.d["Q0"][h_ * QHD:h_ * QHD + 128, qs2[0]:qs2[1]]),
                    gp.dma_start(out=qp_sb[:, par_],
                                 in_=kb.d["Q0"][h_ * QHD + 128:h_ * QHD + QHD,
                                                qs2[0]:qs2[1]])]

            prev["qtok"][par_] = sc.op("gpsimd", qld, wait=[prev["qload"][par_]],
                                       inc=LQ[par_], n=16, ni=2)

        for u, (i, qt) in enumerate(units):
            h = heads[i]
            qs_ = (qt * TT, (qt + 1) * TT)
            par = u % 2
            if True:
                if u == 0:
                    emit_qld(0)
                    emit_qld(1)
                elif u + 1 < len(units):
                    emit_qld(u + 1)
                t_ql = prev["qtok"][par]

                t_exp4 = [None] * 4
                t_cmm4 = [None] * 4

                def emit_cmm(qq):
                    def cmm8(t, i=i, qq=qq):
                        for jj in range(8):
                            kt = qq * 8 + jj
                            t.matmul(PZ[:], V_sb[:, kt, i * 128:(i + 1) * 128],
                                     P_sb[:, qq % 2, jj],
                                     start=(kt == 0), stop=(kt == 31))
                            r = t.matmul(PW[:], ones_sb[:], P_sb[:, qq % 2, jj],
                                         start=(kt == 0), stop=(kt == 31))
                        return r

                    t_cmm4[qq] = sc.op("tensor", cmm8,
                                       wait=[t_exp4[qq],
                                             prev["ctx_done"] if qq == 0 else None],
                                       inc=MM)

                pairtok = prev.setdefault("evroll", [None, None])
                for q in range(4):
                    qp_ = q % 2
                    t_e = None
                    for rr in range(4):
                        pr = rr % 2

                        def smm(t, i=i, q=q, rr=rr, pr=pr, par=par):
                            for u in range(2):
                                kt = q * 8 + 2 * rr + u
                                t.matmul(PSC[:, 2 * pr + u],
                                         K_sb[:, i, kt * 128:(kt + 1) * 128],
                                         qn_sb[:, par], start=True, stop=False)
                                r = t.matmul(PSC[:, 2 * pr + u],
                                             pe_sb[:, kt * 128:(kt + 1) * 128],
                                             qp_sb[:, par], start=False, stop=True)
                            return r

                        t_s = sc.op("tensor", smm,
                                    wait=[t_ql, t_kdone, t_vdone, pairtok[pr]],
                                    inc=MM)

                        def expf(s, i=i, qp_=qp_, rr=rr, pr=pr):
                            return s.activation(
                                out=P_sb[:, qp_, 2 * rr:2 * rr + 2].rearrange(
                                    "p a t -> p (a t)"),
                                in_=PSC[:, 2 * pr:2 * pr + 2].rearrange("p a t -> p (a t)"),
                                func=AF.Exp, bias=bias_sb[:, i:i + 1])

                        t_e = sc.op("scalar", expf,
                                    wait=[t_s, t_bias,
                                          (t_cmm4[q - 2] if q >= 2 else
                                           prev["cmm2" if q == 0 else "cmm3"])
                                          if rr == 0 else None],
                                    inc=AC)
                        pairtok[pr] = t_e
                        if q == 0 and rr == 1 and prev.get("pending_epi"):
                            prev["pending_epi"]()
                            prev["pending_epi"] = None
                    t_exp4[q] = t_e
                    if q >= 1:
                        emit_cmm(q - 1)
                emit_cmm(3)
                t_last_cmm = t_cmm4[3]
                t_ctx = [t_last_cmm] * 32
                prev["qload"][par] = t_s
                prev["cmm2"], prev["cmm3"] = t_cmm4[2], t_cmm4[3]

                def epi1(v):
                    v.tensor_copy(out=su_f[:, 0], in_=PW[:])
                    v.drain()
                    v.reciprocal(out=su_f[:, 1], in_=su_f[:, 0])
                    v.drain()
                    return emit_hi_lo(v, su_f[:, 1], hl_b[:, 0], hl_b[:, 1])

                t_ep1 = sc.op("vector", epi1, wait=[t_ctx[31]], inc=EV)

                def flush_epi(t_ep1=t_ep1, h=h, qs_=qs_):
                    def epbc(t):
                        t.matmul(PBC[:], onesr_sb[:], hl_b[:, 0], start=True, stop=False)
                        return t.matmul(PBC[:], onesr_sb[:], hl_b[:, 1],
                                        start=False, stop=True)

                    t_eb = sc.op("tensor", epbc, wait=[t_ep1], inc=MM)

                    def epi2(v):
                        v.tensor_copy(out=bc_f[:], in_=PBC[:])
                        v.drain()
                        return v.tensor_tensor(out=ct_b[:], in0=PZ[:], in1=bc_f[:],
                                               op=ALU.mult)

                    t_ep2 = sc.op("vector", epi2, wait=[t_eb, prev["st"]], inc=EV)
                    prev["ctx_done"] = t_ep2
                    t_st = sc.op("sync",
                                 lambda e, h=h, qs_=qs_: e.dma_start(
                                     out=kb.d["CT0"][h * 128:(h + 1) * 128,
                                                     qs_[0]:qs_[1]],
                                     in_=ct_b[:]),
                                 wait=[t_ep2, prev["st"]], inc=STC, n=16)
                    prev["st"] = t_st

                prev["pending_epi"] = flush_epi
        if prev.get("pending_epi"):
            prev["pending_epi"]()
            prev["pending_epi"] = None
        prev["grp_done"] = prev["ctx_done"]
        if precast:
            pc_emit(8)

    if precast:
        pc_emit(99)
        t_cw = Tok(CW, CW.v)
    sc.op("sync", None, wait=[Tok(STC, STC.v), t_cw])
    if own:
        with nc.Block() as blk:
            sc.emit(blk)
    return sc



def p3(kb, stack):
    nc = kb.nc
    sc = Sched(kb)
    MM, EV, AC = (kb.sem(x) for x in ("p3mm", "p3ev", "p3ac"))
    LDW, LDC, LDX = (kb.sem(x) for x in ("p3ldw", "p3ldc", "p3ldx"))
    STX, STW = (kb.sem(x) for x in ("p3stx", "p3stw"))
    ctx = stack.enter_context
    ow_sb = ctx(nc.sbuf_tensor("p3_ow", [128, 16, D], BF16))
    gt_sb = ctx(nc.sbuf_tensor("p3_gt", [128, 16, 8], BF16))
    ebc_sb = ctx(nc.sbuf_tensor("p3_ebc", [128, 8], F32))
    ident_sb = ctx(nc.sbuf_tensor("p3_id", [128, 128], BF16))
    ones_sb = ctx(nc.sbuf_tensor("p3_ones", [128, 1], BF16))
    onesr_sb = ctx(nc.sbuf_tensor("p3_onesr", [1, 128], BF16))
    eps_sb = ctx(nc.sbuf_tensor("p3_eps", [1, 1], F32))
    ct_sb = ctx(nc.sbuf_tensor("p3_ct", [128, 16, TT], BF16))
    xf_sb = ctx(nc.sbuf_tensor("p3_xf", [128, 16, TT], F32))
    r_f = ctx(nc.sbuf_tensor("p3_rf", [128, 16, TT], F32))
    rb_b = ctx(nc.sbuf_tensor("p3_rb", [128, 2, TT], BF16))
    sq_b = ctx(nc.sbuf_tensor("p3_sq", [128, 2, TT], BF16))
    st_f = ctx(nc.sbuf_tensor("p3_st", [1, 4, TT], F32))
    hl_b = ctx(nc.sbuf_tensor("p3_hl", [1, 4, TT], BF16))
    mbc_sb = ctx(nc.sbuf_tensor("p3_mbc", [128, TT], F32))
    rbc_sb = ctx(nc.sbuf_tensor("p3_rbc", [128, TT], F32))
    t1_f = ctx(nc.sbuf_tensor("p3_t1", [128, TT], F32))
    x1b_sb = ctx(nc.sbuf_tensor("p3_x1b", [128, 16, TT], BF16))
    gs_f = ctx(nc.sbuf_tensor("p3_gs", [128, 6, 8], F32))
    wgt_b = ctx(nc.sbuf_tensor("p3_wgt", [128, 8], BF16))
    wgtT_b = ctx(nc.sbuf_tensor("p3_wgtT", [8, 128], BF16))
    PA = ctx(nc.psum_tensor("p3_PA", [128, TT], F32))
    PB = ctx(nc.psum_tensor("p3_PB", [128, TT], F32))
    PM = ctx(nc.psum_tensor("p3_PM", [1, TT], F32))
    PS = ctx(nc.psum_tensor("p3_PS", [1, TT], F32))
    PBC = ctx(nc.psum_tensor("p3_PBC", [128, TT], F32))
    PG = ctx(nc.psum_tensor("p3_PG", [128, 8], F32))
    PT = ctx(nc.psum_tensor("p3_PT", [8, 128], BF16))

    for dst, srcd in ((ow_sb, chunked(kb.d["owT"])), (gt_sb, chunked(kb.d["gT"])),
                      (ebc_sb, kb.d["ebc"]), (ident_sb, kb.ident),
                      (ones_sb, kb.ones_bf), (onesr_sb, kb.ones_row),
                      (eps_sb, kb.epsc)):
        sc.op("gpsimd", lambda g, dst=dst, srcd=srcd: g.dma_start(out=dst[:], in_=srcd),
              inc=LDW, n=16)
    t_w = Tok(LDW, LDW.v)

    prev = {
        "mmA": None, "mmB": None, "apply": None, "pm_free": None,
        "x1_store": None, "wg_store": None, "stat": [None, None],
        "gmm": None, "tr": None, "top": None, "ct_used": None, "xf_used": None,
    }

    for tt in range(NTT):
        cs = (tt * TT, (tt + 1) * TT)
        t_ct = sc.op("gpsimd",
                     lambda g, cs=cs: g.dma_start(out=ct_sb[:], in_=chunked(kb.d["CT0"], cs)),
                     wait=[prev["ct_used"]], inc=LDC, n=16)
        t_xf = sc.op("gpsimd",
                     lambda g, cs=cs: g.dma_start(out=xf_sb[:], in_=chunked(kb.d["xT"], cs)),
                     wait=[prev["xf_used"]], inc=LDX, n=16)

        t_radd = [None] * 16
        t_stat = None
        for oc in range(16):
            tgt = PA if oc % 2 == 0 else PB
            warp = prev["mmA"] if oc % 2 == 0 else prev["mmB"]

            def omm(t, oc=oc, tgt=tgt):
                for cc in range(16):
                    r = t.matmul(tgt[:], ow_sb[:, cc, oc * 128:(oc + 1) * 128],
                                 ct_sb[:, cc], start=(cc == 0), stop=(cc == 15))
                return r

            t_mm = sc.op("tensor", omm, wait=[t_ct, t_w, warp], inc=MM)

            t_radd[oc] = sc.op("vector",
                               lambda v, oc=oc, tgt=tgt: v.tensor_tensor(
                                   out=r_f[:, oc], in0=tgt[:], in1=xf_sb[:, oc], op=ALU.add),
                               wait=[t_mm, t_xf, prev["apply"]], inc=EV)
            if oc % 2 == 0:
                prev["mmA"] = t_radd[oc]
            else:
                prev["mmB"] = t_radd[oc]

            par = oc % 2
            t_rb = sc.op("vector",
                         lambda v, oc=oc, par=par: v.tensor_copy(out=rb_b[:, par], in_=r_f[:, oc]),
                         wait=[t_radd[oc], prev["stat"][par]], inc=EV)
            t_sq = sc.op("scalar",
                         lambda s, oc=oc, par=par: s.activation(out=sq_b[:, par], in_=r_f[:, oc],
                                                                func=AF.Square),
                         wait=[t_radd[oc], prev["stat"][par]], inc=AC)

            def smm(t, oc=oc, par=par):
                t.matmul(PM[:], ones_sb[:], rb_b[:, par], start=(oc == 0), stop=(oc == 15))
                return t.matmul(PS[:], ones_sb[:], sq_b[:, par], start=(oc == 0), stop=(oc == 15))

            t_stat = sc.op("tensor", smm, wait=[t_rb, t_sq, prev["pm_free"] if oc == 0 else None],
                           inc=MM)
            prev["stat"][par] = t_stat

        t_bc, t_statsdone = ln_tail(
            sc, kb, nfeat=D, PMp=PM, PSp=PS, st_f=st_f, hl_b=hl_b,
            onesr_sb=onesr_sb, PBC=PBC, mbc_sb=mbc_sb, rbc_sb=rbc_sb,
            stat_tok=t_stat, bc_war_tok=prev["apply"], MM=MM, EV=EV, AC=AC,
            eps_ap=eps_sb[:, :])
        prev["pm_free"] = t_statsdone

        def apply(v):
            for oc in range(16):
                v.tensor_tensor(out=t1_f[:], in0=r_f[:, oc], in1=mbc_sb[:], op=ALU.subtract)
                v.drain()
                r = v.tensor_tensor(out=x1b_sb[:, oc], in0=t1_f[:], in1=rbc_sb[:], op=ALU.mult)
                v.drain()
            return r

        t_apply = sc.op("vector", apply, wait=[t_bc, prev["x1_store"]] + t_radd, inc=EV)
        prev["apply"] = t_apply
        prev["xf_used"] = t_apply

        t_x1s = sc.op("sync",
                      lambda e, cs=cs: e.dma_start(
                          out=chunked(kb.d["X1"], cs), in_=x1b_sb[:]),
                      wait=[t_apply], inc=STX, n=16)
        prev["x1_store"] = t_x1s

        for ts_ in range(4):
            col = (ts_ * 128, (ts_ + 1) * 128)

            def gmm(t, col=col):
                for cc in range(16):
                    r = t.matmul(PG[:], x1b_sb[:, cc, col[0]:col[1]], gt_sb[:, cc],
                                 start=(cc == 0), stop=(cc == 15))
                return r

            t_gmm = sc.op("tensor", gmm, wait=[t_apply, prev["gmm"]], inc=MM)
            t_sig = sc.op("scalar",
                          lambda s: s.activation(out=gs_f[:, 0], in_=PG[:], func=AF.Sigmoid),
                          wait=[t_gmm], inc=AC)
            prev["gmm"] = t_sig

            def top2(v):
                v.tensor_tensor(out=gs_f[:, 1], in0=gs_f[:, 0], in1=ebc_sb[:], op=ALU.add)
                v.drain()
                v.tensor_reduce(out=t1_f[:, 0:1], in_=gs_f[:, 1], axis=AX.X, op=ALU.max)
                v.drain()
                v.tensor_scalar(out=gs_f[:, 2], in0=gs_f[:, 1], scalar1=t1_f[:, 0:1],
                                scalar2=None, op0=ALU.is_equal)
                v.drain()
                v.scalar_tensor_tensor(out=gs_f[:, 3], in0=gs_f[:, 2], scalar=NEGBIG,
                                       in1=gs_f[:, 1], op0=ALU.mult, op1=ALU.add)
                v.drain()
                v.tensor_reduce(out=t1_f[:, 0:1], in_=gs_f[:, 3], axis=AX.X, op=ALU.max)
                v.drain()
                v.tensor_scalar(out=gs_f[:, 4], in0=gs_f[:, 3], scalar1=t1_f[:, 0:1],
                                scalar2=None, op0=ALU.is_equal)
                v.drain()
                v.tensor_tensor(out=gs_f[:, 5], in0=gs_f[:, 2], in1=gs_f[:, 4], op=ALU.add)
                v.drain()
                return v.tensor_tensor(out=wgt_b[:], in0=gs_f[:, 0], in1=gs_f[:, 5], op=ALU.mult)

            t_top = sc.op("vector", top2, wait=[t_sig, t_w, prev["tr"]], inc=EV)
            prev["top"] = t_top

            t_tr = sc.op("tensor",
                         lambda t: t.transpose(PT[:], wgt_b[:], ident_sb[:]),
                         wait=[t_top, prev["tr"]], inc=MM)
            t_wc = sc.op("vector", lambda v: v.tensor_copy(out=wgtT_b[:], in_=PT[:]),
                         wait=[t_tr, prev["wg_store"]], inc=EV)
            prev["tr"] = t_wc
            t_ws = sc.op("sync",
                         lambda e, cs=cs, col=col: e.dma_start(
                             out=kb.d["WG8"][:, cs[0] + col[0]:cs[0] + col[1]], in_=wgtT_b[:]),
                         wait=[t_wc], inc=STW, n=16)
            prev["wg_store"] = t_ws
        prev["ct_used"] = Tok(MM, prev["gmm"].val if False else MM.v)

    sc.op("sync", None, wait=[Tok(STX, STX.v), Tok(STW, STW.v)])
    with nc.Block() as blk:
        sc.emit(blk)


def chunked3(ap3, e, cp=128):
    return ap3[e:e + 1].rearrange("a (c p) o -> p (a c) o", p=cp)



def p4(kb, stack):
    nc = kb.nc
    sc = Sched(kb)
    MM, EV, AC = (kb.sem(x) for x in ("p4mm", "p4ev", "p4ac"))
    LDW, LD2, LDX, LDG = (kb.sem(x) for x in ("p4ldw", "p4ld2", "p4ldx", "p4ldg"))
    DAC = kb.sem("p4dac")
    DA2 = kb.sem("p4da2")
    ctx = stack.enter_context
    w13_sb = ctx(nc.sbuf_tensor("p4_w13", [128, 16, 2 * HID], BF16))
    w2_sb = ctx(nc.sbuf_tensor("p4_w2", [128, 8, D], BF16))
    hT_sb = ctx(nc.sbuf_tensor("p4_hT", [128, 8, 2048], BF16))
    x1t_sb = ctx(nc.sbuf_tensor("p4_x1t", [128, 16, TT], BF16))
    sg2_f = ctx(nc.sbuf_tensor("p4_sg2", [128, 4, TT], F32))
    wg8_sb = ctx(nc.sbuf_tensor("p4_wg8", [8, S], BF16))
    sel_sb = ctx(nc.sbuf_tensor("p4_sel", [8, 8 * 128], BF16))
    wbc_sb = ctx(nc.sbuf_tensor("p4_wbc", [128, TT], F32))
    sg_f = ctx(nc.sbuf_tensor("p4_sg", [128, 4, TT], F32))
    oe_f = ctx(nc.sbuf_tensor("p4_oe", [128, 2, 4, TT], F32))
    PG = ctx(nc.psum_tensor("p4_PG", [128, 4, TT], F32))
    PU = ctx(nc.psum_tensor("p4_PU", [128, 4, TT], F32))

    sc.op("gpsimd", lambda g: g.dma_start(out=wg8_sb[:], in_=kb.d["WG8"]), inc=LDG, n=16)
    sc.op("gpsimd", lambda g: g.dma_start(out=sel_sb[:], in_=kb.sel8), inc=LDG, n=16)
    t_g = Tok(LDG, LDG.v)

    prev = {
        "p1A": None, "p1B": None,
        "w13_used": None, "w2_used": None,
        "x1_used": None, "wbc_used": None,
        "hT_used": None, "sg": [None, None],
        "oe_dma": [None, None], "oe_fill": None,
        "p2evA": None, "p2evB": None,
    }

    experts = [("sh", None)] + [("ex", e) for e in range(NE)]
    for kind, e in experts:
        if kind == "sh":
            def wld(g):
                return [g.dma_start(out=w13_sb[:, :, 0:HID], in_=chunked(kb.d["sw1T"])),
                        g.dma_start(out=w13_sb[:, :, HID:2 * HID], in_=chunked(kb.d["sw3T"]))]

            def w2ld(g):
                return [g.dma_start(out=w2_sb[:], in_=kb.d["sw2T"].rearrange("(c p) o -> p c o", p=128))]
        else:
            def wld(g, e=e):
                return [g.dma_start(out=w13_sb[:, :, 0:HID], in_=chunked3(kb.d["w1T"], e)),
                        g.dma_start(out=w13_sb[:, :, HID:2 * HID], in_=chunked3(kb.d["w3T"], e))]

            def w2ld(g, e=e):
                return [g.dma_start(out=w2_sb[:], in_=chunked3(kb.d["w2T"], e))]

        t_w13 = sc.op("gpsimd", wld, wait=[prev["w13_used"]], inc=LDW, n=16, ni=2)
        t_w2 = sc.op("gpsimd", w2ld, wait=[prev["w2_used"]], inc=LD2, n=16, ni=1)

        for hf in range(2):
            hmuls = []
            for tt4 in range(4):
                tt = hf * 4 + tt4
                cs = (tt * TT, (tt + 1) * TT)
                t_x1 = sc.op("gpsimd",
                             lambda g, cs=cs: g.dma_start(out=x1t_sb[:],
                                                          in_=chunked(kb.d["X1"], cs)),
                             wait=[prev["x1_used"]], inc=LDX, n=16)
                if kind == "ex":
                    def bcmm(t, e=e, cs=cs):
                        return t.matmul(PG[:, 0], sel_sb[:, e * 128:(e + 1) * 128],
                                        wg8_sb[:, cs[0]:cs[1]], start=True, stop=True)

                    t_bc = sc.op("tensor", bcmm,
                                 wait=[t_g, prev["x1_used"], prev["p1A"]], inc=MM)
                    t_bcc = sc.op("vector",
                                  lambda v: v.tensor_copy(out=wbc_sb[:], in_=PG[:, 0]),
                                  wait=[t_bc], inc=EV)
                else:
                    t_bcc = None
                xin = x1t_sb
                x_tok = t_x1

                for q4 in range(2):
                    def gmm4(t, q4=q4, xin=xin):
                        for k in range(4):
                            hc = q4 * 4 + k
                            for fc in range(16):
                                r = t.matmul(PG[:, k], w13_sb[:, fc, hc * 128:(hc + 1) * 128],
                                             xin[:, fc], start=(fc == 0), stop=(fc == 15))
                        return r

                    t_gm = sc.op("tensor", gmm4,
                                 wait=[x_tok, t_w13, prev["p1A"],
                                       t_bcc if (kind == "ex" and q4 == 0) else None],
                                 inc=MM)

                    def umm4(t, q4=q4, xin=xin):
                        for k in range(4):
                            hc = q4 * 4 + k
                            for fc in range(16):
                                r = t.matmul(PU[:, k],
                                             w13_sb[:, fc, HID + hc * 128:HID + (hc + 1) * 128],
                                             xin[:, fc], start=(fc == 0), stop=(fc == 15))
                        return r

                    t_um = sc.op("tensor", umm4, wait=[x_tok, t_w13, prev["p1B"]], inc=MM)
                    wide = lambda ap: ap.rearrange("p a t -> p (a t)")
                    wbc4 = bass.AP(wbc_sb[:].tensor, wbc_sb[:].offset,
                                   [list(wbc_sb[:].ap[0]), [0, 4], [1, TT]])
                    if kind == "ex":
                        def gsc(v, q4=q4):
                            return v.tensor_tensor(out=wide(sg2_f[:]), in0=wide(PG[:]),
                                                   in1=wbc4, op=ALU.mult)

                        t_gs = sc.op("vector", gsc, wait=[t_gm, t_bcc, prev["sg"][0]], inc=EV)
                        prev["p1A"] = t_gs
                        t_si = sc.op("scalar",
                                     lambda s: s.activation(out=wide(sg_f[:]),
                                                            in_=wide(sg2_f[:]), func=AF.Silu),
                                     wait=[t_gs], inc=AC)

                        def hmul4(v, q4=q4, tt4=tt4):
                            v.tensor_tensor(out=wide(sg2_f[:]), in0=wide(sg_f[:]),
                                            in1=wide(PU[:]), op=ALU.mult)
                            v.drain()
                            return v.tensor_tensor(
                                out=hT_sb[:, q4 * 4:(q4 + 1) * 4, tt4 * TT:(tt4 + 1) * TT],
                                in0=wide(sg2_f[:]), in1=wbc4, op=ALU.mult)

                        t_hm = sc.op("vector", hmul4,
                                     wait=[t_si, t_um, prev["hT_used"]], inc=EV)
                    else:
                        t_si = sc.op("scalar",
                                     lambda s: s.activation(out=wide(sg_f[:]),
                                                            in_=wide(PG[:]), func=AF.Silu),
                                     wait=[t_gm, prev["sg"][0]], inc=AC)
                        prev["p1A"] = t_si

                        def hmul4(v, q4=q4, tt4=tt4):
                            return v.tensor_tensor(
                                out=hT_sb[:, q4 * 4:(q4 + 1) * 4, tt4 * TT:(tt4 + 1) * TT],
                                in0=wide(sg_f[:]), in1=wide(PU[:]), op=ALU.mult)

                        t_hm = sc.op("vector", hmul4,
                                     wait=[t_si, t_um, prev["hT_used"]], inc=EV)
                    prev["p1B"] = t_hm
                    prev["sg"][0] = t_hm
                    hmuls.append(t_hm)
                prev["x1_used"] = hmuls[-1]
            t_p1 = hmuls[-1]

            for tt4 in range(4):
                tt = hf * 4 + tt4
                cs = (tt * TT, (tt + 1) * TT)
                for quad in range(4):
                    opar = quad % 2
                    evs = []
                    for j2 in range(2):
                        j0 = 2 * j2
                        oc0 = quad * 4 + j0
                        bk = (j2 % 2) * 2

                        def omm(t, oc0=oc0, tt4=tt4, bk=bk):
                            for k in range(2):
                                for hc in range(8):
                                    r = t.matmul(PG[:, bk + k],
                                                 w2_sb[:, hc, (oc0 + k) * 128:(oc0 + k + 1) * 128],
                                                 hT_sb[:, hc, tt4 * TT:(tt4 + 1) * TT],
                                                 start=(hc == 0), stop=(hc == 7))
                            return r

                        t_om = sc.op("tensor", omm,
                                     wait=[t_p1, t_w2,
                                           prev["p2evA"] if j2 % 2 == 0 else prev["p2evB"]],
                                     inc=MM)

                        def oev(v, j0=j0, bk=bk, opar=opar):
                            return v.tensor_copy(
                                out=oe_f[:, opar, j0:j0 + 2].rearrange("p a t -> p (a t)"),
                                in_=PG[:, bk:bk + 2].rearrange("p a t -> p (a t)"))

                        t_oe = sc.op("vector", oev,
                                     wait=[t_om, prev["oe_dma"][opar] if j2 == 0 else None],
                                     inc=EV)
                        evs.append(t_oe)
                        if j2 % 2 == 0:
                            prev["p2evA"] = t_oe
                        else:
                            prev["p2evB"] = t_oe

                    def accdma(g, cs=cs, quad=quad, kind=kind, opar=opar):
                        dst = kb.d["ACC"][quad * 512:(quad + 1) * 512,
                                          cs[0]:cs[1]].rearrange("(c p) t -> p c t", p=128)
                        if kind == "sh":
                            return g.dma_start(out=dst, in_=oe_f[:, opar])
                        return g.dma_start(out=dst, in_=oe_f[:, opar], accum_op=ALU.add)

                    t_acc = sc.op("gpsimd", accdma, wait=evs,
                                  inc=(DAC if opar == 0 else DA2), n=16)
                    prev["oe_dma"][opar] = t_acc
            prev["p1A"] = evs[-1]
            prev["hT_used"] = evs[-1]
        prev["w13_used"] = t_p1
        prev["w2_used"] = prev["oe_dma"][1]

    sc.op("sync", None, wait=[Tok(DAC, DAC.v), Tok(DA2, DA2.v)])
    with nc.Block() as blk:
        sc.emit(blk)



def p4b(kb, stack):
    nc = kb.nc
    sc = Sched(kb)
    MM, EV, AC = (kb.sem(x) for x in ("p4bmm", "p4bev", "p4bac"))
    LDX, LDA, LDW = (kb.sem(x) for x in ("p4bldx", "p4blda", "p4bldw"))
    STX = kb.sem("p4bstx")
    ctx = stack.enter_context
    ones_sb = ctx(nc.sbuf_tensor("p4b_ones", [128, 1], BF16))
    onesr_sb = ctx(nc.sbuf_tensor("p4b_onesr", [1, 128], BF16))
    eps_sb = ctx(nc.sbuf_tensor("p4b_eps", [1, 1], F32))
    x1t_sb = ctx(nc.sbuf_tensor("p4b_x1t", [128, 16, TT], BF16))
    ac_f = ctx(nc.sbuf_tensor("p4b_ac", [128, 16, TT], F32))
    r_f = ctx(nc.sbuf_tensor("p4b_rf", [128, 16, TT], F32))
    rb_b = ctx(nc.sbuf_tensor("p4b_rb", [128, 2, TT], BF16))
    sq_b = ctx(nc.sbuf_tensor("p4b_sq", [128, 2, TT], BF16))
    st_f = ctx(nc.sbuf_tensor("p4b_st", [1, 4, TT], F32))
    hl_b = ctx(nc.sbuf_tensor("p4b_hl", [1, 4, TT], BF16))
    mbc_sb = ctx(nc.sbuf_tensor("p4b_mbc", [128, TT], F32))
    rbc_sb = ctx(nc.sbuf_tensor("p4b_rbc", [128, TT], F32))
    t1_f = ctx(nc.sbuf_tensor("p4b_t1", [128, TT], F32))
    x2b_sb = ctx(nc.sbuf_tensor("p4b_x2b", [128, 16, TT], BF16))
    PM = ctx(nc.psum_tensor("p4b_PM", [1, TT], F32))
    PS = ctx(nc.psum_tensor("p4b_PS", [1, TT], F32))
    PBC = ctx(nc.psum_tensor("p4b_PBC", [128, TT], F32))

    for dst, srcd in ((ones_sb, kb.ones_bf), (onesr_sb, kb.ones_row), (eps_sb, kb.epsc)):
        sc.op("gpsimd", lambda g, dst=dst, srcd=srcd: g.dma_start(out=dst[:], in_=srcd),
              inc=LDW, n=16)
    t_w = Tok(LDW, LDW.v)

    prev = {"radd": None, "apply": None, "pm_free": None, "x2_store": None,
            "stat": [None, None]}
    for tt in range(NTT):
        cs = (tt * TT, (tt + 1) * TT)
        t_x1 = sc.op("gpsimd",
                     lambda g, cs=cs: g.dma_start(out=x1t_sb[:], in_=chunked(kb.d["X1"], cs)),
                     wait=[prev["radd"]], inc=LDX, n=16)
        t_ac = sc.op("gpsimd",
                     lambda g, cs=cs: g.dma_start(out=ac_f[:], in_=chunked(kb.d["ACC"], cs)),
                     wait=[prev["radd"]], inc=LDA, n=16)

        def radd(v):
            for oc in range(16):
                r = v.tensor_tensor(out=r_f[:, oc], in0=ac_f[:, oc], in1=x1t_sb[:, oc],
                                    op=ALU.add)
            return r

        t_radd = sc.op("vector", radd, wait=[t_x1, t_ac, prev["apply"]], inc=EV)
        prev["radd"] = t_radd

        t_stat = None
        for oc in range(16):
            par = oc % 2
            t_rb = sc.op("vector",
                         lambda v, oc=oc, par=par: v.tensor_copy(out=rb_b[:, par], in_=r_f[:, oc]),
                         wait=[t_radd, prev["stat"][par]], inc=EV)
            t_sq = sc.op("scalar",
                         lambda s, oc=oc, par=par: s.activation(out=sq_b[:, par], in_=r_f[:, oc],
                                                                func=AF.Square),
                         wait=[t_radd, prev["stat"][par]], inc=AC)

            def smm(t, oc=oc, par=par):
                t.matmul(PM[:], ones_sb[:], rb_b[:, par], start=(oc == 0), stop=(oc == 15))
                return t.matmul(PS[:], ones_sb[:], sq_b[:, par], start=(oc == 0), stop=(oc == 15))

            t_stat = sc.op("tensor", smm,
                           wait=[t_rb, t_sq, t_w, prev["pm_free"] if oc == 0 else None], inc=MM)
            prev["stat"][par] = t_stat

        t_bc, t_statsdone = ln_tail(
            sc, kb, nfeat=D, PMp=PM, PSp=PS, st_f=st_f, hl_b=hl_b,
            onesr_sb=onesr_sb, PBC=PBC, mbc_sb=mbc_sb, rbc_sb=rbc_sb,
            stat_tok=t_stat, bc_war_tok=prev["apply"], MM=MM, EV=EV, AC=AC,
            eps_ap=eps_sb[:, :])
        prev["pm_free"] = t_statsdone

        def apply(v):
            for oc in range(16):
                v.tensor_tensor(out=t1_f[:], in0=r_f[:, oc], in1=mbc_sb[:], op=ALU.subtract)
                v.drain()
                r = v.tensor_tensor(out=x2b_sb[:, oc], in0=t1_f[:], in1=rbc_sb[:], op=ALU.mult)
                v.drain()
            return r

        t_apply = sc.op("vector", apply, wait=[t_bc, t_radd, prev["x2_store"]], inc=EV)
        prev["apply"] = t_apply
        t_x2s = sc.op("sync",
                      lambda e, cs=cs: e.dma_start(out=chunked(kb.d["X2"], cs), in_=x2b_sb[:]),
                      wait=[t_apply], inc=STX, n=16)
        prev["x2_store"] = t_x2s

    sc.op("sync", None, wait=[Tok(STX, STX.v)])
    with nc.Block() as blk:
        sc.emit(blk)



def p5(kb, stack, sc=None):
    nc = kb.nc
    own = sc is None
    if own:
        sc = Sched(kb)
    MM, EV, AC = (kb.sem(x) for x in ("p5mm", "p5ev", "p5ac"))
    LDW, LDX, LDT = (kb.sem(x) for x in ("p5ldw", "p5ldx", "p5ldt"))
    STK, STQ, STV, STS = (kb.sem(x) for x in ("p5stk", "p5stq", "p5stv", "p5sts"))
    ctx = stack.enter_context
    wq_sb = ctx(nc.sbuf_tensor("p5_wq", [128, 16, D], BF16))
    wk_sb = ctx(nc.sbuf_tensor("p5_wk", [128, 16, NKV * HD], BF16))
    wv_sb = ctx(nc.sbuf_tensor("p5_wv", [128, 16, NKV * HD], BF16))
    ones_sb = ctx(nc.sbuf_tensor("p5_ones", [128, 1], BF16))
    x2t_sb = ctx(nc.sbuf_tensor("p5_x2t", [128, 16, TT], BF16))
    rtk_sb = ctx(nc.sbuf_tensor("p5_rtk", [128, 4, TT], BF16))
    rtq_sb = ctx(nc.sbuf_tensor("p5_rtq", [128, 4, TT], BF16))
    rsc_f = ctx(nc.sbuf_tensor("p5_rsc", [128, 2, TT], F32))
    olo_b = ctx(nc.sbuf_tensor("p5_olo", [128, TT], BF16))
    ohi_b = ctx(nc.sbuf_tensor("p5_ohi", [128, TT], BF16))
    ksq_b = ctx(nc.sbuf_tensor("p5_ksq", [128, 2, TT], BF16))
    n2_f = ctx(nc.sbuf_tensor("p5_n2", [1, 96], F32))
    vout_b = ctx(nc.sbuf_tensor("p5_vout", [128, 512], BF16))
    PQA = ctx(nc.psum_tensor("p5_PQA", [128, TT], F32))
    PQB = ctx(nc.psum_tensor("p5_PQB", [128, TT], F32))
    PV = ctx(nc.psum_tensor("p5_PV", [128, 512], F32))
    PW = ctx(nc.psum_tensor("p5_PW", [1, TT], F32))

    for dst, srcd in ((wq_sb, chunked(kb.d["wqT"])), (wk_sb, chunked(kb.d["wkT"])),
                      (wv_sb, chunked(kb.d["wvT"])), (ones_sb, kb.ones_bf)):
        sc.op("gpsimd", lambda g, dst=dst, srcd=srcd: g.dma_start(out=dst[:], in_=srcd),
              inc=LDW, n=16)
    t_w = Tok(LDW, LDW.v)

    prev = {"x2_used": None, "tab_used": None, "ropeA": None, "ropeB": None,
            "pv_ev": None, "norm": None, "pst": None, "vst": None}

    def rope_ops(v, cs, tabs):
        v.tensor_tensor(out=rsc_f[:, 0], in0=PQA[:], in1=tabs[:, 0], op=ALU.mult)
        v.tensor_tensor(out=rsc_f[:, 1], in0=PQB[:], in1=tabs[:, 1], op=ALU.mult)
        v.drain()
        v.tensor_tensor(out=olo_b[:], in0=rsc_f[:, 0], in1=rsc_f[:, 1], op=ALU.subtract)
        v.drain()
        v.tensor_tensor(out=rsc_f[:, 0], in0=PQB[:], in1=tabs[:, 2], op=ALU.mult)
        v.tensor_tensor(out=rsc_f[:, 1], in0=PQA[:], in1=tabs[:, 3], op=ALU.mult)
        v.drain()
        v.tensor_tensor(out=ohi_b[:], in0=rsc_f[:, 0], in1=rsc_f[:, 1], op=ALU.add)
        v.drain()
        v.tensor_tensor(out=ksq_b[:, 0], in0=olo_b[:], in1=olo_b[:], op=ALU.mult)
        return v.tensor_tensor(out=ksq_b[:, 1], in0=ohi_b[:], in1=ohi_b[:], op=ALU.mult)

    for tt in range(NTT):
        cs = (tt * TT, (tt + 1) * TT)
        t_x2 = sc.op("gpsimd",
                     lambda g, cs=cs: g.dma_start(out=x2t_sb[:], in_=chunked(kb.d["X2"], cs)),
                     wait=[prev["x2_used"]], inc=LDX, n=16)

        def tabld(g, cs=cs, tt=tt):
            r = [g.dma_start(out=rtk_sb[:, i], in_=kb.d["r1k"][:, i * S + cs[0]:i * S + cs[1]])
                 for i in range(4)]
            if tt < NTO:
                r += [g.dma_start(out=rtq_sb[:, i],
                                  in_=kb.d["r1q"][:, i * 2048 + cs[0]:i * 2048 + cs[1]])
                      for i in range(4)]
            return r

        t_tab = sc.op("gpsimd", tabld, wait=[prev["tab_used"]], inc=LDT, n=16,
                      ni=8 if tt < NTO else 4)

        jobs = [("k", kp) for kp in range(NKV // 2)]
        if tt < NTO:
            jobs += [("q", p) for p in range(NH // 2)]
        for kind, p in jobs:
            wsb = wk_sb if kind == "k" else wq_sb
            tabs = rtk_sb if kind == "k" else rtq_sb

            def pmm(t, wsb=wsb, p=p):
                for fc in range(16):
                    t.matmul(PQA[:], wsb[:, fc, p * 256:p * 256 + 128], x2t_sb[:, fc],
                             start=(fc == 0), stop=(fc == 15))
                for fc in range(16):
                    r = t.matmul(PQB[:], wsb[:, fc, p * 256 + 128:p * 256 + 256], x2t_sb[:, fc],
                                 start=(fc == 0), stop=(fc == 15))
                return r

            t_mm = sc.op("tensor", pmm, wait=[t_x2, t_w, prev["ropeA"], prev["ropeB"]], inc=MM)
            t_rope = sc.op("vector", lambda v, cs=cs, tabs=tabs: rope_ops(v, cs, tabs),
                           wait=[t_mm, t_tab, prev["pst"], prev["norm"]],
                           inc=EV)
            prev["ropeA"] = t_rope
            prev["ropeB"] = t_rope

            for hh, base in ((2 * p, 0), (2 * p + 1, 64)):
                def nmm(t, base=base):
                    t.matmul(PW[:], ones_sb[base:base + 64, :], ksq_b[base:base + 64, 0],
                             start=True, stop=False)
                    return t.matmul(PW[:], ones_sb[base:base + 64, :], ksq_b[base:base + 64, 1],
                                    start=False, stop=True)

                t_nm = sc.op("tensor", nmm, wait=[t_rope, prev["norm"]], inc=MM)
                slot = (64 + hh * 8 + tt) if kind == "k" else (hh * 4 + tt)
                t_nx = sc.op("vector",
                             lambda v, slot=slot: v.tensor_reduce(
                                 out=n2_f[:, slot:slot + 1], in_=PW[:], axis=AX.X, op=ALU.max),
                             wait=[t_nm], inc=EV)
                prev["norm"] = t_nx

            dname = "K1" if kind == "k" else "Q1"
            stsem = STK if kind == "k" else STQ

            def pst(e, p=p, cs=cs, dname=dname):
                dd = kb.d[dname]
                h0, h1 = 2 * p, 2 * p + 1
                return [
                    e.dma_start(out=dd[h0 * HD:h0 * HD + 64, cs[0]:cs[1]], in_=olo_b[0:64, :]),
                    e.dma_start(out=dd[h0 * HD + 64:(h0 + 1) * HD, cs[0]:cs[1]], in_=ohi_b[0:64, :]),
                    e.dma_start(out=dd[h1 * HD:h1 * HD + 64, cs[0]:cs[1]], in_=olo_b[64:128, :]),
                    e.dma_start(out=dd[h1 * HD + 64:(h1 + 1) * HD, cs[0]:cs[1]], in_=ohi_b[64:128, :])]

            t_st = sc.op("sync", pst, wait=[t_rope], inc=stsem, n=16, ni=4)
            prev["pst"] = t_st

        for j in range(4):
            sub = (cs[0] + j * 128, cs[0] + (j + 1) * 128)

            def vmm(t, j=j):
                for fc in range(16):
                    r = t.matmul(PV[:], x2t_sb[:, fc, j * 128:(j + 1) * 128], wv_sb[:, fc],
                                 start=(fc == 0), stop=(fc == 15))
                return r

            t_vm = sc.op("tensor", vmm, wait=[t_x2, t_w, prev["pv_ev"]], inc=MM)
            t_ve = sc.op("vector", lambda v: v.tensor_copy(out=vout_b[:], in_=PV[:]),
                         wait=[t_vm, prev["vst"]], inc=EV)
            prev["pv_ev"] = t_ve
            t_vs = sc.op("sync",
                         lambda e, sub=sub: e.dma_start(out=kb.d["V1"][sub[0]:sub[1], :],
                                                        in_=vout_b[:]),
                         wait=[t_ve], inc=STV, n=16)
            prev["vst"] = t_vs
        prev["x2_used"] = Tok(MM, MM.v)
        prev["tab_used"] = prev["ropeA"]

    t_ns = sc.op("sync", lambda e: e.dma_start(out=kb.d["QS1"], in_=n2_f[:]),
                 wait=[prev["norm"]], inc=STS, n=16)
    sc.op("sync", None, wait=[Tok(STK, STK.v), Tok(STQ, STQ.v), Tok(STV, STV.v),
                              Tok(STS, STS.v)])
    if own:
        with nc.Block() as blk:
            sc.emit(blk)
    return sc



def p7(kb, stack):
    nc = kb.nc
    sc = Sched(kb)
    MM, EV, AC = (kb.sem(x) for x in ("p7mm", "p7ev", "p7ac"))
    LDC = kb.sem("p7ldc")
    LQ = [kb.sem("p7lqa"), kb.sem("p7lqb")]
    STC = kb.sem("p7stc")
    ctx = stack.enter_context
    k_sb = ctx(nc.sbuf_tensor("p7_k", [128, 4, S], BF16))
    v_sb = ctx(nc.sbuf_tensor("p7_v", [128, 32, 512], BF16))
    qs_sb = ctx(nc.sbuf_tensor("p7_qs", [1, 96], F32))
    bs_f = ctx(nc.sbuf_tensor("p7_bs", [1, 4, 16], F32))
    bs_b = ctx(nc.sbuf_tensor("p7_bsb", [1, 16], BF16))
    bias_sb = ctx(nc.sbuf_tensor("p7_bias", [128, 16], F32))
    q_sb = ctx(nc.sbuf_tensor("p7_q", [128, 2, TT], BF16))
    P_sb = ctx(nc.sbuf_tensor("p7_P", [128, 2, 8, TT], BF16))
    su_f = ctx(nc.sbuf_tensor("p7_su", [1, 2, TT], F32))
    hl_b = ctx(nc.sbuf_tensor("p7_hl", [1, 2, TT], BF16))
    bc_f = ctx(nc.sbuf_tensor("p7_bc", [128, TT], F32))
    ct_b = ctx(nc.sbuf_tensor("p7_ct", [128, TT], BF16))
    ones_sb = ctx(nc.sbuf_tensor("p7_ones", [128, 1], BF16))
    onesr_sb = ctx(nc.sbuf_tensor("p7_onesr", [1, 128], BF16))
    PSC = ctx(nc.psum_tensor("p7_PSC", [128, 4, TT], F32))
    PZ = ctx(nc.psum_tensor("p7_PZ", [128, TT], F32))
    PW = ctx(nc.psum_tensor("p7_PW", [1, TT], F32))
    PBC = ctx(nc.psum_tensor("p7_PBC", [128, TT], F32))
    PBI = ctx(nc.psum_tensor("p7_PBI", [128, 16], F32))

    sc.op("gpsimd", lambda g: g.dma_start(out=ones_sb[:], in_=kb.ones_bf), inc=LDC, n=16)
    sc.op("gpsimd", lambda g: g.dma_start(out=onesr_sb[:], in_=kb.ones_row), inc=LDC, n=16)
    sc.op("gpsimd", lambda g: g.dma_start(out=qs_sb[:], in_=kb.d["QS1"]), inc=LDC, n=16)
    sc.op("gpsimd",
          lambda g: g.dma_start(out=k_sb[:], in_=kb.d["K1"].rearrange("(c p) t -> p c t", p=128)),
          inc=LDC, n=16)
    sc.op("gpsimd",
          lambda g: g.dma_start(out=v_sb[:], in_=kb.d["V1"].rearrange("(kt p) o -> p kt o", p=128)),
          inc=LDC, n=16)
    t_res = Tok(LDC, LDC.v)

    def bmax(v):
        for h in range(NH):
            v.tensor_reduce(out=bs_f[:, 0, h:h + 1], in_=qs_sb[:, h * 4:h * 4 + 4],
                            axis=AX.X, op=ALU.max)
            kvh = h // 4
            v.tensor_reduce(out=bs_f[:, 1, h:h + 1], in_=qs_sb[:, 64 + kvh * 8:64 + kvh * 8 + 8],
                            axis=AX.X, op=ALU.max)
        v.drain()
        return v.tensor_tensor(out=bs_f[:, 2], in0=bs_f[:, 0], in1=bs_f[:, 1], op=ALU.mult)

    t_bm = sc.op("vector", bmax, wait=[t_res], inc=EV)
    t_bsq = sc.op("scalar",
                  lambda s: s.activation(out=bs_f[:, 3], in_=bs_f[:, 2], func=AF.Sqrt),
                  wait=[t_bm], inc=AC)
    t_bneg = sc.op("vector",
                   lambda v: v.tensor_scalar(out=bs_b[:], in0=bs_f[:, 3], scalar1=-1.0,
                                             scalar2=None, op0=ALU.mult),
                   wait=[t_bsq], inc=EV)

    def bimm(t):
        for h in range(NH):
            r = t.matmul(PBI[:, h:h + 1], onesr_sb[:], bs_b[:, h:h + 1], start=True, stop=True)
        return r

    t_bimm = sc.op("tensor", bimm, wait=[t_bneg], inc=MM)
    t_bias = sc.op("vector", lambda v: v.tensor_copy(out=bias_sb[:], in_=PBI[:]),
                   wait=[t_bimm], inc=EV)

    prev = {"ctx_done": None, "qload": [None, None], "qtok": [None, None],
            "cmm2": None, "cmm3": None, "pending_epi": None, "st": None}
    units = [(h, qt) for h in range(NH) for qt in range(NTO)]

    def emit_qld(u):
        h_, qt_ = units[u]
        par_ = u % 2
        qs2 = (qt_ * TT, (qt_ + 1) * TT)
        prev["qtok"][par_] = sc.op(
            "gpsimd",
            lambda g, h_=h_, qs2=qs2, par_=par_: g.dma_start(
                out=q_sb[:, par_], in_=kb.d["Q1"][h_ * HD:(h_ + 1) * HD, qs2[0]:qs2[1]]),
            wait=[prev["qload"][par_]], inc=LQ[par_], n=16)

    for uu, (h, qt) in enumerate(units):
        kvh = h // 4
        if True:
            qs_ = (qt * TT, (qt + 1) * TT)
            par = uu % 2
            if uu == 0:
                emit_qld(0)
                emit_qld(1)
            elif uu + 1 < len(units):
                emit_qld(uu + 1)
            t_ql = prev["qtok"][par]

            t_exp4 = [None] * 4
            t_cmm4 = [None] * 4

            def emit_cmm(qq):
                def cmm8(t, kvh=kvh, qq=qq):
                    for jj in range(8):
                        kt = qq * 8 + jj
                        t.matmul(PZ[:], v_sb[:, kt, kvh * 128:(kvh + 1) * 128],
                                 P_sb[:, qq % 2, jj],
                                 start=(kt == 0), stop=(kt == 31))
                        r = t.matmul(PW[:], ones_sb[:], P_sb[:, qq % 2, jj],
                                     start=(kt == 0), stop=(kt == 31))
                    return r

                t_cmm4[qq] = sc.op("tensor", cmm8,
                                   wait=[t_exp4[qq],
                                         prev["ctx_done"] if qq == 0 else None],
                                   inc=MM)

            pairtok = prev.setdefault("evroll", [None, None])
            for q in range(4):
                qp_ = q % 2
                t_e = None
                for rr in range(4):
                    pr = rr % 2

                    def smm(t, kvh=kvh, q=q, rr=rr, pr=pr, par=par):
                        for u in range(2):
                            kt = q * 8 + 2 * rr + u
                            r = t.matmul(PSC[:, 2 * pr + u],
                                         k_sb[:, kvh, kt * 128:(kt + 1) * 128],
                                         q_sb[:, par], start=True, stop=True)
                        return r

                    t_s = sc.op("tensor", smm,
                                wait=[t_ql, t_res, pairtok[pr]],
                                inc=MM)

                    def expf(s, h=h, qp_=qp_, rr=rr, pr=pr):
                        return s.activation(
                            out=P_sb[:, qp_, 2 * rr:2 * rr + 2].rearrange("p a t -> p (a t)"),
                            in_=PSC[:, 2 * pr:2 * pr + 2].rearrange("p a t -> p (a t)"),
                            func=AF.Exp, bias=bias_sb[:, h:h + 1])

                    t_e = sc.op("scalar", expf,
                                wait=[t_s, t_bias,
                                      (t_cmm4[q - 2] if q >= 2 else
                                       prev["cmm2" if q == 0 else "cmm3"])
                                      if rr == 0 else None],
                                inc=AC)
                    pairtok[pr] = t_e
                    if q == 0 and rr == 1 and prev.get("pending_epi"):
                        prev["pending_epi"]()
                        prev["pending_epi"] = None
                t_exp4[q] = t_e
                if q >= 1:
                    emit_cmm(q - 1)
            emit_cmm(3)
            t_ctx = [t_cmm4[3]] * 32
            prev["qload"][par] = t_s
            prev["cmm2"], prev["cmm3"] = t_cmm4[2], t_cmm4[3]

            def epi1(v):
                v.tensor_copy(out=su_f[:, 0], in_=PW[:])
                v.drain()
                v.reciprocal(out=su_f[:, 1], in_=su_f[:, 0])
                v.drain()
                return emit_hi_lo(v, su_f[:, 1], hl_b[:, 0], hl_b[:, 1])

            t_ep1 = sc.op("vector", epi1, wait=[t_ctx[31]], inc=EV)

            def flush_epi(t_ep1=t_ep1, h=h, qs_=qs_):
                def epbc(t):
                    t.matmul(PBC[:], onesr_sb[:], hl_b[:, 0], start=True, stop=False)
                    return t.matmul(PBC[:], onesr_sb[:], hl_b[:, 1], start=False, stop=True)

                t_eb = sc.op("tensor", epbc, wait=[t_ep1], inc=MM)

                def epi2(v):
                    v.tensor_copy(out=bc_f[:], in_=PBC[:])
                    v.drain()
                    return v.tensor_tensor(out=ct_b[:], in0=PZ[:], in1=bc_f[:], op=ALU.mult)

                t_ep2 = sc.op("vector", epi2, wait=[t_eb, prev["st"]], inc=EV)
                prev["ctx_done"] = t_ep2
                t_st = sc.op("sync",
                             lambda e, h=h, qs_=qs_: e.dma_start(
                                 out=kb.d["CT1"][h * 128:(h + 1) * 128, qs_[0]:qs_[1]],
                                 in_=ct_b[:]),
                             wait=[t_ep2, prev["st"]], inc=STC, n=16)
                prev["st"] = t_st

            prev["pending_epi"] = flush_epi

    if prev.get("pending_epi"):
        prev["pending_epi"]()
        prev["pending_epi"] = None
    sc.op("sync", None, wait=[Tok(STC, STC.v)])
    with nc.Block() as blk:
        sc.emit(blk)



def p8(kb, stack):
    nc = kb.nc
    sc = Sched(kb)
    MM, EV, AC = (kb.sem(x) for x in ("p8mm", "p8ev", "p8ac"))
    LDW, LDC, LDX = (kb.sem(x) for x in ("p8ldw", "p8ldc", "p8ldx"))
    STX = kb.sem("p8stx")
    ctx = stack.enter_context
    ow_sb = ctx(nc.sbuf_tensor("p8_ow", [128, 16, D], BF16))
    ones_sb = ctx(nc.sbuf_tensor("p8_ones", [128, 1], BF16))
    onesr_sb = ctx(nc.sbuf_tensor("p8_onesr", [1, 128], BF16))
    eps_sb = ctx(nc.sbuf_tensor("p8_eps", [1, 1], F32))
    ct_sb = ctx(nc.sbuf_tensor("p8_ct", [128, 16, TT], BF16))
    x2t_sb = ctx(nc.sbuf_tensor("p8_x2t", [128, 16, TT], BF16))
    r_f = ctx(nc.sbuf_tensor("p8_rf", [128, 16, TT], F32))
    rb_b = ctx(nc.sbuf_tensor("p8_rb", [128, 2, TT], BF16))
    sq_b = ctx(nc.sbuf_tensor("p8_sq", [128, 2, TT], BF16))
    st_f = ctx(nc.sbuf_tensor("p8_st", [1, 4, TT], F32))
    hl_b = ctx(nc.sbuf_tensor("p8_hl", [1, 4, TT], BF16))
    mbc_sb = ctx(nc.sbuf_tensor("p8_mbc", [128, TT], F32))
    rbc_sb = ctx(nc.sbuf_tensor("p8_rbc", [128, TT], F32))
    t1_f = ctx(nc.sbuf_tensor("p8_t1", [128, TT], F32))
    x3b_sb = ctx(nc.sbuf_tensor("p8_x3b", [128, 16, TT], BF16))
    PA = ctx(nc.psum_tensor("p8_PA", [128, TT], F32))
    PB = ctx(nc.psum_tensor("p8_PB", [128, TT], F32))
    PM = ctx(nc.psum_tensor("p8_PM", [1, TT], F32))
    PS = ctx(nc.psum_tensor("p8_PS", [1, TT], F32))
    PBC = ctx(nc.psum_tensor("p8_PBC", [128, TT], F32))

    for dst, srcd in ((ow_sb, chunked(kb.d["woT"])), (ones_sb, kb.ones_bf),
                      (onesr_sb, kb.ones_row), (eps_sb, kb.epsc)):
        sc.op("gpsimd", lambda g, dst=dst, srcd=srcd: g.dma_start(out=dst[:], in_=srcd),
              inc=LDW, n=16)
    t_w = Tok(LDW, LDW.v)

    prev = {"mmA": None, "mmB": None, "apply": None, "pm_free": None,
            "x3_store": None, "stat": [None, None], "ct_used": None, "x2_used": None}
    for tt in range(NTO):
        cs = (tt * TT, (tt + 1) * TT)
        t_ct = sc.op("gpsimd",
                     lambda g, cs=cs: g.dma_start(out=ct_sb[:], in_=chunked(kb.d["CT1"], cs)),
                     wait=[prev["ct_used"]], inc=LDC, n=16)
        t_x2 = sc.op("gpsimd",
                     lambda g, cs=cs: g.dma_start(out=x2t_sb[:], in_=chunked(kb.d["X2"], cs)),
                     wait=[prev["x2_used"]], inc=LDX, n=16)

        t_radd = [None] * 16
        t_stat = None
        for oc in range(16):
            tgt = PA if oc % 2 == 0 else PB
            warp = prev["mmA"] if oc % 2 == 0 else prev["mmB"]

            def omm(t, oc=oc, tgt=tgt):
                for cc in range(16):
                    r = t.matmul(tgt[:], ow_sb[:, cc, oc * 128:(oc + 1) * 128],
                                 ct_sb[:, cc], start=(cc == 0), stop=(cc == 15))
                return r

            t_mm = sc.op("tensor", omm, wait=[t_ct, t_w, warp], inc=MM)
            t_radd[oc] = sc.op("vector",
                               lambda v, oc=oc, tgt=tgt: v.tensor_tensor(
                                   out=r_f[:, oc], in0=tgt[:], in1=x2t_sb[:, oc], op=ALU.add),
                               wait=[t_mm, t_x2, prev["apply"]], inc=EV)
            if oc % 2 == 0:
                prev["mmA"] = t_radd[oc]
            else:
                prev["mmB"] = t_radd[oc]
            par = oc % 2
            t_rb = sc.op("vector",
                         lambda v, oc=oc, par=par: v.tensor_copy(out=rb_b[:, par], in_=r_f[:, oc]),
                         wait=[t_radd[oc], prev["stat"][par]], inc=EV)
            t_sq = sc.op("scalar",
                         lambda s, oc=oc, par=par: s.activation(out=sq_b[:, par], in_=r_f[:, oc],
                                                                func=AF.Square),
                         wait=[t_radd[oc], prev["stat"][par]], inc=AC)

            def smm(t, oc=oc, par=par):
                t.matmul(PM[:], ones_sb[:], rb_b[:, par], start=(oc == 0), stop=(oc == 15))
                return t.matmul(PS[:], ones_sb[:], sq_b[:, par], start=(oc == 0), stop=(oc == 15))

            t_stat = sc.op("tensor", smm,
                           wait=[t_rb, t_sq, prev["pm_free"] if oc == 0 else None], inc=MM)
            prev["stat"][par] = t_stat

        t_bc, t_statsdone = ln_tail(
            sc, kb, nfeat=D, PMp=PM, PSp=PS, st_f=st_f, hl_b=hl_b,
            onesr_sb=onesr_sb, PBC=PBC, mbc_sb=mbc_sb, rbc_sb=rbc_sb,
            stat_tok=t_stat, bc_war_tok=prev["apply"], MM=MM, EV=EV, AC=AC,
            eps_ap=eps_sb[:, :])
        prev["pm_free"] = t_statsdone

        def apply(v):
            for oc in range(16):
                v.tensor_tensor(out=t1_f[:], in0=r_f[:, oc], in1=mbc_sb[:], op=ALU.subtract)
                v.drain()
                r = v.tensor_tensor(out=x3b_sb[:, oc], in0=t1_f[:], in1=rbc_sb[:], op=ALU.mult)
                v.drain()
            return r

        t_apply = sc.op("vector", apply, wait=[t_bc, prev["x3_store"]] + t_radd, inc=EV)
        prev["apply"] = t_apply
        prev["x2_used"] = t_apply
        prev["ct_used"] = Tok(MM, MM.v)
        t_x3s = sc.op("sync",
                      lambda e, cs=cs: e.dma_start(out=chunked(kb.d["X3"], cs), in_=x3b_sb[:]),
                      wait=[t_apply], inc=STX, n=16)
        prev["x3_store"] = t_x3s

    sc.op("sync", None, wait=[Tok(STX, STX.v)])
    with nc.Block() as blk:
        sc.emit(blk)



def p9(kb, stack, sc=None, precast=True):
    nc = kb.nc
    own = sc is None
    if own:
        sc = Sched(kb)
    MM, EV, AC = (kb.sem(x) for x in ("p9mm", "p9ev", "p9ac"))
    LDH = kb.sem("p9ldh")
    LB = [kb.sem("p9lba"), kb.sem("p9lbb")]
    LA = [kb.sem("p9laa"), kb.sem("p9lab")]
    STB = kb.sem("p9stb")
    SA = [kb.sem("p9saa"), kb.sem("p9sab")]
    ctx = stack.enter_context
    h64_sb = ctx(nc.sbuf_tensor("p9_h64", [64, 64], BF16))
    h128_sb = ctx(nc.sbuf_tensor("p9_h128", [128, 128], BF16))
    w2in = ctx(nc.sbuf_tensor("p9_w2in", [64, 2, D], BF16))
    fout = ctx(nc.sbuf_tensor("p9_fout", [64, D], BF16))
    fsin = ctx(nc.sbuf_tensor("p9_fsin", [128, 2, D], BF16))
    wa_b = ctx(nc.sbuf_tensor("p9_wa", [128, 2, 512], BF16))
    PN = ctx(nc.psum_tensor("p9_PN", [128, 2, 512], F32))

    sc.op("gpsimd", lambda g: g.dma_start(out=h64_sb[:], in_=kb.h64), inc=LDH, n=16)
    sc.op("gpsimd", lambda g: g.dma_start(out=h128_sb[:], in_=kb.h128), inc=LDH, n=16)
    t_h = Tok(LDH, LDH.v)

    prev = {"bmm": [None, None], "bst": None, "bev": None, "ain": None,
            "aev": [None, None], "ast": [None, None], "amm": [None, None]}
    t_cw = None
    if precast:
        CW = kb.sem("p9cw")
        sc.op("gpsimd", lambda g: g.dma_start(out=kb.d["W1B"], in_=kb.d["W1T"]), inc=CW, n=16)
        sc.op("gpsimd", lambda g: g.dma_start(out=kb.d["W3B"], in_=kb.d["W3T"]), inc=CW, n=16)
        t_cw = Tok(CW, CW.v)
    for j in range(128):
        par = j % 2

        def bload(g, j=j, par=par):
            srcv = kb.d["W2T"].rearrange("(a q) o -> q a o", q=128)[j:j + 1].rearrange(
                "z a o -> (z a) o")
            return g.dma_start(out=w2in[:, par], in_=srcv)

        t_bl = sc.op("gpsimd", bload, wait=[prev["bmm"][par]], inc=LB[par], n=16)

        t_be = None
        for seg in range(4):
            sp = seg % 2

            def bmm(t, par=par, seg=seg, sp=sp):
                return t.matmul(PN[0:64, sp], h64_sb[:],
                                w2in[:, par, seg * 512:(seg + 1) * 512], start=True, stop=True)

            t_bm = sc.op("tensor", bmm,
                         wait=[t_bl, t_h, prev["bev"][sp] if isinstance(prev["bev"], list) else prev["bev"]],
                         inc=MM)
            if seg == 3:
                prev["bmm"][par] = t_bm
            t_be = sc.op("vector",
                         lambda v, seg=seg, sp=sp: v.tensor_copy(
                             out=fout[:, seg * 512:(seg + 1) * 512], in_=PN[0:64, sp]),
                         wait=[t_bm, prev["bst"] if seg <= 1 else None], inc=EV)
            if not isinstance(prev["bev"], list):
                prev["bev"] = [None, None]
            prev["bev"][sp] = t_be

        def bstore(e, j=j):
            dstv = kb.d["FS"].rearrange("(a q) o -> q a o", q=128)[j:j + 1].rearrange(
                "z a o -> (z a) o")
            return e.dma_start(out=dstv, in_=fout[:])

        t_bs = sc.op("sync", bstore, wait=[t_be], inc=STB, n=16)
        prev["bst"] = t_bs
    t_ball = Tok(STB, STB.v)

    SCW = float(1.0 / np.sqrt(FF))
    for c in range(64):
        par = c % 2
        t_al = sc.op("gpsimd",
                     lambda g, c=c, par=par: g.dma_start(
                         out=fsin[:, par], in_=kb.d["FS"][c * 128:(c + 1) * 128, :]),
                     wait=[t_ball, prev["amm"][par]], inc=LA[par], n=16)
        for j4 in range(4):
            pj = j4 % 2

            def amm(t, par=par, j4=j4, pj=pj):
                return t.matmul(PN[:, pj], h128_sb[:], fsin[:, par, j4 * 512:(j4 + 1) * 512],
                                start=True, stop=True)

            t_am = sc.op("tensor", amm,
                         wait=[t_al, t_h, prev["aev"][pj], t_ball], inc=MM)
            if j4 == 3:
                prev["amm"][par] = t_am

            t_ae = sc.op("scalar",
                         lambda s, pj=pj: s.mul(wa_b[:, pj], PN[:, pj], SCW),
                         wait=[t_am, prev["ast"][pj]], inc=AC)
            prev["aev"][pj] = t_ae

            def astore(e, c=c, j4=j4, pj=pj):
                dstv = kb.d["W2X"][j4 * 4:(j4 + 1) * 4, c].rearrange("a p o -> p a o")
                return e.dma_start(out=dstv, in_=wa_b[:, pj].rearrange("p (a o) -> p a o", a=4))

            t_as = sc.op("sync", astore, wait=[t_ae], inc=SA[pj], n=16)
            prev["ast"][pj] = t_as

    sc.op("sync", None, wait=[Tok(SA[0], SA[0].v), Tok(SA[1], SA[1].v), t_cw])
    if own:
        with nc.Block() as blk:
            sc.emit(blk)
    return sc



def p10(kb, stack):
    nc = kb.nc
    sc = Sched(kb)
    MM, EV, AC = (kb.sem(x) for x in ("p10mm", "p10ev", "p10ac"))
    LDW, LW3, LD2, LDX, LDC = (kb.sem(x) for x in ("p10ldw", "p10lw3", "p10ld2", "p10ldx", "p10ldc"))
    LW1P = [kb.sem("p10lw1a"), kb.sem("p10lw1b")]
    LW3P = [kb.sem("p10lw3a"), kb.sem("p10lw3b")]
    STO = kb.sem("p10sto")
    ctx = stack.enter_context
    w1g_sb = ctx(nc.sbuf_tensor("p10_w1g", [128, 2, 16, 256], BF16))
    w3g_sb = ctx(nc.sbuf_tensor("p10_w3g", [128, 2, 16, 256], BF16))
    w2x_sb = ctx(nc.sbuf_tensor("p10_w2x", [128, 2, 64, 128], BF16))
    hT_sb = ctx(nc.sbuf_tensor("p10_hT", [128, 64, TT], BF16))
    x3t_sb = ctx(nc.sbuf_tensor("p10_x3t", [128, 16, TT], BF16))
    sg_f = ctx(nc.sbuf_tensor("p10_sg", [128, 1, TT], F32))
    r_f = ctx(nc.sbuf_tensor("p10_rf", [128, 16, TT], F32))
    rb_b = ctx(nc.sbuf_tensor("p10_rb", [128, 2, TT], BF16))
    sq_b = ctx(nc.sbuf_tensor("p10_sq", [128, 2, TT], BF16))
    st_f = ctx(nc.sbuf_tensor("p10_st", [1, 4, TT], F32))
    hl_b = ctx(nc.sbuf_tensor("p10_hl", [1, 4, TT], BF16))
    mbc_sb = ctx(nc.sbuf_tensor("p10_mbc", [128, TT], F32))
    rbc_sb = ctx(nc.sbuf_tensor("p10_rbc", [128, TT], F32))
    t1_f = ctx(nc.sbuf_tensor("p10_t1", [128, TT], F32))
    o_f = ctx(nc.sbuf_tensor("p10_of", [128, 1, TT], F32))
    ones_sb = ctx(nc.sbuf_tensor("p10_ones", [128, 1], BF16))
    onesr_sb = ctx(nc.sbuf_tensor("p10_onesr", [1, 128], BF16))
    eps_sb = ctx(nc.sbuf_tensor("p10_eps", [1, 1], F32))
    PA = ctx(nc.psum_tensor("p10_PA", [128, TT], F32))
    PB = ctx(nc.psum_tensor("p10_PB", [128, TT], F32))
    PC = ctx(nc.psum_tensor("p10_PC", [128, TT], F32))
    PD = ctx(nc.psum_tensor("p10_PD", [128, TT], F32))
    PM = ctx(nc.psum_tensor("p10_PM", [1, TT], F32))
    PS = ctx(nc.psum_tensor("p10_PS", [1, TT], F32))
    PBC = ctx(nc.psum_tensor("p10_PBC", [128, TT], F32))

    for dst, srcd in ((ones_sb, kb.ones_bf), (onesr_sb, kb.ones_row), (eps_sb, kb.epsc)):
        sc.op("gpsimd", lambda g, dst=dst, srcd=srcd: g.dma_start(out=dst[:], in_=srcd),
              inc=LDC, n=16)
    t_w = Tok(LDC, LDC.v)

    prev = {"p1A": [None, None], "p1B": [None, None], "wh_used": [None, None],
            "x3_used": None,
            "hT_used": None, "sg": [None, None], "w2_used": [None, None],
            "radd": [None, None], "apply": None, "pm_free": None,
            "stat": [None, None], "ost": None, "out_ev": [None, None]}
    for tt in range(NTO):
        cs = (tt * TT, (tt + 1) * TT)
        t_x3 = sc.op("gpsimd",
                     lambda g, cs=cs: g.dma_start(out=x3t_sb[:], in_=chunked(kb.d["X3"], cs)),
                     wait=[prev["x3_used"]], inc=LDX, n=16)

        t_hm = None
        for hg in range(16):
            for hf in range(2):
                ph = (hg * 2 + hf) % 2
                c0 = hg * 512 + hf * 256
                t_w1 = sc.op("gpsimd",
                             lambda g, c0=c0, ph=ph: g.dma_start(
                                 out=w1g_sb[:, ph], in_=chunked(kb.d["W1B"], (c0, c0 + 256))),
                             wait=[prev["wh_used"][ph]], inc=LW1P[ph], n=16)
                t_w3 = sc.op("gpsimd",
                             lambda g, c0=c0, ph=ph: g.dma_start(
                                 out=w3g_sb[:, ph], in_=chunked(kb.d["W3B"], (c0, c0 + 256))),
                             wait=[prev["wh_used"][ph]], inc=LW3P[ph], n=16)
                for k in range(2):
                    hc8 = hf * 2 + k
                    hc = hg * 4 + hc8
                    par = hc % 2
                    G = PA if par == 0 else PC
                    U = PB if par == 0 else PD

                    def gmm(t, k=k, ph=ph, G=G):
                        for fc in range(16):
                            r = t.matmul(G[:], w1g_sb[:, ph, fc, k * 128:(k + 1) * 128],
                                         x3t_sb[:, fc], start=(fc == 0), stop=(fc == 15))
                        return r

                    t_gm = sc.op("tensor", gmm, wait=[t_x3, t_w1, prev["p1A"][par]], inc=MM)

                    def umm(t, k=k, ph=ph, U=U):
                        for fc in range(16):
                            r = t.matmul(U[:], w3g_sb[:, ph, fc, k * 128:(k + 1) * 128],
                                         x3t_sb[:, fc], start=(fc == 0), stop=(fc == 15))
                        return r

                    t_um = sc.op("tensor", umm, wait=[t_x3, t_w3, prev["p1B"][par]], inc=MM)
                    t_si = sc.op("scalar",
                                 lambda s, par=par, G=G: s.activation(out=sg_f[:, 0], in_=G[:],
                                                                      func=AF.Silu),
                                 wait=[t_gm, prev["sg"][0]], inc=AC)
                    prev["p1A"][par] = t_si
                    t_hm = sc.op("vector",
                                 lambda v, hc=hc, U=U: v.tensor_tensor(
                                     out=hT_sb[:, hc], in0=sg_f[:, 0], in1=U[:], op=ALU.mult),
                                 wait=[t_si, t_um, prev["hT_used"] if hc <= 1 else None], inc=EV)
                    prev["p1B"][par] = t_hm
                    prev["sg"][0] = t_hm
                prev["wh_used"][ph] = t_um
        t_p1 = prev["p1B"][1]

        t_radds = []
        t_stat = None
        for oc in range(16):
            par = oc % 2
            tgt = PA if par == 0 else PB
            t_w2 = sc.op("gpsimd",
                         lambda g, oc=oc, par=par: g.dma_start(
                             out=w2x_sb[:, par], in_=kb.d["W2X"][oc].rearrange("a p o -> p a o")),
                         wait=[prev["w2_used"][par], t_p1 if oc <= 1 else None],
                         inc=(LD2 if par == 0 else LW3), n=16)

            def omm(t, tgt=tgt, par=par):
                for hc in range(64):
                    r = t.matmul(tgt[:], w2x_sb[:, par, hc, :], hT_sb[:, hc],
                                 start=(hc == 0), stop=(hc == 63))
                return r

            t_om = sc.op("tensor", omm,
                         wait=[t_w2, t_p1, prev["radd"][par]], inc=MM)
            prev["w2_used"][par] = t_om

            t_ra = sc.op("vector",
                         lambda v, oc=oc, tgt=tgt: v.tensor_tensor(
                             out=r_f[:, oc], in0=tgt[:], in1=x3t_sb[:, oc], op=ALU.add),
                         wait=[t_om, prev["apply"]], inc=EV)
            prev["radd"][par] = t_ra
            t_radds.append(t_ra)
            t_rb = sc.op("vector",
                         lambda v, oc=oc, par=par: v.tensor_copy(out=rb_b[:, par], in_=r_f[:, oc]),
                         wait=[t_ra, prev["stat"][par]], inc=EV)
            t_sq = sc.op("scalar",
                         lambda s, oc=oc, par=par: s.activation(out=sq_b[:, par], in_=r_f[:, oc],
                                                                func=AF.Square),
                         wait=[t_ra, prev["stat"][par]], inc=AC)

            def smm(t, oc=oc, par=par):
                t.matmul(PM[:], ones_sb[:], rb_b[:, par], start=(oc == 0), stop=(oc == 15))
                return t.matmul(PS[:], ones_sb[:], sq_b[:, par], start=(oc == 0), stop=(oc == 15))

            t_stat = sc.op("tensor", smm,
                           wait=[t_rb, t_sq, t_w, prev["pm_free"] if oc == 0 else None], inc=MM)
            prev["stat"][par] = t_stat
        prev["x3_used"] = t_radds[-1]
        prev["hT_used"] = prev["w2_used"][1]
        prev["p1A"] = [prev["radd"][0], prev["p1A"][1]]
        prev["p1B"] = [prev["radd"][1], prev["p1B"][1]]

        t_bc, t_statsdone = ln_tail(
            sc, kb, nfeat=D, PMp=PM, PSp=PS, st_f=st_f, hl_b=hl_b,
            onesr_sb=onesr_sb, PBC=PBC, mbc_sb=mbc_sb, rbc_sb=rbc_sb,
            stat_tok=t_stat, bc_war_tok=prev["apply"], MM=MM, EV=EV, AC=AC,
            eps_ap=eps_sb[:, :])
        prev["pm_free"] = t_statsdone

        for oc in range(16):
            def apply1(v, oc=oc):
                v.tensor_tensor(out=t1_f[:], in0=r_f[:, oc], in1=mbc_sb[:], op=ALU.subtract)
                v.drain()
                r = v.tensor_tensor(out=o_f[:, 0], in0=t1_f[:], in1=rbc_sb[:], op=ALU.mult)
                v.drain()
                return r

            t_ap = sc.op("vector", apply1,
                         wait=[t_bc, t_radds[oc], prev["out_ev"][0]], inc=EV)
            prev["apply"] = t_ap
            t_os = sc.op("sync",
                         lambda e, oc=oc, cs=cs: e.dma_start(
                             out=kb.d["OUT"][oc * 128:(oc + 1) * 128, cs[0]:cs[1]],
                             in_=o_f[:, 0]),
                         wait=[t_ap], inc=STO, n=16)
            prev["out_ev"][0] = t_os

    sc.op("sync", None, wait=[Tok(STO, STO.v)])
    with nc.Block() as blk:
        sc.emit(blk)



PHASE_FNS = {"p0": p0, "p2": p2, "p3": p3, "p4": p4, "p4b": p4b,
             "p5": p5, "p7": p7, "p8": p8, "p9": p9, "p10": p10}

from contextlib import ExitStack


def build_program(phases):
    kb = KB(phases)
    skip = set()
    for ph in phases:
        if ph in skip:
            continue
        if ph == "p5" and "p9" in phases:
            with ExitStack() as stack:
                sc = Sched(kb)
                p5(kb, stack, sc)
                sc9 = Sched(kb)
                p9(kb, stack, sc9, precast=not ("p2" in phases))
                sc.merge(sc9)
                with kb.nc.Block() as blk:
                    sc.emit(blk)
            skip.add("p9")
            continue
        if ph == "p2" and "p9" in phases and "p5" in phases:
            with ExitStack() as stack:
                p2(kb, stack, precast=True)
            continue
        with ExitStack() as stack:
            PHASE_FNS[ph](kb, stack)
    return kb



def _cont(a):
    return np.ascontiguousarray(a)


def host_prep(inputs):
    x = np.asarray(inputs["L0_attn_hidden_states"])
    f32 = np.float32
    shared = {
        "qwT": None,
        "qwTpe": None,
        "kvdT": _cont(np.asarray(inputs["L0_attn_kv_down_w"]).T),
        "lng": _cont(np.asarray(inputs["L0_attn_ln_g"]).reshape(4, 128).T.astype(f32)),
        "lnb": _cont(np.asarray(inputs["L0_attn_ln_b"]).reshape(4, 128).T.astype(f32)),
        "kvuT": _cont(np.asarray(inputs["L0_attn_kv_up_w"]).T),
        "owT": _cont(np.asarray(inputs["L0_attn_out_w"]).T),
        "gT": _cont(np.asarray(inputs["L0_ffn_gate"]).T),
        "ebc": _cont(np.broadcast_to(np.asarray(inputs["L0_ffn_expert_bias"]), (128, 8)).astype(f32)),
        "w1T": _cont(np.transpose(np.asarray(inputs["L0_ffn_w1"]), (0, 2, 1))),
        "w2T": _cont(np.transpose(np.asarray(inputs["L0_ffn_w2"]), (0, 2, 1))),
        "w3T": _cont(np.transpose(np.asarray(inputs["L0_ffn_w3"]), (0, 2, 1))),
        "sw1T": _cont(np.asarray(inputs["L0_ffn_shared_w1"]).T),
        "sw2T": _cont(np.asarray(inputs["L0_ffn_shared_w2"]).T),
        "sw3T": _cont(np.asarray(inputs["L0_ffn_shared_w3"]).T),
        "wqT": None, "wkT": None,
        "wvT": _cont(np.asarray(inputs["L1_attn_Wv"]).T),
        "woT": _cont(np.asarray(inputs["L1_attn_Wo"]).T),
        "W1T": _cont(np.asarray(inputs["L1_ffn_W1"]).T),
        "W3T": _cont(np.asarray(inputs["L1_ffn_W3"]).T),
        "W2T": _cont(np.asarray(inputs["L1_ffn_W2"]).T),
    }
    def pair_cols(nheads):
        idx = []
        for p in range(nheads // 2):
            h0, h1 = 2 * p, 2 * p + 1
            idx += [np.arange(h0 * HD, h0 * HD + 64), np.arange(h1 * HD, h1 * HD + 64),
                    np.arange(h0 * HD + 64, (h0 + 1) * HD), np.arange(h1 * HD + 64, (h1 + 1) * HD)]
        return np.concatenate(idx)

    shared["wqT"] = _cont(np.asarray(inputs["L1_attn_Wq"]).T[:, pair_cols(NH)])
    shared["wkT"] = _cont(np.asarray(inputs["L1_attn_Wk"]).T[:, pair_cols(NKV)])
    qwT_full = np.asarray(inputs["L0_attn_q_w"]).T
    idx_nope = np.concatenate([np.arange(h * QHD, h * QHD + DQK) for h in range(NH)])
    shared["qwT"] = _cont(qwT_full[:, idx_nope])
    idx_pe = []
    for p in range(NH // 2):
        h0, h1 = 2 * p, 2 * p + 1
        idx_pe += [np.arange(h0 * QHD + 128, h0 * QHD + 160),
                   np.arange(h1 * QHD + 128, h1 * QHD + 160),
                   np.arange(h0 * QHD + 160, h0 * QHD + 192),
                   np.arange(h1 * QHD + 160, h1 * QHD + 192)]
    shared["qwTpe"] = _cont(qwT_full[:, np.concatenate(idx_pe)])
    inv0 = (1.0 / (10000.0 ** (np.arange(0, DPE, 2, dtype=f32) / DPE))).astype(f32)
    inv1 = (1.0 / (10000.0 ** (np.arange(0, HD, 2, dtype=f32) / HD))).astype(f32)
    percore = []
    for c in range(8):
        b, h = c // 2, c % 2
        perm = np.concatenate([np.arange(2048) + h * 2048, np.arange(2048) + (1 - h) * 2048])
        pos = perm.astype(f32)
        fr0 = pos[:, None] * inv0[None, :]
        c0, s0 = np.cos(fr0).astype(f32), np.sin(fr0).astype(f32)
        fr1 = pos[:, None] * inv1[None, :]
        emb = np.concatenate([fr1, fr1], axis=1)
        c1, s1 = np.cos(emb).astype(f32), np.sin(emb).astype(f32)
        d = {
            "xT": _cont(x[b].T[:, perm].astype(f32)),
            "r0k": _cont(np.concatenate([c0.T, s0.T], axis=1)),
            "r0q": _cont(np.concatenate([
                np.concatenate([c0.T, c0.T], axis=0) * SC0,
                np.concatenate([s0.T, s0.T], axis=0) * SC0], axis=1)),
            "r1k": _cont(np.concatenate([
                np.concatenate([c1.T[0:64], c1.T[0:64]], axis=0),
                np.concatenate([s1.T[0:64], s1.T[0:64]], axis=0),
                np.concatenate([c1.T[64:128], c1.T[64:128]], axis=0),
                np.concatenate([s1.T[64:128], s1.T[64:128]], axis=0)], axis=1)),
            "r1q": _cont(np.concatenate([
                np.concatenate([c1.T[0:64, 0:2048], c1.T[0:64, 0:2048]], axis=0),
                np.concatenate([s1.T[0:64, 0:2048], s1.T[0:64, 0:2048]], axis=0),
                np.concatenate([c1.T[64:128, 0:2048], c1.T[64:128, 0:2048]], axis=0),
                np.concatenate([s1.T[64:128, 0:2048], s1.T[64:128, 0:2048]], axis=0)], axis=1) * SC1),
        }
        percore.append(d)
    return shared, percore


LAUNCH_PLAN = [list(ALL_PHASES)]


def run_launches(inputs, launch_plan=None, extra_outputs=()):
    launch_plan = launch_plan or LAUNCH_PLAN
    shared, percore = host_prep(inputs)
    carry = [dict() for _ in range(8)]
    for phases in launch_plan:
        kb = build_program(phases)
        in_maps = []
        for c in range(8):
            m = {}
            for name in kb.in_names:
                if name in shared:
                    m[name] = shared[name]
                elif name in percore[c]:
                    m[name] = percore[c][name]
                else:
                    m[name] = carry[c][name]
            in_maps.append(m)
        res = run_bass_kernel_spmd(kb.nc, in_maps, core_ids=list(range(8)))
        for c in range(8):
            for name in kb.out_names:
                carry[c][name] = res.results[c][name]
    return carry


def kernel(**inputs):
    carry = run_launches(inputs)
    out = np.zeros((B, S, D), np.float32)
    for c in range(8):
        b, h = c // 2, c % 2
        out[b, h * 2048:(h + 1) * 2048, :] = carry[c]["OUT"].T
    return out


# version 12
# speedup vs baseline: 1.0502x; 1.0502x over previous
import numpy as np
import ml_dtypes
import concourse.bass as bass
import concourse.mybir as mybir
from concourse.bass_utils import run_bass_kernel_spmd

F32 = mybir.dt.float32
BF16 = mybir.dt.bfloat16
AF = mybir.ActivationFunctionType
ALU = mybir.AluOpType
AX = mybir.AxisListType
BF = ml_dtypes.bfloat16

B, S, D = 4, 4096, 2048
NH, DQK, DPE, DV = 16, 128, 64, 128
QHD = DQK + DPE
LATD = 512
NE, HID = 8, 1024
NKV, HD = 4, 128
FF = 8192
TT = 512
NTT = S // TT
NTO = 2048 // TT
EPS = 1e-5
SC0 = float(1.0 / np.sqrt(QHD))
SC1 = float(1.0 / np.sqrt(HD))
NEGBIG = -1.0e30

ALL_PHASES = ("p0", "p2", "p3", "p4", "p4b", "p5", "p9", "p7", "p8", "p10")

DRAM_SPECS = {
    "Q0":  ([NH * QHD, S], BF16, "p0", ("p2",)),
    "LAT": ([LATD + DPE, S], BF16, "p0", ("p2",)),
    "QS0": ([1, 128], F32, "p0", ("p2",)),
    "CT0": ([D, S], BF16, "p2", ("p3",)),
    "X1":  ([D, S], BF16, "p3", ("p4", "p4b")),
    "WG8": ([8, S], BF16, "p3", ("p4",)),
    "ACC": ([D, S], F32, "p4", ("p4b",)),
    "X2":  ([D, S], BF16, "p4b", ("p5", "p8")),
    "Q1":  ([D, 2048], BF16, "p5", ("p7",)),
    "K1":  ([NKV * HD, S], BF16, "p5", ("p7",)),
    "V1":  ([S, NKV * HD], BF16, "p5", ("p7",)),
    "QS1": ([1, 96], F32, "p5", ("p7",)),
    "CT1": ([D, 2048], BF16, "p7", ("p8",)),
    "X3":  ([D, 2048], BF16, "p8", ("p10",)),
    "FS":  ([FF, D], BF16, "p9", ()),
    "W1B": ([D, FF], BF16, "p9", ("p10",)),
    "W3B": ([D, FF], BF16, "p9", ("p10",)),
    "W2X": ([16, 64, 128, 128], BF16, "p9", ("p10",)),
    "W1E": ([NE, D, HID], BF16, "p2", ("p4",)),
    "W3E": ([NE, D, HID], BF16, "p2", ("p4",)),
    "W2E": ([NE, HID, D], BF16, "p2", ("p4",)),
    "OUT": ([D, 2048], F32, "p10", ()),
}

IN_SPECS = {
    "xT":   ([D, S], F32, ("p0", "p3")),
    "qwT":  ([D, NH * DQK], F32, ("p0",)),
    "qwTpe": ([D, NH * DPE], F32, ("p0",)),
    "kvdT": ([D, LATD + DPE], F32, ("p0",)),
    "lng":  ([128, 4], F32, ("p0",)),
    "lnb":  ([128, 4], F32, ("p0",)),
    "r0k":  ([32, 2 * S], F32, ("p0",)),
    "r0q":  ([64, 2 * S], F32, ("p0",)),
    "kvuT": ([LATD, NH * (DQK + DV)], F32, ("p2",)),
    "owT":  ([D, D], F32, ("p3",)),
    "gT":   ([D, 8], F32, ("p3",)),
    "ebc":  ([128, 8], F32, ("p3",)),
    "w1T":  ([NE, D, HID], F32, ("p4", "p2")),
    "w2T":  ([NE, HID, D], F32, ("p4", "p2")),
    "w3T":  ([NE, D, HID], F32, ("p4", "p2")),
    "sw1T": ([D, HID], F32, ("p4",)),
    "sw2T": ([HID, D], F32, ("p4",)),
    "sw3T": ([D, HID], F32, ("p4",)),
    "wqT":  ([D, D], F32, ("p5",)),
    "wkT":  ([D, NKV * HD], F32, ("p5",)),
    "wvT":  ([D, NKV * HD], F32, ("p5",)),
    "r1k":  ([128, 4 * S], F32, ("p5",)),
    "r1q":  ([128, 4 * 2048], F32, ("p5",)),
    "woT":  ([D, D], F32, ("p8",)),
    "W2T":  ([FF, D], F32, ("p9",)),
    "W1T":  ([D, FF], F32, ("p9",)),
    "W3T":  ([D, FF], F32, ("p9",)),
}


def _hadamard(n):
    h = np.array([[1.0]], dtype=np.float32)
    while h.shape[0] < n:
        h = np.block([[h, h], [h, -h]])
    return h


class Sem:
    __slots__ = ("h", "v")

    def __init__(self, nc, name):
        self.h = nc.alloc_semaphore(name=name)
        self.v = 0


class Tok:
    __slots__ = ("sem", "val")

    def __init__(self, sem, val):
        self.sem = sem
        self.val = val


class Sched:

    ENGINES = ("gpsimd", "sync", "tensor", "scalar", "vector")

    def __init__(self, kb):
        self.kb = kb
        self.qs = {e: [] for e in self.ENGINES}
        self.seq = 0

    def op(self, engine, fn, wait=(), inc=None, n=1, ni=1):
        tok = None
        if inc is not None:
            inc.v += n * ni
            tok = Tok(inc, inc.v)
        waits = [(w.sem, w.val) for w in wait if w is not None]
        self.qs[engine].append((fn, waits, inc, n, ni, self.seq))
        self.seq += 1
        return tok

    def merge(self, other):
        ta, tb = max(self.seq, 1), max(other.seq, 1)
        for e in self.ENGINES:
            merged = sorted(
                [(op[-1] / ta, 0, k) for k, op in enumerate(self.qs[e])]
                + [(op[-1] / tb, 1, k) for k, op in enumerate(other.qs[e])])
            self.qs[e] = [(self.qs[e] if ph == 0 else other.qs[e])[k]
                          for _, ph, k in merged]

    def emit(self, blk):
        for name in self.ENGINES:
            ops = self.qs[name]
            if not ops:
                continue
            dec = getattr(blk, name)

            def run(eng, ops=ops):
                seen = {}
                for fn, waits, inc, n, ni, _ in ops:
                    for sem, val in waits:
                        if seen.get(id(sem), -1) >= val:
                            continue
                        eng.wait_ge(sem.h, val)
                        seen[id(sem)] = val
                    inst = fn(eng) if fn is not None else None
                    if inc is not None:
                        insts = inst if isinstance(inst, (list, tuple)) else [inst]
                        assert len(insts) == ni and all(i is not None for i in insts)
                        for i in insts:
                            i.then_inc(inc.h, n)

            dec(run)


class KB:

    def __init__(self, phases):
        self.phases = list(phases)
        nc = bass.Bass("TRN2", target_bir_lowering=False)
        self.nc = nc
        pset = set(self.phases)
        self.d = {}
        self.in_names = []
        self.out_names = []
        for name, (shape, dt, consumers) in IN_SPECS.items():
            if pset & set(consumers):
                self.d[name] = nc.dram_tensor(name, shape, dt, kind="ExternalInput").ap()
                self.in_names.append(name)
        for name, (shape, dt, prod, cons) in DRAM_SPECS.items():
            inside_p = prod in pset
            needed = pset & set(cons)
            later = [c for c in cons if c not in pset]
            if inside_p and (name == "OUT" or later):
                self.d[name] = nc.dram_tensor(name, shape, dt, kind="ExternalOutput").ap()
                self.out_names.append(name)
            elif inside_p:
                self.d[name] = nc.dram_tensor(name, shape, dt, kind="Internal").ap()
            elif needed:
                self.d[name] = nc.dram_tensor(name, shape, dt, kind="ExternalInput").ap()
                self.in_names.append(name)
        self.ones_bf = nc.inline_tensor(np.ones((128, 1), dtype=BF), name="c_ones").ap()
        self.ones_row = nc.inline_tensor(np.ones((1, 128), dtype=BF), name="c_onesr").ap()
        sel = np.zeros((8, 8 * 128), dtype=BF)
        for e in range(8):
            sel[e, e * 128:(e + 1) * 128] = 1
        self.sel8 = nc.inline_tensor(sel, name="c_sel8").ap()
        self.ident = nc.inline_tensor(np.eye(128, dtype=BF), name="c_ident").ap()
        self.h64 = nc.inline_tensor(_hadamard(64).astype(BF), name="c_h64").ap()
        self.epsc = nc.inline_tensor(np.full((1, 1), EPS, np.float32), name="c_eps").ap()
        self.h128 = nc.inline_tensor(_hadamard(128).astype(BF), name="c_h128").ap()
        self._sems = {}

    def sem(self, name):
        if name not in self._sems:
            self._sems[name] = Sem(self.nc, name)
        return self._sems[name]


def chunked(dram_ap, cols=None):
    ap = dram_ap if cols is None else dram_ap[:, cols[0]:cols[1]]
    return ap.rearrange("(c p) o -> p c o", p=128)



def emit_hi_lo(v, src_f32, hi_bf, lo_bf):
    v.tensor_copy(out=hi_bf, in_=src_f32)
    v.drain()
    return v.tensor_tensor(out=lo_bf, in0=src_f32, in1=hi_bf, op=ALU.subtract)


def ln_tail(sc, kb, *, nfeat, PMp, PSp, st_f, hl_b, onesr_sb, PBC, mbc_sb, rbc_sb,
            stat_tok, bc_war_tok, MM, EV, AC, eps_ap):
    inv = 1.0 / float(nfeat)

    def stats(v):
        v.tensor_scalar(out=st_f[:, 0], in0=PMp[:], scalar1=inv, scalar2=None, op0=ALU.mult)
        v.tensor_scalar(out=st_f[:, 1], in0=PSp[:], scalar1=inv, scalar2=None, op0=ALU.mult)
        v.drain()
        v.tensor_tensor(out=st_f[:, 2], in0=st_f[:, 0], in1=st_f[:, 0], op=ALU.mult)
        v.drain()
        return v.tensor_tensor(out=st_f[:, 3], in0=st_f[:, 1], in1=st_f[:, 2], op=ALU.subtract)

    t_stats = sc.op("vector", stats, wait=[stat_tok], inc=EV)
    t_sqrt = sc.op("scalar",
                   lambda s: s.activation(out=st_f[:, 1], in_=st_f[:, 3], func=AF.Sqrt, bias=eps_ap),
                   wait=[t_stats], inc=AC)
    t_rec = sc.op("vector", lambda v: v.reciprocal(out=st_f[:, 2], in_=st_f[:, 1]),
                  wait=[t_sqrt], inc=EV)

    def hilo(v):
        emit_hi_lo(v, st_f[:, 0], hl_b[:, 0], hl_b[:, 1])
        v.drain()
        return emit_hi_lo(v, st_f[:, 2], hl_b[:, 2], hl_b[:, 3])

    t_hl = sc.op("vector", hilo, wait=[t_rec], inc=EV)

    def bc1(t):
        t.matmul(PBC[:], onesr_sb[:], hl_b[:, 0], start=True, stop=False)
        return t.matmul(PBC[:], onesr_sb[:], hl_b[:, 1], start=False, stop=True)

    t_b1 = sc.op("tensor", bc1, wait=[t_hl, bc_war_tok], inc=MM)
    t_e1 = sc.op("vector", lambda v: v.tensor_copy(out=mbc_sb[:], in_=PBC[:]),
                 wait=[t_b1], inc=EV)

    def bc2(t):
        t.matmul(PBC[:], onesr_sb[:], hl_b[:, 2], start=True, stop=False)
        return t.matmul(PBC[:], onesr_sb[:], hl_b[:, 3], start=False, stop=True)

    t_b2 = sc.op("tensor", bc2, wait=[t_hl, t_e1], inc=MM)
    t_e2 = sc.op("vector", lambda v: v.tensor_copy(out=rbc_sb[:], in_=PBC[:]),
                 wait=[t_b2], inc=EV)
    return t_e2, t_stats



def p0(kb, stack):
    nc = kb.nc
    sc = Sched(kb)
    MM, EV, AC = (kb.sem(x) for x in ("p0mm", "p0ev", "p0ac"))
    LDW, LDX, LDT = (kb.sem(x) for x in ("p0ldw", "p0ldx", "p0ldt"))
    STL, STQ, STS = (kb.sem(x) for x in ("p0stl", "p0stq", "p0sts"))

    ctx = stack.enter_context
    qw_sb = ctx(nc.sbuf_tensor("p0_qw", [128, 16, NH * DQK], BF16))
    qwp_sb = ctx(nc.sbuf_tensor("p0_qwp", [128, 16, NH * DPE], BF16))
    kvd_sb = ctx(nc.sbuf_tensor("p0_kvd", [128, 16, LATD + DPE], BF16))
    lng_sb = ctx(nc.sbuf_tensor("p0_lng", [128, 4], F32))
    lnb_sb = ctx(nc.sbuf_tensor("p0_lnb", [128, 4], F32))
    ones_sb = ctx(nc.sbuf_tensor("p0_ones", [128, 1], BF16))
    eps_sb = ctx(nc.sbuf_tensor("p0_eps", [1, 1], F32))
    onesr_sb = ctx(nc.sbuf_tensor("p0_onesr", [1, 128], BF16))
    rtk_sb = ctx(nc.sbuf_tensor("p0_rtk", [32, 2, TT], BF16))
    rtq_sb = ctx(nc.sbuf_tensor("p0_rtq", [64, 2, TT], BF16))
    xt_sb = ctx(nc.sbuf_tensor("p0_xt", [128, 16, TT], BF16))
    lat_f = ctx(nc.sbuf_tensor("p0_latf", [128, 4, TT], F32))
    lat_b = ctx(nc.sbuf_tensor("p0_latb", [128, 4, TT], BF16))
    sq_b = ctx(nc.sbuf_tensor("p0_sqb", [128, 4, TT], BF16))
    st_f = ctx(nc.sbuf_tensor("p0_stf", [1, 4, TT], F32))
    hl_b = ctx(nc.sbuf_tensor("p0_hlb", [1, 4, TT], BF16))
    mbc_sb = ctx(nc.sbuf_tensor("p0_mbc", [128, TT], F32))
    rbc_sb = ctx(nc.sbuf_tensor("p0_rbc", [128, TT], F32))
    t1_f = ctx(nc.sbuf_tensor("p0_t1f", [128, TT], F32))
    latout = ctx(nc.sbuf_tensor("p0_latout", [128, 4, TT], BF16))
    peout = ctx(nc.sbuf_tensor("p0_peout", [32, 2, TT], BF16))
    pesc = ctx(nc.sbuf_tensor("p0_pesc", [64, 2, TT], F32))
    qn_b = ctx(nc.sbuf_tensor("p0_qnb", [128, 2, TT], BF16))
    qp_b = ctx(nc.sbuf_tensor("p0_qpb", [64, 2, TT], BF16))
    qsq_b = ctx(nc.sbuf_tensor("p0_qsqb", [128, 2, TT], BF16))
    qpsq_b = ctx(nc.sbuf_tensor("p0_qpsqb", [64, 2, TT], BF16))
    qs_f = ctx(nc.sbuf_tensor("p0_qsf", [1, 128], F32))
    PA = ctx(nc.psum_tensor("p0_PA", [128, TT], F32))
    PB = ctx(nc.psum_tensor("p0_PB", [128, TT], F32))
    PPA = ctx(nc.psum_tensor("p0_PPA", [64, TT], F32))
    PPB = ctx(nc.psum_tensor("p0_PPB", [64, TT], F32))
    PM = ctx(nc.psum_tensor("p0_PM", [1, TT], F32))
    PS = ctx(nc.psum_tensor("p0_PS", [1, TT], F32))
    PQ2 = ctx(nc.psum_tensor("p0_PQ2", [1, TT], F32))
    PBC = ctx(nc.psum_tensor("p0_PBC", [128, TT], F32))

    for dst, srcd in ((qw_sb, chunked(kb.d["qwT"])), (qwp_sb, chunked(kb.d["qwTpe"])),
                      (kvd_sb, chunked(kb.d["kvdT"])),
                      (lng_sb, kb.d["lng"]), (lnb_sb, kb.d["lnb"]),
                      (ones_sb, kb.ones_bf), (onesr_sb, kb.ones_row),
                      (eps_sb, kb.epsc)):
        sc.op("gpsimd", lambda g, dst=dst, srcd=srcd: g.dma_start(out=dst[:], in_=srcd),
              inc=LDW, n=16)
    t_w = Tok(LDW, LDW.v)

    prev = {
        "evA": None, "evB": None, "evPA": None, "evPB": None,
        "bc_done": None, "apply": None, "qnorm": None,
        "q_store": None, "lat_store": None, "pm_free": None,
        "tab_used": None, "tile_done": None,
    }

    for tt in range(NTT):
        cs = (tt * TT, (tt + 1) * TT)
        x_tok = sc.op("gpsimd",
                      lambda g, cs=cs: g.dma_start(out=xt_sb[:], in_=chunked(kb.d["xT"], cs)),
                      wait=[t_w, prev["tile_done"]], inc=LDX, n=16)
        t_tab = sc.op("gpsimd",
                      lambda g, cs=cs: [
                          g.dma_start(out=rtk_sb[:, 0], in_=kb.d["r0k"][:, cs[0]:cs[1]]),
                          g.dma_start(out=rtk_sb[:, 1], in_=kb.d["r0k"][:, S + cs[0]:S + cs[1]]),
                          g.dma_start(out=rtq_sb[:, 0], in_=kb.d["r0q"][:, cs[0]:cs[1]]),
                          g.dma_start(out=rtq_sb[:, 1], in_=kb.d["r0q"][:, S + cs[0]:S + cs[1]])],
                      wait=[prev["tab_used"], prev["tile_done"]], inc=LDT, n=16, ni=4)
        xb = xt_sb

        ev_lat = [None] * 4
        for oc in range(4):
            tgt = PA if oc % 2 == 0 else PB
            warp = prev["evA"] if oc % 2 == 0 else prev["evB"]

            def mms(t, tgt=tgt, oc=oc):
                for fc in range(16):
                    r = t.matmul(tgt[:], kvd_sb[:, fc, oc * 128:(oc + 1) * 128],
                                 xb[:, fc], start=(fc == 0), stop=(fc == 15))
                return r

            t_mm = sc.op("tensor", mms, wait=[x_tok, warp], inc=MM)

            def evac(v, oc=oc, tgt=tgt):
                v.tensor_copy(out=lat_f[:, oc], in_=tgt[:])
                return v.tensor_copy(out=lat_b[:, oc], in_=tgt[:])

            tok = sc.op("vector", evac, wait=[t_mm, prev["apply"]], inc=EV)
            ev_lat[oc] = tok
            if oc % 2 == 0:
                prev["evA"] = tok
            else:
                prev["evB"] = tok

        def pemm(t):
            for fc in range(16):
                t.matmul(PPA[0:32, :], kvd_sb[:, fc, 512:544], xb[:, fc],
                         start=(fc == 0), stop=(fc == 15))
            for fc in range(16):
                r = t.matmul(PPB[0:32, :], kvd_sb[:, fc, 544:576], xb[:, fc],
                             start=(fc == 0), stop=(fc == 15))
            return r

        t_pemm = sc.op("tensor", pemm, wait=[x_tok, prev["evPA"], prev["evPB"]], inc=MM)

        def ropepe(v):
            a = PPA[0:32, :]
            b = PPB[0:32, :]
            c = rtk_sb[:, 0]
            s_ = rtk_sb[:, 1]
            v.tensor_tensor(out=pesc[0:32, 0], in0=a, in1=c, op=ALU.mult)
            v.tensor_tensor(out=pesc[0:32, 1], in0=b, in1=s_, op=ALU.mult)
            v.drain()
            v.tensor_tensor(out=peout[:, 0], in0=pesc[0:32, 0], in1=pesc[0:32, 1],
                            op=ALU.subtract)
            v.drain()
            v.tensor_tensor(out=pesc[0:32, 0], in0=a, in1=s_, op=ALU.mult)
            v.tensor_tensor(out=pesc[0:32, 1], in0=b, in1=c, op=ALU.mult)
            v.drain()
            return v.tensor_tensor(out=peout[:, 1], in0=pesc[0:32, 1], in1=pesc[0:32, 0],
                                   op=ALU.add)

        t_rope = sc.op("vector", ropepe, wait=[t_pemm, t_tab, prev["lat_store"]], inc=EV)
        prev["evPA"] = t_rope
        prev["evPB"] = t_rope

        def sqs(s):
            for c in range(4):
                r = s.activation(out=sq_b[:, c], in_=lat_b[:, c], func=AF.Square)
            return r

        t_sq = sc.op("scalar", sqs, wait=ev_lat, inc=AC)

        def statmm(t):
            for c in range(4):
                t.matmul(PM[:], ones_sb[:], lat_b[:, c], start=(c == 0), stop=(c == 3))
            for c in range(4):
                r = t.matmul(PS[:], ones_sb[:], sq_b[:, c], start=(c == 0), stop=(c == 3))
            return r

        t_stat = sc.op("tensor", statmm, wait=ev_lat + [t_sq, prev["pm_free"]], inc=MM)

        t_bc, t_statsdone = ln_tail(
            sc, kb, nfeat=LATD, PMp=PM, PSp=PS, st_f=st_f, hl_b=hl_b,
            onesr_sb=onesr_sb, PBC=PBC, mbc_sb=mbc_sb, rbc_sb=rbc_sb,
            stat_tok=t_stat, bc_war_tok=prev["apply"], MM=MM, EV=EV, AC=AC,
            eps_ap=eps_sb[:, :])
        prev["pm_free"] = t_statsdone
        prev["bc_done"] = t_bc

        def apply(v):
            for oc in range(4):
                v.tensor_tensor(out=t1_f[:], in0=lat_f[:, oc], in1=mbc_sb[:], op=ALU.subtract)
                v.drain()
                v.tensor_tensor(out=t1_f[:], in0=t1_f[:], in1=rbc_sb[:], op=ALU.mult)
                v.drain()
                r = v.tensor_scalar(out=latout[:, oc], in0=t1_f[:],
                                    scalar1=lng_sb[:, oc:oc + 1],
                                    scalar2=lnb_sb[:, oc:oc + 1],
                                    op0=ALU.mult, op1=ALU.add)
            return r

        t_apply = sc.op("vector", apply, wait=[t_bc, prev["lat_store"]] + ev_lat, inc=EV)
        prev["apply"] = t_apply

        def latst(e, cs=cs):
            return [
                e.dma_start(out=kb.d["LAT"][0:LATD, cs[0]:cs[1]].rearrange("(c p) t -> p c t", p=128),
                            in_=latout[:]),
                e.dma_start(out=kb.d["LAT"][LATD:LATD + 32, cs[0]:cs[1]], in_=peout[:, 0]),
                e.dma_start(out=kb.d["LAT"][LATD + 32:LATD + 64, cs[0]:cs[1]], in_=peout[:, 1])]

        prev["lat_store"] = sc.op("sync", latst, wait=[t_apply, t_rope], inc=STL, n=16, ni=3)

        for p in range(NH // 2):
            h0, h1 = 2 * p, 2 * p + 1

            def qmm(t, p=p, h0=h0, h1=h1):
                for fc in range(16):
                    t.matmul(PA[:], qw_sb[:, fc, h0 * DQK:(h0 + 1) * DQK], xb[:, fc],
                             start=(fc == 0), stop=(fc == 15))
                for fc in range(16):
                    t.matmul(PB[:], qw_sb[:, fc, h1 * DQK:(h1 + 1) * DQK], xb[:, fc],
                             start=(fc == 0), stop=(fc == 15))
                for fc in range(16):
                    t.matmul(PPA[:], qwp_sb[:, fc, p * 128:p * 128 + 64], xb[:, fc],
                             start=(fc == 0), stop=(fc == 15))
                for fc in range(16):
                    r = t.matmul(PPB[:], qwp_sb[:, fc, p * 128 + 64:p * 128 + 128], xb[:, fc],
                                 start=(fc == 0), stop=(fc == 15))
                return r

            t_qmm = sc.op("tensor", qmm,
                          wait=[x_tok, prev["evA"], prev["evB"], prev["evPA"], prev["evPB"],
                                prev["apply"] if p == 0 else None],
                          inc=MM)

            def qevac(s):
                s.activation(out=qn_b[:, 0], in_=PA[:], func=AF.Copy, scale=SC0)
                s.activation(out=qsq_b[:, 0], in_=PA[:], func=AF.Square, scale=SC0)
                s.activation(out=qn_b[:, 1], in_=PB[:], func=AF.Copy, scale=SC0)
                return s.activation(out=qsq_b[:, 1], in_=PB[:], func=AF.Square, scale=SC0)

            t_qe = sc.op("scalar", qevac, wait=[t_qmm, prev["q_store"]], inc=AC)
            prev["evA"] = t_qe
            prev["evB"] = t_qe

            def qrope(v):
                c = rtq_sb[:, 0]
                s_ = rtq_sb[:, 1]
                v.tensor_tensor(out=pesc[:, 0], in0=PPA[:], in1=c, op=ALU.mult)
                v.tensor_tensor(out=pesc[:, 1], in0=PPB[:], in1=s_, op=ALU.mult)
                v.drain()
                v.tensor_tensor(out=qp_b[:, 0], in0=pesc[:, 0], in1=pesc[:, 1],
                                op=ALU.subtract)
                v.drain()
                v.tensor_tensor(out=pesc[:, 0], in0=PPA[:], in1=s_, op=ALU.mult)
                v.tensor_tensor(out=pesc[:, 1], in0=PPB[:], in1=c, op=ALU.mult)
                v.drain()
                v.tensor_tensor(out=qp_b[:, 1], in0=pesc[:, 1], in1=pesc[:, 0], op=ALU.add)
                v.drain()
                v.tensor_tensor(out=qpsq_b[:, 0], in0=qp_b[:, 0], in1=qp_b[:, 0], op=ALU.mult)
                return v.tensor_tensor(out=qpsq_b[:, 1], in0=qp_b[:, 1], in1=qp_b[:, 1],
                                       op=ALU.mult)

            t_qr = sc.op("vector", qrope, wait=[t_qmm, t_tab, prev["q_store"], t_rope], inc=EV)
            prev["evPA"] = t_qr
            prev["evPB"] = t_qr

            for hh, base in ((h0, 0), (h1, 32)):
                def qnmm(t, base=base, hh=hh):
                    t.matmul(PQ2[:], ones_sb[:], qsq_b[:, 0 if base == 0 else 1],
                             start=True, stop=False)
                    t.matmul(PQ2[:], ones_sb[base:base + 32, :], qpsq_b[base:base + 32, 0],
                             start=False, stop=False)
                    return t.matmul(PQ2[:], ones_sb[base:base + 32, :], qpsq_b[base:base + 32, 1],
                                    start=False, stop=True)

                t_qn = sc.op("tensor", qnmm, wait=[t_qe, t_qr, prev["qnorm"]], inc=MM)
                slot = hh * 8 + tt
                t_qx = sc.op("vector",
                             lambda v, slot=slot: v.tensor_reduce(
                                 out=qs_f[:, slot:slot + 1], in_=PQ2[:], axis=AX.X, op=ALU.max),
                             wait=[t_qn], inc=EV)
                prev["qnorm"] = t_qx

            def qst(e, h0=h0, h1=h1, cs=cs):
                return [
                    e.dma_start(out=kb.d["Q0"][h0 * QHD:h0 * QHD + 128, cs[0]:cs[1]], in_=qn_b[:, 0]),
                    e.dma_start(out=kb.d["Q0"][h1 * QHD:h1 * QHD + 128, cs[0]:cs[1]], in_=qn_b[:, 1]),
                    e.dma_start(out=kb.d["Q0"][h0 * QHD + 128:h0 * QHD + 160, cs[0]:cs[1]],
                                in_=qp_b[0:32, 0]),
                    e.dma_start(out=kb.d["Q0"][h0 * QHD + 160:h0 * QHD + 192, cs[0]:cs[1]],
                                in_=qp_b[0:32, 1]),
                    e.dma_start(out=kb.d["Q0"][h1 * QHD + 128:h1 * QHD + 160, cs[0]:cs[1]],
                                in_=qp_b[32:64, 0]),
                    e.dma_start(out=kb.d["Q0"][h1 * QHD + 160:h1 * QHD + 192, cs[0]:cs[1]],
                                in_=qp_b[32:64, 1])]

            t_qs = sc.op("sync", qst, wait=[t_qe, t_qr], inc=STQ, n=16, ni=6)
            prev["q_store"] = t_qs
        prev["tile_done"] = prev["qnorm"]
        prev["tab_used"] = prev["evPA"]

    sc.op("sync", lambda e: e.dma_start(out=kb.d["QS0"], in_=qs_f[:]),
          wait=[prev["qnorm"]], inc=STS, n=16)
    sc.op("sync", None, wait=[Tok(STL, STL.v), Tok(STQ, STQ.v), Tok(STS, STS.v)])

    with nc.Block() as blk:
        sc.emit(blk)



P2_NG = 4
P2_NQ = NTT
P2_SECT = 3
P2_SUB = 2


def p2(kb, stack, sc=None, precast=False):
    nc = kb.nc
    own = sc is None
    if own:
        sc = Sched(kb)
    t_cw = None
    pc_jobs = []
    if precast:
        CW = kb.sem("p9cw")
        for w in ("W1T", "W3T"):
            for ch in range(16):
                pc_jobs.append((kb.d[w.replace("T", "B")][ch * 128:(ch + 1) * 128, :],
                                kb.d[w][ch * 128:(ch + 1) * 128, :]))
        for m, M in (("w1T", "W1E"), ("w3T", "W3E"), ("w2T", "W2E")):
            for e in range(NE):
                sflat = kb.d[m][e:e + 1].rearrange("a p o -> (a p) o")
                dflat = kb.d[M][e:e + 1].rearrange("a p o -> (a p) o")
                nr = sflat.shape[0]
                for ch in range(8):
                    r0, r1 = ch * nr // 8, (ch + 1) * nr // 8
                    pc_jobs.append((dflat[r0:r1, :], sflat[r0:r1, :]))

        def pc_emit(njobs):
            for _ in range(njobs):
                if not pc_jobs:
                    return
                dst, srcd = pc_jobs.pop(0)
                sc.op("gpsimd",
                      lambda g, dst=dst, srcd=srcd: g.dma_start(out=dst, in_=srcd),
                      inc=CW, n=16)
    MM, EV, AC = (kb.sem(x) for x in ("p2mm", "p2ev", "p2ac"))
    LDC, LDK = (kb.sem(x) for x in ("p2ldc", "p2ldk"))
    LQ = [kb.sem("p2lqa"), kb.sem("p2lqb")]
    STC = kb.sem("p2stc")
    ctx = stack.enter_context
    lat_sb = ctx(nc.sbuf_tensor("p2_lat", [128, 4, S], BF16))
    pe_sb = ctx(nc.sbuf_tensor("p2_pe", [64, S], BF16))
    qs_sb = ctx(nc.sbuf_tensor("p2_qs", [1, 128], F32))
    pex_f = ctx(nc.sbuf_tensor("p2_pex", [1, 9], F32))
    pesq_b = ctx(nc.sbuf_tensor("p2_pesq", [64, TT], BF16))
    kvuK_sb = ctx(nc.sbuf_tensor("p2_kvuK", [128, 4, 4, 128], BF16))
    kvuV_sb = ctx(nc.sbuf_tensor("p2_kvuV", [128, 4, 4, 128], BF16))
    K_sb = ctx(nc.sbuf_tensor("p2_K", [128, 4, S], BF16))
    V_sb = ctx(nc.sbuf_tensor("p2_V", [128, 32, 512], BF16))
    ksq_b = ctx(nc.sbuf_tensor("p2_ksq", [128, TT], BF16))
    k2_f = ctx(nc.sbuf_tensor("p2_k2", [1, 4, 8], F32))
    k2t_f = ctx(nc.sbuf_tensor("p2_k2t", [1, TT], F32))
    bs_f = ctx(nc.sbuf_tensor("p2_bs", [1, 4, 4], F32))
    bs_b = ctx(nc.sbuf_tensor("p2_bsb", [1, 4], BF16))
    bias_sb = ctx(nc.sbuf_tensor("p2_bias", [128, 4], F32))
    qn_sb = ctx(nc.sbuf_tensor("p2_qn", [128, 2, TT], BF16))
    qp_sb = ctx(nc.sbuf_tensor("p2_qp", [64, 2, TT], BF16))
    P_sb = ctx(nc.sbuf_tensor("p2_P", [128, 2, 8, TT], BF16))
    su_f = ctx(nc.sbuf_tensor("p2_su", [1, 2, TT], F32))
    hl_b = ctx(nc.sbuf_tensor("p2_hl", [1, 2, TT], BF16))
    bc_f = ctx(nc.sbuf_tensor("p2_bc", [128, TT], F32))
    ct_b = ctx(nc.sbuf_tensor("p2_ct", [128, TT], BF16))
    ones_sb = ctx(nc.sbuf_tensor("p2_ones", [128, 1], BF16))
    onesr_sb = ctx(nc.sbuf_tensor("p2_onesr", [1, 128], BF16))
    PSC = ctx(nc.psum_tensor("p2_PSC", [128, 4, TT], F32))
    PZ = ctx(nc.psum_tensor("p2_PZ", [128, TT], F32))
    PW = ctx(nc.psum_tensor("p2_PW", [1, TT], F32))
    PBC = ctx(nc.psum_tensor("p2_PBC", [128, TT], F32))
    PBI = ctx(nc.psum_tensor("p2_PBI", [128, 4], F32))

    sc.op("gpsimd", lambda g: g.dma_start(out=ones_sb[:], in_=kb.ones_bf), inc=LDC, n=16)
    sc.op("gpsimd", lambda g: g.dma_start(out=onesr_sb[:], in_=kb.ones_row), inc=LDC, n=16)
    sc.op("gpsimd", lambda g: g.dma_start(out=qs_sb[:], in_=kb.d["QS0"]), inc=LDC, n=16)
    sc.op("gpsimd",
          lambda g: g.dma_start(out=lat_sb[:],
                                in_=kb.d["LAT"][0:LATD, :].rearrange("(c p) t -> p c t", p=128)),
          inc=LDC, n=16)
    sc.op("gpsimd", lambda g: g.dma_start(out=pe_sb[:], in_=kb.d["LAT"][LATD:LATD + 64, :]),
          inc=LDC, n=16)
    t_res = Tok(LDC, LDC.v)

    t_prev = None
    for seg in range(NTT):
        ss = (seg * TT, (seg + 1) * TT)
        t_a = sc.op("scalar",
                    lambda s, ss=ss: s.activation(out=pesq_b[:], in_=pe_sb[:, ss[0]:ss[1]],
                                                  func=AF.Square),
                    wait=[t_res, t_prev], inc=AC)
        t_m = sc.op("tensor",
                    lambda t: t.matmul(PW[:], ones_sb[0:64, :], pesq_b[:], start=True, stop=True),
                    wait=[t_a, t_prev], inc=MM)
        t_prev = sc.op("vector",
                       lambda v, seg=seg: v.tensor_reduce(out=pex_f[:, seg:seg + 1], in_=PW[:],
                                                          axis=AX.X, op=ALU.max),
                       wait=[t_m], inc=EV)
    t_pe2 = sc.op("vector",
                  lambda v: v.tensor_reduce(out=pex_f[:, 8:9], in_=pex_f[:, 0:8],
                                            axis=AX.X, op=ALU.max),
                  wait=[t_prev], inc=EV)

    prev = {
        "grp_done": None,
        "kv_loads": None,
        "expA": [None], "expB": None,
        "ctx_done": None,
        "qload": [None, None],
        "qtok": [None, None],
        "cmm2": None, "cmm3": None,
        "pending_epi": None,
        "st": None,
    }

    for g in range(P2_NG):
        heads = [4 * g + i for i in range(4)]
        def kvld(gp, heads=heads):
            r = []
            for i, h in enumerate(heads):
                r.append(gp.dma_start(
                    out=kvuK_sb[:, :, i, :],
                    in_=kb.d["kvuT"][:, h * 256:h * 256 + 128].rearrange("(c p) o -> p c o", p=128)))
                r.append(gp.dma_start(
                    out=kvuV_sb[:, :, i, :],
                    in_=kb.d["kvuT"][:, h * 256 + 128:h * 256 + 256].rearrange("(c p) o -> p c o", p=128)))
            return r

        t_kv = sc.op("gpsimd", kvld, wait=[prev["grp_done"]], inc=LDK, n=16, ni=8)

        if P2_SECT < 1:
            continue
        kwar = {0: prev["grp_done"], 1: prev["grp_done"]}
        kprev_k2 = None
        kidx = 0
        for i in range(4 if P2_SUB >= 1 else 0):
            for seg in range(NTT):
                ss = (seg * TT, (seg + 1) * TT)
                par2 = kidx % 2
                tgt = PSC[:, par2]

                def kmm(t, i=i, ss=ss, tgt=tgt):
                    for lc in range(4):
                        r = t.matmul(tgt, kvuK_sb[:, lc, i, :], lat_sb[:, lc, ss[0]:ss[1]],
                                     start=(lc == 0), stop=(lc == 3))
                    return r

                t_kmm = sc.op("tensor", kmm,
                              wait=[t_kv, t_res, kwar[par2]],
                              inc=MM)

                def kev_f(v, i=i, ss=ss, tgt=tgt):
                    return v.tensor_copy(out=K_sb[:, i, ss[0]:ss[1]], in_=tgt)

                t_kev = sc.op("vector", kev_f, wait=[t_kmm], inc=EV)

                if P2_SUB < 2:
                    kwar[par2] = t_kev
                    kprev_k2 = t_kev
                    kidx += 1
                    continue
                t_ksq = sc.op("scalar",
                              lambda s, tgt=tgt: s.activation(out=ksq_b[:], in_=tgt,
                                                              func=AF.Square),
                              wait=[t_kmm, t_kev, kprev_k2], inc=AC)
                t_k2m = sc.op("tensor",
                              lambda t: t.matmul(PW[:], ones_sb[:], ksq_b[:],
                                                 start=True, stop=True),
                              wait=[t_ksq, t_pe2, kprev_k2, prev["ctx_done"]], inc=MM)

                def k2fin(v, i=i, seg=seg):
                    return v.tensor_reduce(out=k2_f[:, i, seg:seg + 1], in_=PW[:],
                                           axis=AX.X, op=ALU.max)

                t_k2 = sc.op("vector", k2fin, wait=[t_k2m], inc=EV)
                kwar[par2] = t_k2
                kprev_k2 = t_k2
                kidx += 1
        t_kdone = kprev_k2

        if P2_SECT < 2:
            prev["grp_done"] = t_kdone
            continue
        vev = []
        for kt in range(32):
            def vmm(t, kt=kt):
                for lc in range(4):
                    r = t.matmul(PZ[:], lat_sb[:, lc, kt * 128:(kt + 1) * 128],
                                 kvuV_sb[:, lc].rearrange("p a b -> p (a b)"),
                                 start=(lc == 0), stop=(lc == 3))
                return r

            t_vmm = sc.op("tensor", vmm,
                          wait=[t_kv, t_res, prev["ctx_done"],
                                vev[-1] if vev else prev["grp_done"]],
                          inc=MM)
            t_vev = sc.op("vector",
                          lambda v, kt=kt: v.tensor_copy(out=V_sb[:, kt, :], in_=PZ[:]),
                          wait=[t_vmm], inc=EV)
            vev.append(t_vev)
        t_vdone = vev[-1]

        def bmax(v, heads=heads):
            for i, h in enumerate(heads):
                v.tensor_reduce(out=bs_f[:, 0, i:i + 1], in_=qs_sb[:, h * 8:h * 8 + 8],
                                axis=AX.X, op=ALU.max)
                v.tensor_reduce(out=bs_f[:, 1, i:i + 1], in_=k2_f[:, i, :],
                                axis=AX.X, op=ALU.max)
            v.drain()
            v.tensor_scalar(out=bs_f[:, 1], in0=bs_f[:, 1], scalar1=pex_f[:, 8:9],
                            scalar2=None, op0=ALU.add)
            v.drain()
            return v.tensor_tensor(out=bs_f[:, 2], in0=bs_f[:, 0], in1=bs_f[:, 1],
                                   op=ALU.mult)

        t_bm = sc.op("vector", bmax, wait=[t_kdone, t_res, t_pe2], inc=EV)
        t_bsq = sc.op("scalar",
                      lambda s: s.activation(out=bs_f[:, 3], in_=bs_f[:, 2], func=AF.Sqrt),
                      wait=[t_bm], inc=AC)
        t_bneg = sc.op("vector",
                       lambda v: v.tensor_scalar(out=bs_b[:], in0=bs_f[:, 3],
                                                 scalar1=-1.0, scalar2=None, op0=ALU.mult),
                       wait=[t_bsq], inc=EV)

        def bimm(t):
            for i in range(4):
                r = t.matmul(PBI[:, i:i + 1], onesr_sb[:], bs_b[:, i:i + 1],
                             start=True, stop=True)
            return r

        t_bimm = sc.op("tensor", bimm, wait=[t_bneg, prev["grp_done"]], inc=MM)
        t_bias = sc.op("vector", lambda v: v.tensor_copy(out=bias_sb[:], in_=PBI[:]),
                       wait=[t_bimm], inc=EV)

        if P2_SECT < 3:
            prev["grp_done"] = t_bias
            continue
        units = [(i, qt) for i in range(4) for qt in range(P2_NQ)]

        def emit_qld(u, heads=heads):
            i_, qt_ = units[u]
            h_ = heads[i_]
            par_ = u % 2
            qs2 = (qt_ * TT, (qt_ + 1) * TT)

            def qld(gp, h_=h_, qs2=qs2, par_=par_):
                return [
                    gp.dma_start(out=qn_sb[:, par_],
                                 in_=kb.d["Q0"][h_ * QHD:h_ * QHD + 128, qs2[0]:qs2[1]]),
                    gp.dma_start(out=qp_sb[:, par_],
                                 in_=kb.d["Q0"][h_ * QHD + 128:h_ * QHD + QHD,
                                                qs2[0]:qs2[1]])]

            prev["qtok"][par_] = sc.op("gpsimd", qld, wait=[prev["qload"][par_]],
                                       inc=LQ[par_], n=16, ni=2)

        for u, (i, qt) in enumerate(units):
            h = heads[i]
            qs_ = (qt * TT, (qt + 1) * TT)
            par = u % 2
            if True:
                if u == 0:
                    emit_qld(0)
                    emit_qld(1)
                elif u + 1 < len(units):
                    emit_qld(u + 1)
                t_ql = prev["qtok"][par]

                t_exp4 = [None] * 4
                t_cmm4 = [None] * 4

                def emit_cmm(qq):
                    def cmm8(t, i=i, qq=qq):
                        for jj in range(8):
                            kt = qq * 8 + jj
                            t.matmul(PZ[:], V_sb[:, kt, i * 128:(i + 1) * 128],
                                     P_sb[:, qq % 2, jj],
                                     start=(kt == 0), stop=(kt == 31))
                            r = t.matmul(PW[:], ones_sb[:], P_sb[:, qq % 2, jj],
                                         start=(kt == 0), stop=(kt == 31))
                        return r

                    t_cmm4[qq] = sc.op("tensor", cmm8,
                                       wait=[t_exp4[qq],
                                             prev["ctx_done"] if qq == 0 else None],
                                       inc=MM)

                pairtok = prev.setdefault("evroll", [None, None])
                for q in range(4):
                    qp_ = q % 2
                    t_e = None
                    for rr in range(4):
                        pr = rr % 2

                        def smm(t, i=i, q=q, rr=rr, pr=pr, par=par):
                            for u in range(2):
                                kt = q * 8 + 2 * rr + u
                                t.matmul(PSC[:, 2 * pr + u],
                                         K_sb[:, i, kt * 128:(kt + 1) * 128],
                                         qn_sb[:, par], start=True, stop=False)
                                r = t.matmul(PSC[:, 2 * pr + u],
                                             pe_sb[:, kt * 128:(kt + 1) * 128],
                                             qp_sb[:, par], start=False, stop=True)
                            return r

                        t_s = sc.op("tensor", smm,
                                    wait=[t_ql, t_kdone, t_vdone, pairtok[pr]],
                                    inc=MM)

                        def expf(s, i=i, qp_=qp_, rr=rr, pr=pr):
                            return s.activation(
                                out=P_sb[:, qp_, 2 * rr:2 * rr + 2].rearrange(
                                    "p a t -> p (a t)"),
                                in_=PSC[:, 2 * pr:2 * pr + 2].rearrange("p a t -> p (a t)"),
                                func=AF.Exp, bias=bias_sb[:, i:i + 1])

                        t_e = sc.op("scalar", expf,
                                    wait=[t_s, t_bias,
                                          (t_cmm4[q - 2] if q >= 2 else
                                           prev["cmm2" if q == 0 else "cmm3"])
                                          if rr == 0 else None],
                                    inc=AC)
                        pairtok[pr] = t_e
                        if q == 0 and rr == 1 and prev.get("pending_epi"):
                            prev["pending_epi"]()
                            prev["pending_epi"] = None
                    t_exp4[q] = t_e
                    if q >= 1:
                        emit_cmm(q - 1)
                emit_cmm(3)
                t_last_cmm = t_cmm4[3]
                t_ctx = [t_last_cmm] * 32
                prev["qload"][par] = t_s
                prev["cmm2"], prev["cmm3"] = t_cmm4[2], t_cmm4[3]

                def epi1(v):
                    v.tensor_copy(out=su_f[:, 0], in_=PW[:])
                    v.drain()
                    v.reciprocal(out=su_f[:, 1], in_=su_f[:, 0])
                    v.drain()
                    return emit_hi_lo(v, su_f[:, 1], hl_b[:, 0], hl_b[:, 1])

                t_ep1 = sc.op("vector", epi1, wait=[t_ctx[31]], inc=EV)

                def flush_epi(t_ep1=t_ep1, h=h, qs_=qs_):
                    def epbc(t):
                        t.matmul(PBC[:], onesr_sb[:], hl_b[:, 0], start=True, stop=False)
                        return t.matmul(PBC[:], onesr_sb[:], hl_b[:, 1],
                                        start=False, stop=True)

                    t_eb = sc.op("tensor", epbc, wait=[t_ep1], inc=MM)

                    def epi2(v):
                        v.tensor_copy(out=bc_f[:], in_=PBC[:])
                        v.drain()
                        return v.tensor_tensor(out=ct_b[:], in0=PZ[:], in1=bc_f[:],
                                               op=ALU.mult)

                    t_ep2 = sc.op("vector", epi2, wait=[t_eb, prev["st"]], inc=EV)
                    prev["ctx_done"] = t_ep2
                    t_st = sc.op("sync",
                                 lambda e, h=h, qs_=qs_: e.dma_start(
                                     out=kb.d["CT0"][h * 128:(h + 1) * 128,
                                                     qs_[0]:qs_[1]],
                                     in_=ct_b[:]),
                                 wait=[t_ep2, prev["st"]], inc=STC, n=16)
                    prev["st"] = t_st

                prev["pending_epi"] = flush_epi
        if prev.get("pending_epi"):
            prev["pending_epi"]()
            prev["pending_epi"] = None
        prev["grp_done"] = prev["ctx_done"]
        if precast:
            pc_emit(56)

    if precast:
        pc_emit(99)
        t_cw = Tok(CW, CW.v)
    sc.op("sync", None, wait=[Tok(STC, STC.v), t_cw])
    if own:
        with nc.Block() as blk:
            sc.emit(blk)
    return sc



def p3(kb, stack):
    nc = kb.nc
    sc = Sched(kb)
    MM, EV, AC = (kb.sem(x) for x in ("p3mm", "p3ev", "p3ac"))
    LDW, LDC, LDX = (kb.sem(x) for x in ("p3ldw", "p3ldc", "p3ldx"))
    STX, STW = (kb.sem(x) for x in ("p3stx", "p3stw"))
    ctx = stack.enter_context
    ow_sb = ctx(nc.sbuf_tensor("p3_ow", [128, 16, D], BF16))
    gt_sb = ctx(nc.sbuf_tensor("p3_gt", [128, 16, 8], BF16))
    ebc_sb = ctx(nc.sbuf_tensor("p3_ebc", [128, 8], F32))
    ident_sb = ctx(nc.sbuf_tensor("p3_id", [128, 128], BF16))
    ones_sb = ctx(nc.sbuf_tensor("p3_ones", [128, 1], BF16))
    onesr_sb = ctx(nc.sbuf_tensor("p3_onesr", [1, 128], BF16))
    eps_sb = ctx(nc.sbuf_tensor("p3_eps", [1, 1], F32))
    ct_sb = ctx(nc.sbuf_tensor("p3_ct", [128, 16, TT], BF16))
    xf_sb = ctx(nc.sbuf_tensor("p3_xf", [128, 16, TT], F32))
    r_f = ctx(nc.sbuf_tensor("p3_rf", [128, 16, TT], F32))
    rb_b = ctx(nc.sbuf_tensor("p3_rb", [128, 2, TT], BF16))
    sq_b = ctx(nc.sbuf_tensor("p3_sq", [128, 2, TT], BF16))
    st_f = ctx(nc.sbuf_tensor("p3_st", [1, 4, TT], F32))
    hl_b = ctx(nc.sbuf_tensor("p3_hl", [1, 4, TT], BF16))
    mbc_sb = ctx(nc.sbuf_tensor("p3_mbc", [128, TT], F32))
    rbc_sb = ctx(nc.sbuf_tensor("p3_rbc", [128, TT], F32))
    t1_f = ctx(nc.sbuf_tensor("p3_t1", [128, TT], F32))
    x1b_sb = ctx(nc.sbuf_tensor("p3_x1b", [128, 16, TT], BF16))
    gs_f = ctx(nc.sbuf_tensor("p3_gs", [128, 6, 8], F32))
    wgt_b = ctx(nc.sbuf_tensor("p3_wgt", [128, 8], BF16))
    wgtT_b = ctx(nc.sbuf_tensor("p3_wgtT", [8, 128], BF16))
    PA = ctx(nc.psum_tensor("p3_PA", [128, TT], F32))
    PB = ctx(nc.psum_tensor("p3_PB", [128, TT], F32))
    PM = ctx(nc.psum_tensor("p3_PM", [1, TT], F32))
    PS = ctx(nc.psum_tensor("p3_PS", [1, TT], F32))
    PBC = ctx(nc.psum_tensor("p3_PBC", [128, TT], F32))
    PG = ctx(nc.psum_tensor("p3_PG", [128, 8], F32))
    PT = ctx(nc.psum_tensor("p3_PT", [8, 128], BF16))

    for dst, srcd in ((ow_sb, chunked(kb.d["owT"])), (gt_sb, chunked(kb.d["gT"])),
                      (ebc_sb, kb.d["ebc"]), (ident_sb, kb.ident),
                      (ones_sb, kb.ones_bf), (onesr_sb, kb.ones_row),
                      (eps_sb, kb.epsc)):
        sc.op("gpsimd", lambda g, dst=dst, srcd=srcd: g.dma_start(out=dst[:], in_=srcd),
              inc=LDW, n=16)
    t_w = Tok(LDW, LDW.v)

    prev = {
        "mmA": None, "mmB": None, "apply": None, "pm_free": None,
        "x1_store": None, "wg_store": None, "stat": [None, None],
        "gmm": None, "tr": None, "top": None, "ct_used": None, "xf_used": None,
    }

    for tt in range(NTT):
        cs = (tt * TT, (tt + 1) * TT)
        t_ct = sc.op("gpsimd",
                     lambda g, cs=cs: g.dma_start(out=ct_sb[:], in_=chunked(kb.d["CT0"], cs)),
                     wait=[prev["ct_used"]], inc=LDC, n=16)
        t_xf = sc.op("gpsimd",
                     lambda g, cs=cs: g.dma_start(out=xf_sb[:], in_=chunked(kb.d["xT"], cs)),
                     wait=[prev["xf_used"]], inc=LDX, n=16)

        t_radd = [None] * 16
        t_stat = None
        for oc in range(16):
            tgt = PA if oc % 2 == 0 else PB
            warp = prev["mmA"] if oc % 2 == 0 else prev["mmB"]

            def omm(t, oc=oc, tgt=tgt):
                for cc in range(16):
                    r = t.matmul(tgt[:], ow_sb[:, cc, oc * 128:(oc + 1) * 128],
                                 ct_sb[:, cc], start=(cc == 0), stop=(cc == 15))
                return r

            t_mm = sc.op("tensor", omm, wait=[t_ct, t_w, warp], inc=MM)

            t_radd[oc] = sc.op("vector",
                               lambda v, oc=oc, tgt=tgt: v.tensor_tensor(
                                   out=r_f[:, oc], in0=tgt[:], in1=xf_sb[:, oc], op=ALU.add),
                               wait=[t_mm, t_xf, prev["apply"]], inc=EV)
            if oc % 2 == 0:
                prev["mmA"] = t_radd[oc]
            else:
                prev["mmB"] = t_radd[oc]

            par = oc % 2
            t_rb = sc.op("vector",
                         lambda v, oc=oc, par=par: v.tensor_copy(out=rb_b[:, par], in_=r_f[:, oc]),
                         wait=[t_radd[oc], prev["stat"][par]], inc=EV)
            t_sq = sc.op("scalar",
                         lambda s, oc=oc, par=par: s.activation(out=sq_b[:, par], in_=r_f[:, oc],
                                                                func=AF.Square),
                         wait=[t_radd[oc], prev["stat"][par]], inc=AC)

            def smm(t, oc=oc, par=par):
                t.matmul(PM[:], ones_sb[:], rb_b[:, par], start=(oc == 0), stop=(oc == 15))
                return t.matmul(PS[:], ones_sb[:], sq_b[:, par], start=(oc == 0), stop=(oc == 15))

            t_stat = sc.op("tensor", smm, wait=[t_rb, t_sq, prev["pm_free"] if oc == 0 else None],
                           inc=MM)
            prev["stat"][par] = t_stat

        t_bc, t_statsdone = ln_tail(
            sc, kb, nfeat=D, PMp=PM, PSp=PS, st_f=st_f, hl_b=hl_b,
            onesr_sb=onesr_sb, PBC=PBC, mbc_sb=mbc_sb, rbc_sb=rbc_sb,
            stat_tok=t_stat, bc_war_tok=prev["apply"], MM=MM, EV=EV, AC=AC,
            eps_ap=eps_sb[:, :])
        prev["pm_free"] = t_statsdone

        def apply(v):
            for oc in range(16):
                v.tensor_tensor(out=t1_f[:], in0=r_f[:, oc], in1=mbc_sb[:], op=ALU.subtract)
                v.drain()
                r = v.tensor_tensor(out=x1b_sb[:, oc], in0=t1_f[:], in1=rbc_sb[:], op=ALU.mult)
                v.drain()
            return r

        t_apply = sc.op("vector", apply, wait=[t_bc, prev["x1_store"]] + t_radd, inc=EV)
        prev["apply"] = t_apply
        prev["xf_used"] = t_apply

        t_x1s = sc.op("sync",
                      lambda e, cs=cs: e.dma_start(
                          out=chunked(kb.d["X1"], cs), in_=x1b_sb[:]),
                      wait=[t_apply], inc=STX, n=16)
        prev["x1_store"] = t_x1s

        for ts_ in range(4):
            col = (ts_ * 128, (ts_ + 1) * 128)

            def gmm(t, col=col):
                for cc in range(16):
                    r = t.matmul(PG[:], x1b_sb[:, cc, col[0]:col[1]], gt_sb[:, cc],
                                 start=(cc == 0), stop=(cc == 15))
                return r

            t_gmm = sc.op("tensor", gmm, wait=[t_apply, prev["gmm"]], inc=MM)
            t_sig = sc.op("scalar",
                          lambda s: s.activation(out=gs_f[:, 0], in_=PG[:], func=AF.Sigmoid),
                          wait=[t_gmm], inc=AC)
            prev["gmm"] = t_sig

            def top2(v):
                v.tensor_tensor(out=gs_f[:, 1], in0=gs_f[:, 0], in1=ebc_sb[:], op=ALU.add)
                v.drain()
                v.tensor_reduce(out=t1_f[:, 0:1], in_=gs_f[:, 1], axis=AX.X, op=ALU.max)
                v.drain()
                v.tensor_scalar(out=gs_f[:, 2], in0=gs_f[:, 1], scalar1=t1_f[:, 0:1],
                                scalar2=None, op0=ALU.is_equal)
                v.drain()
                v.scalar_tensor_tensor(out=gs_f[:, 3], in0=gs_f[:, 2], scalar=NEGBIG,
                                       in1=gs_f[:, 1], op0=ALU.mult, op1=ALU.add)
                v.drain()
                v.tensor_reduce(out=t1_f[:, 0:1], in_=gs_f[:, 3], axis=AX.X, op=ALU.max)
                v.drain()
                v.tensor_scalar(out=gs_f[:, 4], in0=gs_f[:, 3], scalar1=t1_f[:, 0:1],
                                scalar2=None, op0=ALU.is_equal)
                v.drain()
                v.tensor_tensor(out=gs_f[:, 5], in0=gs_f[:, 2], in1=gs_f[:, 4], op=ALU.add)
                v.drain()
                return v.tensor_tensor(out=wgt_b[:], in0=gs_f[:, 0], in1=gs_f[:, 5], op=ALU.mult)

            t_top = sc.op("vector", top2, wait=[t_sig, t_w, prev["tr"]], inc=EV)
            prev["top"] = t_top

            t_tr = sc.op("tensor",
                         lambda t: t.transpose(PT[:], wgt_b[:], ident_sb[:]),
                         wait=[t_top, prev["tr"]], inc=MM)
            t_wc = sc.op("vector", lambda v: v.tensor_copy(out=wgtT_b[:], in_=PT[:]),
                         wait=[t_tr, prev["wg_store"]], inc=EV)
            prev["tr"] = t_wc
            t_ws = sc.op("sync",
                         lambda e, cs=cs, col=col: e.dma_start(
                             out=kb.d["WG8"][:, cs[0] + col[0]:cs[0] + col[1]], in_=wgtT_b[:]),
                         wait=[t_wc], inc=STW, n=16)
            prev["wg_store"] = t_ws
        prev["ct_used"] = Tok(MM, prev["gmm"].val if False else MM.v)

    sc.op("sync", None, wait=[Tok(STX, STX.v), Tok(STW, STW.v)])
    with nc.Block() as blk:
        sc.emit(blk)


def chunked3(ap3, e, cp=128):
    return ap3[e:e + 1].rearrange("a (c p) o -> p (a c) o", p=cp)



def p4(kb, stack):
    nc = kb.nc
    sc = Sched(kb)
    MM, EV, AC = (kb.sem(x) for x in ("p4mm", "p4ev", "p4ac"))
    LDW, LD2, LDX, LDG = (kb.sem(x) for x in ("p4ldw", "p4ld2", "p4ldx", "p4ldg"))
    LDWE, LD2E = kb.sem("p4ldwe"), kb.sem("p4ld2e")
    DAC = kb.sem("p4dac")
    DA2 = kb.sem("p4da2")
    ctx = stack.enter_context
    w13_sb = ctx(nc.sbuf_tensor("p4_w13", [128, 16, 2 * HID], BF16))
    w2_sb = ctx(nc.sbuf_tensor("p4_w2", [128, 8, D], BF16))
    hT_sb = ctx(nc.sbuf_tensor("p4_hT", [128, 8, 2048], BF16))
    x1t_sb = ctx(nc.sbuf_tensor("p4_x1t", [128, 16, TT], BF16))
    sg2_f = ctx(nc.sbuf_tensor("p4_sg2", [128, 4, TT], F32))
    wg8_sb = ctx(nc.sbuf_tensor("p4_wg8", [8, S], BF16))
    sel_sb = ctx(nc.sbuf_tensor("p4_sel", [8, 8 * 128], BF16))
    wbc_sb = ctx(nc.sbuf_tensor("p4_wbc", [128, TT], F32))
    sg_f = ctx(nc.sbuf_tensor("p4_sg", [128, 4, TT], F32))
    oe_f = ctx(nc.sbuf_tensor("p4_oe", [128, 2, 4, TT], F32))
    PG = ctx(nc.psum_tensor("p4_PG", [128, 4, TT], F32))
    PU = ctx(nc.psum_tensor("p4_PU", [128, 4, TT], F32))

    sc.op("gpsimd", lambda g: g.dma_start(out=wg8_sb[:], in_=kb.d["WG8"]), inc=LDG, n=16)
    sc.op("gpsimd", lambda g: g.dma_start(out=sel_sb[:], in_=kb.sel8), inc=LDG, n=16)
    t_g = Tok(LDG, LDG.v)

    prev = {
        "p1A": None, "p1B": None,
        "w13_used": None, "w2_used": None,
        "x1_used": None, "wbc_used": None,
        "hT_used": None, "sg": [None, None],
        "oe_dma": [None, None], "oe_fill": None,
        "p2evA": None, "p2evB": None,
    }

    experts = [("sh", None)] + [("ex", e) for e in range(NE)]
    for kind, e in experts:
        if kind == "sh":
            def wld(g):
                return [g.dma_start(out=w13_sb[:, :, 0:HID], in_=chunked(kb.d["sw1T"])),
                        g.dma_start(out=w13_sb[:, :, HID:2 * HID], in_=chunked(kb.d["sw3T"]))]

            def w2ld(g):
                return [g.dma_start(out=w2_sb[:], in_=kb.d["sw2T"].rearrange("(c p) o -> p c o", p=128))]
        else:
            def wld(g, e=e):
                return [g.dma_start(out=w13_sb[:, :, 0:HID], in_=chunked3(kb.d["W1E"], e)),
                        g.dma_start(out=w13_sb[:, :, HID:2 * HID], in_=chunked3(kb.d["W3E"], e))]

            def w2ld(g, e=e):
                return [g.dma_start(out=w2_sb[:], in_=chunked3(kb.d["W2E"], e))]

        eng_w = "gpsimd" if kind == "sh" else "sync"
        t_w13 = sc.op(eng_w, wld, wait=[prev["w13_used"]],
                      inc=(LDW if kind == "sh" else LDWE), n=16, ni=2)
        t_w2 = sc.op(eng_w, w2ld, wait=[prev["w2_used"]],
                     inc=(LD2 if kind == "sh" else LD2E), n=16, ni=1)

        for hf in range(2):
            hmuls = []
            for tt4 in range(4):
                tt = hf * 4 + tt4
                cs = (tt * TT, (tt + 1) * TT)
                t_x1 = sc.op("gpsimd",
                             lambda g, cs=cs: g.dma_start(out=x1t_sb[:],
                                                          in_=chunked(kb.d["X1"], cs)),
                             wait=[prev["x1_used"]], inc=LDX, n=16)
                if kind == "ex":
                    def bcmm(t, e=e, cs=cs):
                        return t.matmul(PG[:, 0], sel_sb[:, e * 128:(e + 1) * 128],
                                        wg8_sb[:, cs[0]:cs[1]], start=True, stop=True)

                    t_bc = sc.op("tensor", bcmm,
                                 wait=[t_g, prev["x1_used"], prev["p1A"]], inc=MM)
                    t_bcc = sc.op("vector",
                                  lambda v: v.tensor_copy(out=wbc_sb[:], in_=PG[:, 0]),
                                  wait=[t_bc], inc=EV)
                else:
                    t_bcc = None
                xin = x1t_sb
                x_tok = t_x1

                for q4 in range(2):
                    def gmm4(t, q4=q4, xin=xin):
                        for k in range(4):
                            hc = q4 * 4 + k
                            for fc in range(16):
                                r = t.matmul(PG[:, k], w13_sb[:, fc, hc * 128:(hc + 1) * 128],
                                             xin[:, fc], start=(fc == 0), stop=(fc == 15))
                        return r

                    t_gm = sc.op("tensor", gmm4,
                                 wait=[x_tok, t_w13, prev["p1A"],
                                       t_bcc if (kind == "ex" and q4 == 0) else None],
                                 inc=MM)

                    def umm4(t, q4=q4, xin=xin):
                        for k in range(4):
                            hc = q4 * 4 + k
                            for fc in range(16):
                                r = t.matmul(PU[:, k],
                                             w13_sb[:, fc, HID + hc * 128:HID + (hc + 1) * 128],
                                             xin[:, fc], start=(fc == 0), stop=(fc == 15))
                        return r

                    t_um = sc.op("tensor", umm4, wait=[x_tok, t_w13, prev["p1B"]], inc=MM)
                    wide = lambda ap: ap.rearrange("p a t -> p (a t)")
                    wbc4 = bass.AP(wbc_sb[:].tensor, wbc_sb[:].offset,
                                   [list(wbc_sb[:].ap[0]), [0, 4], [1, TT]])
                    if kind == "ex":
                        def gsc(v, q4=q4):
                            return v.tensor_tensor(out=wide(sg2_f[:]), in0=wide(PG[:]),
                                                   in1=wbc4, op=ALU.mult)

                        t_gs = sc.op("vector", gsc, wait=[t_gm, t_bcc, prev["sg"][0]], inc=EV)
                        prev["p1A"] = t_gs
                        t_si = sc.op("scalar",
                                     lambda s: s.activation(out=wide(sg_f[:]),
                                                            in_=wide(sg2_f[:]), func=AF.Silu),
                                     wait=[t_gs], inc=AC)

                        def hmul4(v, q4=q4, tt4=tt4):
                            v.tensor_tensor(out=wide(sg2_f[:]), in0=wide(sg_f[:]),
                                            in1=wide(PU[:]), op=ALU.mult)
                            v.drain()
                            return v.tensor_tensor(
                                out=hT_sb[:, q4 * 4:(q4 + 1) * 4, tt4 * TT:(tt4 + 1) * TT],
                                in0=wide(sg2_f[:]), in1=wbc4, op=ALU.mult)

                        t_hm = sc.op("vector", hmul4,
                                     wait=[t_si, t_um, prev["hT_used"]], inc=EV)
                    else:
                        t_si = sc.op("scalar",
                                     lambda s: s.activation(out=wide(sg_f[:]),
                                                            in_=wide(PG[:]), func=AF.Silu),
                                     wait=[t_gm, prev["sg"][0]], inc=AC)
                        prev["p1A"] = t_si

                        def hmul4(v, q4=q4, tt4=tt4):
                            return v.tensor_tensor(
                                out=hT_sb[:, q4 * 4:(q4 + 1) * 4, tt4 * TT:(tt4 + 1) * TT],
                                in0=wide(sg_f[:]), in1=wide(PU[:]), op=ALU.mult)

                        t_hm = sc.op("vector", hmul4,
                                     wait=[t_si, t_um, prev["hT_used"]], inc=EV)
                    prev["p1B"] = t_hm
                    prev["sg"][0] = t_hm
                    hmuls.append(t_hm)
                prev["x1_used"] = hmuls[-1]
            t_p1 = hmuls[-1]

            for tt4 in range(4):
                tt = hf * 4 + tt4
                cs = (tt * TT, (tt + 1) * TT)
                for quad in range(4):
                    opar = quad % 2
                    evs = []
                    for j2 in range(2):
                        j0 = 2 * j2
                        oc0 = quad * 4 + j0
                        bk = (j2 % 2) * 2

                        def omm(t, oc0=oc0, tt4=tt4, bk=bk):
                            for k in range(2):
                                for hc in range(8):
                                    r = t.matmul(PG[:, bk + k],
                                                 w2_sb[:, hc, (oc0 + k) * 128:(oc0 + k + 1) * 128],
                                                 hT_sb[:, hc, tt4 * TT:(tt4 + 1) * TT],
                                                 start=(hc == 0), stop=(hc == 7))
                            return r

                        t_om = sc.op("tensor", omm,
                                     wait=[t_p1, t_w2,
                                           prev["p2evA"] if j2 % 2 == 0 else prev["p2evB"]],
                                     inc=MM)

                        def oev(v, j0=j0, bk=bk, opar=opar):
                            return v.tensor_copy(
                                out=oe_f[:, opar, j0:j0 + 2].rearrange("p a t -> p (a t)"),
                                in_=PG[:, bk:bk + 2].rearrange("p a t -> p (a t)"))

                        t_oe = sc.op("vector", oev,
                                     wait=[t_om, prev["oe_dma"][opar] if j2 == 0 else None],
                                     inc=EV)
                        evs.append(t_oe)
                        if j2 % 2 == 0:
                            prev["p2evA"] = t_oe
                        else:
                            prev["p2evB"] = t_oe

                    def accdma(g, cs=cs, quad=quad, kind=kind, opar=opar):
                        dst = kb.d["ACC"][quad * 512:(quad + 1) * 512,
                                          cs[0]:cs[1]].rearrange("(c p) t -> p c t", p=128)
                        if kind == "sh":
                            return g.dma_start(out=dst, in_=oe_f[:, opar])
                        return g.dma_start(out=dst, in_=oe_f[:, opar], accum_op=ALU.add)

                    t_acc = sc.op("gpsimd", accdma, wait=evs,
                                  inc=(DAC if opar == 0 else DA2), n=16)
                    prev["oe_dma"][opar] = t_acc
            prev["p1A"] = evs[-1]
            prev["hT_used"] = evs[-1]
        prev["w13_used"] = t_p1
        prev["w2_used"] = prev["oe_dma"][1]

    sc.op("sync", None, wait=[Tok(DAC, DAC.v), Tok(DA2, DA2.v)])
    with nc.Block() as blk:
        sc.emit(blk)



def p4b(kb, stack):
    nc = kb.nc
    sc = Sched(kb)
    MM, EV, AC = (kb.sem(x) for x in ("p4bmm", "p4bev", "p4bac"))
    LDX, LDA, LDW = (kb.sem(x) for x in ("p4bldx", "p4blda", "p4bldw"))
    STX = kb.sem("p4bstx")
    ctx = stack.enter_context
    ones_sb = ctx(nc.sbuf_tensor("p4b_ones", [128, 1], BF16))
    onesr_sb = ctx(nc.sbuf_tensor("p4b_onesr", [1, 128], BF16))
    eps_sb = ctx(nc.sbuf_tensor("p4b_eps", [1, 1], F32))
    x1t_sb = ctx(nc.sbuf_tensor("p4b_x1t", [128, 16, TT], BF16))
    ac_f = ctx(nc.sbuf_tensor("p4b_ac", [128, 16, TT], F32))
    r_f = ctx(nc.sbuf_tensor("p4b_rf", [128, 16, TT], F32))
    rb_b = ctx(nc.sbuf_tensor("p4b_rb", [128, 2, TT], BF16))
    sq_b = ctx(nc.sbuf_tensor("p4b_sq", [128, 2, TT], BF16))
    st_f = ctx(nc.sbuf_tensor("p4b_st", [1, 4, TT], F32))
    hl_b = ctx(nc.sbuf_tensor("p4b_hl", [1, 4, TT], BF16))
    mbc_sb = ctx(nc.sbuf_tensor("p4b_mbc", [128, TT], F32))
    rbc_sb = ctx(nc.sbuf_tensor("p4b_rbc", [128, TT], F32))
    t1_f = ctx(nc.sbuf_tensor("p4b_t1", [128, TT], F32))
    x2b_sb = ctx(nc.sbuf_tensor("p4b_x2b", [128, 16, TT], BF16))
    PM = ctx(nc.psum_tensor("p4b_PM", [1, TT], F32))
    PS = ctx(nc.psum_tensor("p4b_PS", [1, TT], F32))
    PBC = ctx(nc.psum_tensor("p4b_PBC", [128, TT], F32))

    for dst, srcd in ((ones_sb, kb.ones_bf), (onesr_sb, kb.ones_row), (eps_sb, kb.epsc)):
        sc.op("gpsimd", lambda g, dst=dst, srcd=srcd: g.dma_start(out=dst[:], in_=srcd),
              inc=LDW, n=16)
    t_w = Tok(LDW, LDW.v)

    prev = {"radd": None, "apply": None, "pm_free": None, "x2_store": None,
            "stat": [None, None]}
    for tt in range(NTT):
        cs = (tt * TT, (tt + 1) * TT)
        t_x1 = sc.op("gpsimd",
                     lambda g, cs=cs: g.dma_start(out=x1t_sb[:], in_=chunked(kb.d["X1"], cs)),
                     wait=[prev["radd"]], inc=LDX, n=16)
        t_ac = sc.op("gpsimd",
                     lambda g, cs=cs: g.dma_start(out=ac_f[:], in_=chunked(kb.d["ACC"], cs)),
                     wait=[prev["radd"]], inc=LDA, n=16)

        def radd(v):
            for oc in range(16):
                r = v.tensor_tensor(out=r_f[:, oc], in0=ac_f[:, oc], in1=x1t_sb[:, oc],
                                    op=ALU.add)
            return r

        t_radd = sc.op("vector", radd, wait=[t_x1, t_ac, prev["apply"]], inc=EV)
        prev["radd"] = t_radd

        t_stat = None
        for oc in range(16):
            par = oc % 2
            t_rb = sc.op("vector",
                         lambda v, oc=oc, par=par: v.tensor_copy(out=rb_b[:, par], in_=r_f[:, oc]),
                         wait=[t_radd, prev["stat"][par]], inc=EV)
            t_sq = sc.op("scalar",
                         lambda s, oc=oc, par=par: s.activation(out=sq_b[:, par], in_=r_f[:, oc],
                                                                func=AF.Square),
                         wait=[t_radd, prev["stat"][par]], inc=AC)

            def smm(t, oc=oc, par=par):
                t.matmul(PM[:], ones_sb[:], rb_b[:, par], start=(oc == 0), stop=(oc == 15))
                return t.matmul(PS[:], ones_sb[:], sq_b[:, par], start=(oc == 0), stop=(oc == 15))

            t_stat = sc.op("tensor", smm,
                           wait=[t_rb, t_sq, t_w, prev["pm_free"] if oc == 0 else None], inc=MM)
            prev["stat"][par] = t_stat

        t_bc, t_statsdone = ln_tail(
            sc, kb, nfeat=D, PMp=PM, PSp=PS, st_f=st_f, hl_b=hl_b,
            onesr_sb=onesr_sb, PBC=PBC, mbc_sb=mbc_sb, rbc_sb=rbc_sb,
            stat_tok=t_stat, bc_war_tok=prev["apply"], MM=MM, EV=EV, AC=AC,
            eps_ap=eps_sb[:, :])
        prev["pm_free"] = t_statsdone

        def apply(v):
            for oc in range(16):
                v.tensor_tensor(out=t1_f[:], in0=r_f[:, oc], in1=mbc_sb[:], op=ALU.subtract)
                v.drain()
                r = v.tensor_tensor(out=x2b_sb[:, oc], in0=t1_f[:], in1=rbc_sb[:], op=ALU.mult)
                v.drain()
            return r

        t_apply = sc.op("vector", apply, wait=[t_bc, t_radd, prev["x2_store"]], inc=EV)
        prev["apply"] = t_apply
        t_x2s = sc.op("sync",
                      lambda e, cs=cs: e.dma_start(out=chunked(kb.d["X2"], cs), in_=x2b_sb[:]),
                      wait=[t_apply], inc=STX, n=16)
        prev["x2_store"] = t_x2s

    sc.op("sync", None, wait=[Tok(STX, STX.v)])
    with nc.Block() as blk:
        sc.emit(blk)



def p5(kb, stack, sc=None):
    nc = kb.nc
    own = sc is None
    if own:
        sc = Sched(kb)
    MM, EV, AC = (kb.sem(x) for x in ("p5mm", "p5ev", "p5ac"))
    LDW, LDX, LDT = (kb.sem(x) for x in ("p5ldw", "p5ldx", "p5ldt"))
    STK, STQ, STV, STS = (kb.sem(x) for x in ("p5stk", "p5stq", "p5stv", "p5sts"))
    ctx = stack.enter_context
    wq_sb = ctx(nc.sbuf_tensor("p5_wq", [128, 16, D], BF16))
    wk_sb = ctx(nc.sbuf_tensor("p5_wk", [128, 16, NKV * HD], BF16))
    wv_sb = ctx(nc.sbuf_tensor("p5_wv", [128, 16, NKV * HD], BF16))
    ones_sb = ctx(nc.sbuf_tensor("p5_ones", [128, 1], BF16))
    x2t_sb = ctx(nc.sbuf_tensor("p5_x2t", [128, 16, TT], BF16))
    rtk_sb = ctx(nc.sbuf_tensor("p5_rtk", [128, 4, TT], BF16))
    rtq_sb = ctx(nc.sbuf_tensor("p5_rtq", [128, 4, TT], BF16))
    rsc_f = ctx(nc.sbuf_tensor("p5_rsc", [128, 2, TT], F32))
    olo_b = ctx(nc.sbuf_tensor("p5_olo", [128, TT], BF16))
    ohi_b = ctx(nc.sbuf_tensor("p5_ohi", [128, TT], BF16))
    ksq_b = ctx(nc.sbuf_tensor("p5_ksq", [128, 2, TT], BF16))
    n2_f = ctx(nc.sbuf_tensor("p5_n2", [1, 96], F32))
    vout_b = ctx(nc.sbuf_tensor("p5_vout", [128, 512], BF16))
    PQA = ctx(nc.psum_tensor("p5_PQA", [128, TT], F32))
    PQB = ctx(nc.psum_tensor("p5_PQB", [128, TT], F32))
    PV = ctx(nc.psum_tensor("p5_PV", [128, 512], F32))
    PW = ctx(nc.psum_tensor("p5_PW", [1, TT], F32))

    for dst, srcd in ((wq_sb, chunked(kb.d["wqT"])), (wk_sb, chunked(kb.d["wkT"])),
                      (wv_sb, chunked(kb.d["wvT"])), (ones_sb, kb.ones_bf)):
        sc.op("gpsimd", lambda g, dst=dst, srcd=srcd: g.dma_start(out=dst[:], in_=srcd),
              inc=LDW, n=16)
    t_w = Tok(LDW, LDW.v)

    prev = {"x2_used": None, "tab_used": None, "ropeA": None, "ropeB": None,
            "pv_ev": None, "norm": None, "pst": None, "vst": None}

    def rope_ops(v, cs, tabs):
        v.tensor_tensor(out=rsc_f[:, 0], in0=PQA[:], in1=tabs[:, 0], op=ALU.mult)
        v.tensor_tensor(out=rsc_f[:, 1], in0=PQB[:], in1=tabs[:, 1], op=ALU.mult)
        v.drain()
        v.tensor_tensor(out=olo_b[:], in0=rsc_f[:, 0], in1=rsc_f[:, 1], op=ALU.subtract)
        v.drain()
        v.tensor_tensor(out=rsc_f[:, 0], in0=PQB[:], in1=tabs[:, 2], op=ALU.mult)
        v.tensor_tensor(out=rsc_f[:, 1], in0=PQA[:], in1=tabs[:, 3], op=ALU.mult)
        v.drain()
        v.tensor_tensor(out=ohi_b[:], in0=rsc_f[:, 0], in1=rsc_f[:, 1], op=ALU.add)
        v.drain()
        v.tensor_tensor(out=ksq_b[:, 0], in0=olo_b[:], in1=olo_b[:], op=ALU.mult)
        return v.tensor_tensor(out=ksq_b[:, 1], in0=ohi_b[:], in1=ohi_b[:], op=ALU.mult)

    for tt in range(NTT):
        cs = (tt * TT, (tt + 1) * TT)
        t_x2 = sc.op("gpsimd",
                     lambda g, cs=cs: g.dma_start(out=x2t_sb[:], in_=chunked(kb.d["X2"], cs)),
                     wait=[prev["x2_used"]], inc=LDX, n=16)

        def tabld(g, cs=cs, tt=tt):
            r = [g.dma_start(out=rtk_sb[:, i], in_=kb.d["r1k"][:, i * S + cs[0]:i * S + cs[1]])
                 for i in range(4)]
            if tt < NTO:
                r += [g.dma_start(out=rtq_sb[:, i],
                                  in_=kb.d["r1q"][:, i * 2048 + cs[0]:i * 2048 + cs[1]])
                      for i in range(4)]
            return r

        t_tab = sc.op("gpsimd", tabld, wait=[prev["tab_used"]], inc=LDT, n=16,
                      ni=8 if tt < NTO else 4)

        jobs = [("k", kp) for kp in range(NKV // 2)]
        if tt < NTO:
            jobs += [("q", p) for p in range(NH // 2)]
        for kind, p in jobs:
            wsb = wk_sb if kind == "k" else wq_sb
            tabs = rtk_sb if kind == "k" else rtq_sb

            def pmm(t, wsb=wsb, p=p):
                for fc in range(16):
                    t.matmul(PQA[:], wsb[:, fc, p * 256:p * 256 + 128], x2t_sb[:, fc],
                             start=(fc == 0), stop=(fc == 15))
                for fc in range(16):
                    r = t.matmul(PQB[:], wsb[:, fc, p * 256 + 128:p * 256 + 256], x2t_sb[:, fc],
                                 start=(fc == 0), stop=(fc == 15))
                return r

            t_mm = sc.op("tensor", pmm, wait=[t_x2, t_w, prev["ropeA"], prev["ropeB"]], inc=MM)
            t_rope = sc.op("vector", lambda v, cs=cs, tabs=tabs: rope_ops(v, cs, tabs),
                           wait=[t_mm, t_tab, prev["pst"], prev["norm"]],
                           inc=EV)
            prev["ropeA"] = t_rope
            prev["ropeB"] = t_rope

            for hh, base in ((2 * p, 0), (2 * p + 1, 64)):
                def nmm(t, base=base):
                    t.matmul(PW[:], ones_sb[base:base + 64, :], ksq_b[base:base + 64, 0],
                             start=True, stop=False)
                    return t.matmul(PW[:], ones_sb[base:base + 64, :], ksq_b[base:base + 64, 1],
                                    start=False, stop=True)

                t_nm = sc.op("tensor", nmm, wait=[t_rope, prev["norm"]], inc=MM)
                slot = (64 + hh * 8 + tt) if kind == "k" else (hh * 4 + tt)
                t_nx = sc.op("vector",
                             lambda v, slot=slot: v.tensor_reduce(
                                 out=n2_f[:, slot:slot + 1], in_=PW[:], axis=AX.X, op=ALU.max),
                             wait=[t_nm], inc=EV)
                prev["norm"] = t_nx

            dname = "K1" if kind == "k" else "Q1"
            stsem = STK if kind == "k" else STQ

            def pst(e, p=p, cs=cs, dname=dname):
                dd = kb.d[dname]
                h0, h1 = 2 * p, 2 * p + 1
                return [
                    e.dma_start(out=dd[h0 * HD:h0 * HD + 64, cs[0]:cs[1]], in_=olo_b[0:64, :]),
                    e.dma_start(out=dd[h0 * HD + 64:(h0 + 1) * HD, cs[0]:cs[1]], in_=ohi_b[0:64, :]),
                    e.dma_start(out=dd[h1 * HD:h1 * HD + 64, cs[0]:cs[1]], in_=olo_b[64:128, :]),
                    e.dma_start(out=dd[h1 * HD + 64:(h1 + 1) * HD, cs[0]:cs[1]], in_=ohi_b[64:128, :])]

            t_st = sc.op("sync", pst, wait=[t_rope], inc=stsem, n=16, ni=4)
            prev["pst"] = t_st

        for j in range(4):
            sub = (cs[0] + j * 128, cs[0] + (j + 1) * 128)

            def vmm(t, j=j):
                for fc in range(16):
                    r = t.matmul(PV[:], x2t_sb[:, fc, j * 128:(j + 1) * 128], wv_sb[:, fc],
                                 start=(fc == 0), stop=(fc == 15))
                return r

            t_vm = sc.op("tensor", vmm, wait=[t_x2, t_w, prev["pv_ev"]], inc=MM)
            t_ve = sc.op("vector", lambda v: v.tensor_copy(out=vout_b[:], in_=PV[:]),
                         wait=[t_vm, prev["vst"]], inc=EV)
            prev["pv_ev"] = t_ve
            t_vs = sc.op("sync",
                         lambda e, sub=sub: e.dma_start(out=kb.d["V1"][sub[0]:sub[1], :],
                                                        in_=vout_b[:]),
                         wait=[t_ve], inc=STV, n=16)
            prev["vst"] = t_vs
        prev["x2_used"] = Tok(MM, MM.v)
        prev["tab_used"] = prev["ropeA"]

    t_ns = sc.op("sync", lambda e: e.dma_start(out=kb.d["QS1"], in_=n2_f[:]),
                 wait=[prev["norm"]], inc=STS, n=16)
    sc.op("sync", None, wait=[Tok(STK, STK.v), Tok(STQ, STQ.v), Tok(STV, STV.v),
                              Tok(STS, STS.v)])
    if own:
        with nc.Block() as blk:
            sc.emit(blk)
    return sc



def p7(kb, stack):
    nc = kb.nc
    sc = Sched(kb)
    MM, EV, AC = (kb.sem(x) for x in ("p7mm", "p7ev", "p7ac"))
    LDC = kb.sem("p7ldc")
    LQ = [kb.sem("p7lqa"), kb.sem("p7lqb")]
    STC = kb.sem("p7stc")
    ctx = stack.enter_context
    k_sb = ctx(nc.sbuf_tensor("p7_k", [128, 4, S], BF16))
    v_sb = ctx(nc.sbuf_tensor("p7_v", [128, 32, 512], BF16))
    qs_sb = ctx(nc.sbuf_tensor("p7_qs", [1, 96], F32))
    bs_f = ctx(nc.sbuf_tensor("p7_bs", [1, 4, 16], F32))
    bs_b = ctx(nc.sbuf_tensor("p7_bsb", [1, 16], BF16))
    bias_sb = ctx(nc.sbuf_tensor("p7_bias", [128, 16], F32))
    q_sb = ctx(nc.sbuf_tensor("p7_q", [128, 2, TT], BF16))
    P_sb = ctx(nc.sbuf_tensor("p7_P", [128, 2, 8, TT], BF16))
    su_f = ctx(nc.sbuf_tensor("p7_su", [1, 2, TT], F32))
    hl_b = ctx(nc.sbuf_tensor("p7_hl", [1, 2, TT], BF16))
    bc_f = ctx(nc.sbuf_tensor("p7_bc", [128, TT], F32))
    ct_b = ctx(nc.sbuf_tensor("p7_ct", [128, TT], BF16))
    ones_sb = ctx(nc.sbuf_tensor("p7_ones", [128, 1], BF16))
    onesr_sb = ctx(nc.sbuf_tensor("p7_onesr", [1, 128], BF16))
    PSC = ctx(nc.psum_tensor("p7_PSC", [128, 4, TT], F32))
    PZ = ctx(nc.psum_tensor("p7_PZ", [128, TT], F32))
    PW = ctx(nc.psum_tensor("p7_PW", [1, TT], F32))
    PBC = ctx(nc.psum_tensor("p7_PBC", [128, TT], F32))
    PBI = ctx(nc.psum_tensor("p7_PBI", [128, 16], F32))

    sc.op("gpsimd", lambda g: g.dma_start(out=ones_sb[:], in_=kb.ones_bf), inc=LDC, n=16)
    sc.op("gpsimd", lambda g: g.dma_start(out=onesr_sb[:], in_=kb.ones_row), inc=LDC, n=16)
    sc.op("gpsimd", lambda g: g.dma_start(out=qs_sb[:], in_=kb.d["QS1"]), inc=LDC, n=16)
    sc.op("gpsimd",
          lambda g: g.dma_start(out=k_sb[:], in_=kb.d["K1"].rearrange("(c p) t -> p c t", p=128)),
          inc=LDC, n=16)
    sc.op("gpsimd",
          lambda g: g.dma_start(out=v_sb[:], in_=kb.d["V1"].rearrange("(kt p) o -> p kt o", p=128)),
          inc=LDC, n=16)
    t_res = Tok(LDC, LDC.v)

    def bmax(v):
        for h in range(NH):
            v.tensor_reduce(out=bs_f[:, 0, h:h + 1], in_=qs_sb[:, h * 4:h * 4 + 4],
                            axis=AX.X, op=ALU.max)
            kvh = h // 4
            v.tensor_reduce(out=bs_f[:, 1, h:h + 1], in_=qs_sb[:, 64 + kvh * 8:64 + kvh * 8 + 8],
                            axis=AX.X, op=ALU.max)
        v.drain()
        return v.tensor_tensor(out=bs_f[:, 2], in0=bs_f[:, 0], in1=bs_f[:, 1], op=ALU.mult)

    t_bm = sc.op("vector", bmax, wait=[t_res], inc=EV)
    t_bsq = sc.op("scalar",
                  lambda s: s.activation(out=bs_f[:, 3], in_=bs_f[:, 2], func=AF.Sqrt),
                  wait=[t_bm], inc=AC)
    t_bneg = sc.op("vector",
                   lambda v: v.tensor_scalar(out=bs_b[:], in0=bs_f[:, 3], scalar1=-1.0,
                                             scalar2=None, op0=ALU.mult),
                   wait=[t_bsq], inc=EV)

    def bimm(t):
        for h in range(NH):
            r = t.matmul(PBI[:, h:h + 1], onesr_sb[:], bs_b[:, h:h + 1], start=True, stop=True)
        return r

    t_bimm = sc.op("tensor", bimm, wait=[t_bneg], inc=MM)
    t_bias = sc.op("vector", lambda v: v.tensor_copy(out=bias_sb[:], in_=PBI[:]),
                   wait=[t_bimm], inc=EV)

    prev = {"ctx_done": None, "qload": [None, None], "qtok": [None, None],
            "cmm2": None, "cmm3": None, "pending_epi": None, "st": None}
    units = [(h, qt) for h in range(NH) for qt in range(NTO)]

    def emit_qld(u):
        h_, qt_ = units[u]
        par_ = u % 2
        qs2 = (qt_ * TT, (qt_ + 1) * TT)
        prev["qtok"][par_] = sc.op(
            "gpsimd",
            lambda g, h_=h_, qs2=qs2, par_=par_: g.dma_start(
                out=q_sb[:, par_], in_=kb.d["Q1"][h_ * HD:(h_ + 1) * HD, qs2[0]:qs2[1]]),
            wait=[prev["qload"][par_]], inc=LQ[par_], n=16)

    for uu, (h, qt) in enumerate(units):
        kvh = h // 4
        if True:
            qs_ = (qt * TT, (qt + 1) * TT)
            par = uu % 2
            if uu == 0:
                emit_qld(0)
                emit_qld(1)
            elif uu + 1 < len(units):
                emit_qld(uu + 1)
            t_ql = prev["qtok"][par]

            t_exp4 = [None] * 4
            t_cmm4 = [None] * 4

            def emit_cmm(qq):
                def cmm8(t, kvh=kvh, qq=qq):
                    for jj in range(8):
                        kt = qq * 8 + jj
                        t.matmul(PZ[:], v_sb[:, kt, kvh * 128:(kvh + 1) * 128],
                                 P_sb[:, qq % 2, jj],
                                 start=(kt == 0), stop=(kt == 31))
                        r = t.matmul(PW[:], ones_sb[:], P_sb[:, qq % 2, jj],
                                     start=(kt == 0), stop=(kt == 31))
                    return r

                t_cmm4[qq] = sc.op("tensor", cmm8,
                                   wait=[t_exp4[qq],
                                         prev["ctx_done"] if qq == 0 else None],
                                   inc=MM)

            pairtok = prev.setdefault("evroll", [None, None])
            for q in range(4):
                qp_ = q % 2
                t_e = None
                for rr in range(4):
                    pr = rr % 2

                    def smm(t, kvh=kvh, q=q, rr=rr, pr=pr, par=par):
                        for u in range(2):
                            kt = q * 8 + 2 * rr + u
                            r = t.matmul(PSC[:, 2 * pr + u],
                                         k_sb[:, kvh, kt * 128:(kt + 1) * 128],
                                         q_sb[:, par], start=True, stop=True)
                        return r

                    t_s = sc.op("tensor", smm,
                                wait=[t_ql, t_res, pairtok[pr]],
                                inc=MM)

                    def expf(s, h=h, qp_=qp_, rr=rr, pr=pr):
                        return s.activation(
                            out=P_sb[:, qp_, 2 * rr:2 * rr + 2].rearrange("p a t -> p (a t)"),
                            in_=PSC[:, 2 * pr:2 * pr + 2].rearrange("p a t -> p (a t)"),
                            func=AF.Exp, bias=bias_sb[:, h:h + 1])

                    t_e = sc.op("scalar", expf,
                                wait=[t_s, t_bias,
                                      (t_cmm4[q - 2] if q >= 2 else
                                       prev["cmm2" if q == 0 else "cmm3"])
                                      if rr == 0 else None],
                                inc=AC)
                    pairtok[pr] = t_e
                    if q == 0 and rr == 1 and prev.get("pending_epi"):
                        prev["pending_epi"]()
                        prev["pending_epi"] = None
                t_exp4[q] = t_e
                if q >= 1:
                    emit_cmm(q - 1)
            emit_cmm(3)
            t_ctx = [t_cmm4[3]] * 32
            prev["qload"][par] = t_s
            prev["cmm2"], prev["cmm3"] = t_cmm4[2], t_cmm4[3]

            def epi1(v):
                v.tensor_copy(out=su_f[:, 0], in_=PW[:])
                v.drain()
                v.reciprocal(out=su_f[:, 1], in_=su_f[:, 0])
                v.drain()
                return emit_hi_lo(v, su_f[:, 1], hl_b[:, 0], hl_b[:, 1])

            t_ep1 = sc.op("vector", epi1, wait=[t_ctx[31]], inc=EV)

            def flush_epi(t_ep1=t_ep1, h=h, qs_=qs_):
                def epbc(t):
                    t.matmul(PBC[:], onesr_sb[:], hl_b[:, 0], start=True, stop=False)
                    return t.matmul(PBC[:], onesr_sb[:], hl_b[:, 1], start=False, stop=True)

                t_eb = sc.op("tensor", epbc, wait=[t_ep1], inc=MM)

                def epi2(v):
                    v.tensor_copy(out=bc_f[:], in_=PBC[:])
                    v.drain()
                    return v.tensor_tensor(out=ct_b[:], in0=PZ[:], in1=bc_f[:], op=ALU.mult)

                t_ep2 = sc.op("vector", epi2, wait=[t_eb, prev["st"]], inc=EV)
                prev["ctx_done"] = t_ep2
                t_st = sc.op("sync",
                             lambda e, h=h, qs_=qs_: e.dma_start(
                                 out=kb.d["CT1"][h * 128:(h + 1) * 128, qs_[0]:qs_[1]],
                                 in_=ct_b[:]),
                             wait=[t_ep2, prev["st"]], inc=STC, n=16)
                prev["st"] = t_st

            prev["pending_epi"] = flush_epi

    if prev.get("pending_epi"):
        prev["pending_epi"]()
        prev["pending_epi"] = None
    sc.op("sync", None, wait=[Tok(STC, STC.v)])
    with nc.Block() as blk:
        sc.emit(blk)



def p8(kb, stack):
    nc = kb.nc
    sc = Sched(kb)
    MM, EV, AC = (kb.sem(x) for x in ("p8mm", "p8ev", "p8ac"))
    LDW, LDC, LDX = (kb.sem(x) for x in ("p8ldw", "p8ldc", "p8ldx"))
    STX = kb.sem("p8stx")
    ctx = stack.enter_context
    ow_sb = ctx(nc.sbuf_tensor("p8_ow", [128, 16, D], BF16))
    ones_sb = ctx(nc.sbuf_tensor("p8_ones", [128, 1], BF16))
    onesr_sb = ctx(nc.sbuf_tensor("p8_onesr", [1, 128], BF16))
    eps_sb = ctx(nc.sbuf_tensor("p8_eps", [1, 1], F32))
    ct_sb = ctx(nc.sbuf_tensor("p8_ct", [128, 16, TT], BF16))
    x2t_sb = ctx(nc.sbuf_tensor("p8_x2t", [128, 16, TT], BF16))
    r_f = ctx(nc.sbuf_tensor("p8_rf", [128, 16, TT], F32))
    rb_b = ctx(nc.sbuf_tensor("p8_rb", [128, 2, TT], BF16))
    sq_b = ctx(nc.sbuf_tensor("p8_sq", [128, 2, TT], BF16))
    st_f = ctx(nc.sbuf_tensor("p8_st", [1, 4, TT], F32))
    hl_b = ctx(nc.sbuf_tensor("p8_hl", [1, 4, TT], BF16))
    mbc_sb = ctx(nc.sbuf_tensor("p8_mbc", [128, TT], F32))
    rbc_sb = ctx(nc.sbuf_tensor("p8_rbc", [128, TT], F32))
    t1_f = ctx(nc.sbuf_tensor("p8_t1", [128, TT], F32))
    x3b_sb = ctx(nc.sbuf_tensor("p8_x3b", [128, 16, TT], BF16))
    PA = ctx(nc.psum_tensor("p8_PA", [128, TT], F32))
    PB = ctx(nc.psum_tensor("p8_PB", [128, TT], F32))
    PM = ctx(nc.psum_tensor("p8_PM", [1, TT], F32))
    PS = ctx(nc.psum_tensor("p8_PS", [1, TT], F32))
    PBC = ctx(nc.psum_tensor("p8_PBC", [128, TT], F32))

    for dst, srcd in ((ow_sb, chunked(kb.d["woT"])), (ones_sb, kb.ones_bf),
                      (onesr_sb, kb.ones_row), (eps_sb, kb.epsc)):
        sc.op("gpsimd", lambda g, dst=dst, srcd=srcd: g.dma_start(out=dst[:], in_=srcd),
              inc=LDW, n=16)
    t_w = Tok(LDW, LDW.v)

    prev = {"mmA": None, "mmB": None, "apply": None, "pm_free": None,
            "x3_store": None, "stat": [None, None], "ct_used": None, "x2_used": None}
    for tt in range(NTO):
        cs = (tt * TT, (tt + 1) * TT)
        t_ct = sc.op("gpsimd",
                     lambda g, cs=cs: g.dma_start(out=ct_sb[:], in_=chunked(kb.d["CT1"], cs)),
                     wait=[prev["ct_used"]], inc=LDC, n=16)
        t_x2 = sc.op("gpsimd",
                     lambda g, cs=cs: g.dma_start(out=x2t_sb[:], in_=chunked(kb.d["X2"], cs)),
                     wait=[prev["x2_used"]], inc=LDX, n=16)

        t_radd = [None] * 16
        t_stat = None
        for oc in range(16):
            tgt = PA if oc % 2 == 0 else PB
            warp = prev["mmA"] if oc % 2 == 0 else prev["mmB"]

            def omm(t, oc=oc, tgt=tgt):
                for cc in range(16):
                    r = t.matmul(tgt[:], ow_sb[:, cc, oc * 128:(oc + 1) * 128],
                                 ct_sb[:, cc], start=(cc == 0), stop=(cc == 15))
                return r

            t_mm = sc.op("tensor", omm, wait=[t_ct, t_w, warp], inc=MM)
            t_radd[oc] = sc.op("vector",
                               lambda v, oc=oc, tgt=tgt: v.tensor_tensor(
                                   out=r_f[:, oc], in0=tgt[:], in1=x2t_sb[:, oc], op=ALU.add),
                               wait=[t_mm, t_x2, prev["apply"]], inc=EV)
            if oc % 2 == 0:
                prev["mmA"] = t_radd[oc]
            else:
                prev["mmB"] = t_radd[oc]
            par = oc % 2
            t_rb = sc.op("vector",
                         lambda v, oc=oc, par=par: v.tensor_copy(out=rb_b[:, par], in_=r_f[:, oc]),
                         wait=[t_radd[oc], prev["stat"][par]], inc=EV)
            t_sq = sc.op("scalar",
                         lambda s, oc=oc, par=par: s.activation(out=sq_b[:, par], in_=r_f[:, oc],
                                                                func=AF.Square),
                         wait=[t_radd[oc], prev["stat"][par]], inc=AC)

            def smm(t, oc=oc, par=par):
                t.matmul(PM[:], ones_sb[:], rb_b[:, par], start=(oc == 0), stop=(oc == 15))
                return t.matmul(PS[:], ones_sb[:], sq_b[:, par], start=(oc == 0), stop=(oc == 15))

            t_stat = sc.op("tensor", smm,
                           wait=[t_rb, t_sq, prev["pm_free"] if oc == 0 else None], inc=MM)
            prev["stat"][par] = t_stat

        t_bc, t_statsdone = ln_tail(
            sc, kb, nfeat=D, PMp=PM, PSp=PS, st_f=st_f, hl_b=hl_b,
            onesr_sb=onesr_sb, PBC=PBC, mbc_sb=mbc_sb, rbc_sb=rbc_sb,
            stat_tok=t_stat, bc_war_tok=prev["apply"], MM=MM, EV=EV, AC=AC,
            eps_ap=eps_sb[:, :])
        prev["pm_free"] = t_statsdone

        def apply(v):
            for oc in range(16):
                v.tensor_tensor(out=t1_f[:], in0=r_f[:, oc], in1=mbc_sb[:], op=ALU.subtract)
                v.drain()
                r = v.tensor_tensor(out=x3b_sb[:, oc], in0=t1_f[:], in1=rbc_sb[:], op=ALU.mult)
                v.drain()
            return r

        t_apply = sc.op("vector", apply, wait=[t_bc, prev["x3_store"]] + t_radd, inc=EV)
        prev["apply"] = t_apply
        prev["x2_used"] = t_apply
        prev["ct_used"] = Tok(MM, MM.v)
        t_x3s = sc.op("sync",
                      lambda e, cs=cs: e.dma_start(out=chunked(kb.d["X3"], cs), in_=x3b_sb[:]),
                      wait=[t_apply], inc=STX, n=16)
        prev["x3_store"] = t_x3s

    sc.op("sync", None, wait=[Tok(STX, STX.v)])
    with nc.Block() as blk:
        sc.emit(blk)



def p9(kb, stack, sc=None, precast=True):
    nc = kb.nc
    own = sc is None
    if own:
        sc = Sched(kb)
    MM, EV, AC = (kb.sem(x) for x in ("p9mm", "p9ev", "p9ac"))
    LDH = kb.sem("p9ldh")
    LB = [kb.sem("p9lba"), kb.sem("p9lbb")]
    LA = [kb.sem("p9laa"), kb.sem("p9lab")]
    STB = kb.sem("p9stb")
    SA = [kb.sem("p9saa"), kb.sem("p9sab")]
    ctx = stack.enter_context
    h64_sb = ctx(nc.sbuf_tensor("p9_h64", [64, 64], BF16))
    h128_sb = ctx(nc.sbuf_tensor("p9_h128", [128, 128], BF16))
    w2in = ctx(nc.sbuf_tensor("p9_w2in", [64, 2, D], BF16))
    fout = ctx(nc.sbuf_tensor("p9_fout", [64, D], BF16))
    fsin = ctx(nc.sbuf_tensor("p9_fsin", [128, 2, D], BF16))
    wa_b = ctx(nc.sbuf_tensor("p9_wa", [128, 2, 512], BF16))
    PN = ctx(nc.psum_tensor("p9_PN", [128, 2, 512], F32))

    sc.op("gpsimd", lambda g: g.dma_start(out=h64_sb[:], in_=kb.h64), inc=LDH, n=16)
    sc.op("gpsimd", lambda g: g.dma_start(out=h128_sb[:], in_=kb.h128), inc=LDH, n=16)
    t_h = Tok(LDH, LDH.v)

    prev = {"bmm": [None, None], "bst": None, "bev": None, "ain": None,
            "aev": [None, None], "ast": [None, None], "amm": [None, None]}
    t_cw = None
    if precast:
        CW = kb.sem("p9cw")
        sc.op("gpsimd", lambda g: g.dma_start(out=kb.d["W1B"], in_=kb.d["W1T"]), inc=CW, n=16)
        sc.op("gpsimd", lambda g: g.dma_start(out=kb.d["W3B"], in_=kb.d["W3T"]), inc=CW, n=16)
        t_cw = Tok(CW, CW.v)
    for j in range(128):
        par = j % 2

        def bload(g, j=j, par=par):
            srcv = kb.d["W2T"].rearrange("(a q) o -> q a o", q=128)[j:j + 1].rearrange(
                "z a o -> (z a) o")
            return g.dma_start(out=w2in[:, par], in_=srcv)

        t_bl = sc.op("gpsimd", bload, wait=[prev["bmm"][par]], inc=LB[par], n=16)

        t_be = None
        for seg in range(4):
            sp = seg % 2

            def bmm(t, par=par, seg=seg, sp=sp):
                return t.matmul(PN[0:64, sp], h64_sb[:],
                                w2in[:, par, seg * 512:(seg + 1) * 512], start=True, stop=True)

            t_bm = sc.op("tensor", bmm,
                         wait=[t_bl, t_h, prev["bev"][sp] if isinstance(prev["bev"], list) else prev["bev"]],
                         inc=MM)
            if seg == 3:
                prev["bmm"][par] = t_bm
            t_be = sc.op("vector",
                         lambda v, seg=seg, sp=sp: v.tensor_copy(
                             out=fout[:, seg * 512:(seg + 1) * 512], in_=PN[0:64, sp]),
                         wait=[t_bm, prev["bst"] if seg <= 1 else None], inc=EV)
            if not isinstance(prev["bev"], list):
                prev["bev"] = [None, None]
            prev["bev"][sp] = t_be

        def bstore(e, j=j):
            dstv = kb.d["FS"].rearrange("(a q) o -> q a o", q=128)[j:j + 1].rearrange(
                "z a o -> (z a) o")
            return e.dma_start(out=dstv, in_=fout[:])

        t_bs = sc.op("sync", bstore, wait=[t_be], inc=STB, n=16)
        prev["bst"] = t_bs
    t_ball = Tok(STB, STB.v)

    SCW = float(1.0 / np.sqrt(FF))
    for c in range(64):
        par = c % 2
        t_al = sc.op("gpsimd",
                     lambda g, c=c, par=par: g.dma_start(
                         out=fsin[:, par], in_=kb.d["FS"][c * 128:(c + 1) * 128, :]),
                     wait=[t_ball, prev["amm"][par]], inc=LA[par], n=16)
        for j4 in range(4):
            pj = j4 % 2

            def amm(t, par=par, j4=j4, pj=pj):
                return t.matmul(PN[:, pj], h128_sb[:], fsin[:, par, j4 * 512:(j4 + 1) * 512],
                                start=True, stop=True)

            t_am = sc.op("tensor", amm,
                         wait=[t_al, t_h, prev["aev"][pj], t_ball], inc=MM)
            if j4 == 3:
                prev["amm"][par] = t_am

            t_ae = sc.op("scalar",
                         lambda s, pj=pj: s.mul(wa_b[:, pj], PN[:, pj], SCW),
                         wait=[t_am, prev["ast"][pj]], inc=AC)
            prev["aev"][pj] = t_ae

            def astore(e, c=c, j4=j4, pj=pj):
                dstv = kb.d["W2X"][j4 * 4:(j4 + 1) * 4, c].rearrange("a p o -> p a o")
                return e.dma_start(out=dstv, in_=wa_b[:, pj].rearrange("p (a o) -> p a o", a=4))

            t_as = sc.op("sync", astore, wait=[t_ae], inc=SA[pj], n=16)
            prev["ast"][pj] = t_as

    sc.op("sync", None, wait=[Tok(SA[0], SA[0].v), Tok(SA[1], SA[1].v), t_cw])
    if own:
        with nc.Block() as blk:
            sc.emit(blk)
    return sc



def p10(kb, stack):
    nc = kb.nc
    sc = Sched(kb)
    MM, EV, AC = (kb.sem(x) for x in ("p10mm", "p10ev", "p10ac"))
    LDW, LW3, LD2, LDX, LDC = (kb.sem(x) for x in ("p10ldw", "p10lw3", "p10ld2", "p10ldx", "p10ldc"))
    LW1P = [kb.sem("p10lw1a"), kb.sem("p10lw1b")]
    LW3P = [kb.sem("p10lw3a"), kb.sem("p10lw3b")]
    STO = kb.sem("p10sto")
    ctx = stack.enter_context
    w1g_sb = ctx(nc.sbuf_tensor("p10_w1g", [128, 2, 16, 256], BF16))
    w3g_sb = ctx(nc.sbuf_tensor("p10_w3g", [128, 2, 16, 256], BF16))
    w2x_sb = ctx(nc.sbuf_tensor("p10_w2x", [128, 2, 64, 128], BF16))
    hT_sb = ctx(nc.sbuf_tensor("p10_hT", [128, 64, TT], BF16))
    x3t_sb = ctx(nc.sbuf_tensor("p10_x3t", [128, 16, TT], BF16))
    sg_f = ctx(nc.sbuf_tensor("p10_sg", [128, 1, TT], F32))
    r_f = ctx(nc.sbuf_tensor("p10_rf", [128, 16, TT], F32))
    rb_b = ctx(nc.sbuf_tensor("p10_rb", [128, 2, TT], BF16))
    sq_b = ctx(nc.sbuf_tensor("p10_sq", [128, 2, TT], BF16))
    st_f = ctx(nc.sbuf_tensor("p10_st", [1, 4, TT], F32))
    hl_b = ctx(nc.sbuf_tensor("p10_hl", [1, 4, TT], BF16))
    mbc_sb = ctx(nc.sbuf_tensor("p10_mbc", [128, TT], F32))
    rbc_sb = ctx(nc.sbuf_tensor("p10_rbc", [128, TT], F32))
    t1_f = ctx(nc.sbuf_tensor("p10_t1", [128, TT], F32))
    o_f = ctx(nc.sbuf_tensor("p10_of", [128, 1, TT], F32))
    ones_sb = ctx(nc.sbuf_tensor("p10_ones", [128, 1], BF16))
    onesr_sb = ctx(nc.sbuf_tensor("p10_onesr", [1, 128], BF16))
    eps_sb = ctx(nc.sbuf_tensor("p10_eps", [1, 1], F32))
    PA = ctx(nc.psum_tensor("p10_PA", [128, TT], F32))
    PB = ctx(nc.psum_tensor("p10_PB", [128, TT], F32))
    PC = ctx(nc.psum_tensor("p10_PC", [128, TT], F32))
    PD = ctx(nc.psum_tensor("p10_PD", [128, TT], F32))
    PM = ctx(nc.psum_tensor("p10_PM", [1, TT], F32))
    PS = ctx(nc.psum_tensor("p10_PS", [1, TT], F32))
    PBC = ctx(nc.psum_tensor("p10_PBC", [128, TT], F32))

    for dst, srcd in ((ones_sb, kb.ones_bf), (onesr_sb, kb.ones_row), (eps_sb, kb.epsc)):
        sc.op("gpsimd", lambda g, dst=dst, srcd=srcd: g.dma_start(out=dst[:], in_=srcd),
              inc=LDC, n=16)
    t_w = Tok(LDC, LDC.v)

    prev = {"p1A": [None, None], "p1B": [None, None], "wh_used": [None, None],
            "x3_used": None,
            "hT_used": None, "sg": [None, None], "w2_used": [None, None],
            "radd": [None, None], "apply": None, "pm_free": None,
            "stat": [None, None], "ost": None, "out_ev": [None, None]}
    for tt in range(NTO):
        cs = (tt * TT, (tt + 1) * TT)
        t_x3 = sc.op("gpsimd",
                     lambda g, cs=cs: g.dma_start(out=x3t_sb[:], in_=chunked(kb.d["X3"], cs)),
                     wait=[prev["x3_used"]], inc=LDX, n=16)

        t_hm = None
        for hg in range(16):
            for hf in range(2):
                ph = (hg * 2 + hf) % 2
                c0 = hg * 512 + hf * 256
                t_w1 = sc.op("gpsimd",
                             lambda g, c0=c0, ph=ph: g.dma_start(
                                 out=w1g_sb[:, ph], in_=chunked(kb.d["W1B"], (c0, c0 + 256))),
                             wait=[prev["wh_used"][ph]], inc=LW1P[ph], n=16)
                t_w3 = sc.op("gpsimd",
                             lambda g, c0=c0, ph=ph: g.dma_start(
                                 out=w3g_sb[:, ph], in_=chunked(kb.d["W3B"], (c0, c0 + 256))),
                             wait=[prev["wh_used"][ph]], inc=LW3P[ph], n=16)
                for k in range(2):
                    hc8 = hf * 2 + k
                    hc = hg * 4 + hc8
                    par = hc % 2
                    G = PA if par == 0 else PC
                    U = PB if par == 0 else PD

                    def gmm(t, k=k, ph=ph, G=G):
                        for fc in range(16):
                            r = t.matmul(G[:], w1g_sb[:, ph, fc, k * 128:(k + 1) * 128],
                                         x3t_sb[:, fc], start=(fc == 0), stop=(fc == 15))
                        return r

                    t_gm = sc.op("tensor", gmm, wait=[t_x3, t_w1, prev["p1A"][par]], inc=MM)

                    def umm(t, k=k, ph=ph, U=U):
                        for fc in range(16):
                            r = t.matmul(U[:], w3g_sb[:, ph, fc, k * 128:(k + 1) * 128],
                                         x3t_sb[:, fc], start=(fc == 0), stop=(fc == 15))
                        return r

                    t_um = sc.op("tensor", umm, wait=[t_x3, t_w3, prev["p1B"][par]], inc=MM)
                    t_si = sc.op("scalar",
                                 lambda s, par=par, G=G: s.activation(out=sg_f[:, 0], in_=G[:],
                                                                      func=AF.Silu),
                                 wait=[t_gm, prev["sg"][0]], inc=AC)
                    prev["p1A"][par] = t_si
                    t_hm = sc.op("vector",
                                 lambda v, hc=hc, U=U: v.tensor_tensor(
                                     out=hT_sb[:, hc], in0=sg_f[:, 0], in1=U[:], op=ALU.mult),
                                 wait=[t_si, t_um, prev["hT_used"] if hc <= 1 else None], inc=EV)
                    prev["p1B"][par] = t_hm
                    prev["sg"][0] = t_hm
                prev["wh_used"][ph] = t_um
        t_p1 = prev["p1B"][1]

        t_radds = []
        t_stat = None
        for oc in range(16):
            par = oc % 2
            tgt = PA if par == 0 else PB
            t_w2 = sc.op("gpsimd",
                         lambda g, oc=oc, par=par: g.dma_start(
                             out=w2x_sb[:, par], in_=kb.d["W2X"][oc].rearrange("a p o -> p a o")),
                         wait=[prev["w2_used"][par], t_p1 if oc <= 1 else None],
                         inc=(LD2 if par == 0 else LW3), n=16)

            def omm(t, tgt=tgt, par=par):
                for hc in range(64):
                    r = t.matmul(tgt[:], w2x_sb[:, par, hc, :], hT_sb[:, hc],
                                 start=(hc == 0), stop=(hc == 63))
                return r

            t_om = sc.op("tensor", omm,
                         wait=[t_w2, t_p1, prev["radd"][par]], inc=MM)
            prev["w2_used"][par] = t_om

            t_ra = sc.op("vector",
                         lambda v, oc=oc, tgt=tgt: v.tensor_tensor(
                             out=r_f[:, oc], in0=tgt[:], in1=x3t_sb[:, oc], op=ALU.add),
                         wait=[t_om, prev["apply"]], inc=EV)
            prev["radd"][par] = t_ra
            t_radds.append(t_ra)
            t_rb = sc.op("vector",
                         lambda v, oc=oc, par=par: v.tensor_copy(out=rb_b[:, par], in_=r_f[:, oc]),
                         wait=[t_ra, prev["stat"][par]], inc=EV)
            t_sq = sc.op("scalar",
                         lambda s, oc=oc, par=par: s.activation(out=sq_b[:, par], in_=r_f[:, oc],
                                                                func=AF.Square),
                         wait=[t_ra, prev["stat"][par]], inc=AC)

            def smm(t, oc=oc, par=par):
                t.matmul(PM[:], ones_sb[:], rb_b[:, par], start=(oc == 0), stop=(oc == 15))
                return t.matmul(PS[:], ones_sb[:], sq_b[:, par], start=(oc == 0), stop=(oc == 15))

            t_stat = sc.op("tensor", smm,
                           wait=[t_rb, t_sq, t_w, prev["pm_free"] if oc == 0 else None], inc=MM)
            prev["stat"][par] = t_stat
        prev["x3_used"] = t_radds[-1]
        prev["hT_used"] = prev["w2_used"][1]
        prev["p1A"] = [prev["radd"][0], prev["p1A"][1]]
        prev["p1B"] = [prev["radd"][1], prev["p1B"][1]]

        t_bc, t_statsdone = ln_tail(
            sc, kb, nfeat=D, PMp=PM, PSp=PS, st_f=st_f, hl_b=hl_b,
            onesr_sb=onesr_sb, PBC=PBC, mbc_sb=mbc_sb, rbc_sb=rbc_sb,
            stat_tok=t_stat, bc_war_tok=prev["apply"], MM=MM, EV=EV, AC=AC,
            eps_ap=eps_sb[:, :])
        prev["pm_free"] = t_statsdone

        for oc in range(16):
            def apply1(v, oc=oc):
                v.tensor_tensor(out=t1_f[:], in0=r_f[:, oc], in1=mbc_sb[:], op=ALU.subtract)
                v.drain()
                r = v.tensor_tensor(out=o_f[:, 0], in0=t1_f[:], in1=rbc_sb[:], op=ALU.mult)
                v.drain()
                return r

            t_ap = sc.op("vector", apply1,
                         wait=[t_bc, t_radds[oc], prev["out_ev"][0]], inc=EV)
            prev["apply"] = t_ap
            t_os = sc.op("sync",
                         lambda e, oc=oc, cs=cs: e.dma_start(
                             out=kb.d["OUT"][oc * 128:(oc + 1) * 128, cs[0]:cs[1]],
                             in_=o_f[:, 0]),
                         wait=[t_ap], inc=STO, n=16)
            prev["out_ev"][0] = t_os

    sc.op("sync", None, wait=[Tok(STO, STO.v)])
    with nc.Block() as blk:
        sc.emit(blk)



PHASE_FNS = {"p0": p0, "p2": p2, "p3": p3, "p4": p4, "p4b": p4b,
             "p5": p5, "p7": p7, "p8": p8, "p9": p9, "p10": p10}

from contextlib import ExitStack


def build_program(phases):
    kb = KB(phases)
    skip = set()
    for ph in phases:
        if ph in skip:
            continue
        if ph == "p5" and "p9" in phases:
            with ExitStack() as stack:
                sc = Sched(kb)
                p5(kb, stack, sc)
                sc9 = Sched(kb)
                p9(kb, stack, sc9, precast=not ("p2" in phases))
                sc.merge(sc9)
                with kb.nc.Block() as blk:
                    sc.emit(blk)
            skip.add("p9")
            continue
        if ph == "p2" and "p9" in phases and "p5" in phases:
            with ExitStack() as stack:
                p2(kb, stack, precast=True)
            continue
        with ExitStack() as stack:
            PHASE_FNS[ph](kb, stack)
    return kb



def _cont(a):
    return np.ascontiguousarray(a)


def host_prep(inputs):
    x = np.asarray(inputs["L0_attn_hidden_states"])
    f32 = np.float32
    shared = {
        "qwT": None,
        "qwTpe": None,
        "kvdT": _cont(np.asarray(inputs["L0_attn_kv_down_w"]).T),
        "lng": _cont(np.asarray(inputs["L0_attn_ln_g"]).reshape(4, 128).T.astype(f32)),
        "lnb": _cont(np.asarray(inputs["L0_attn_ln_b"]).reshape(4, 128).T.astype(f32)),
        "kvuT": _cont(np.asarray(inputs["L0_attn_kv_up_w"]).T),
        "owT": _cont(np.asarray(inputs["L0_attn_out_w"]).T),
        "gT": _cont(np.asarray(inputs["L0_ffn_gate"]).T),
        "ebc": _cont(np.broadcast_to(np.asarray(inputs["L0_ffn_expert_bias"]), (128, 8)).astype(f32)),
        "w1T": _cont(np.transpose(np.asarray(inputs["L0_ffn_w1"]), (0, 2, 1))),
        "w2T": _cont(np.transpose(np.asarray(inputs["L0_ffn_w2"]), (0, 2, 1))),
        "w3T": _cont(np.transpose(np.asarray(inputs["L0_ffn_w3"]), (0, 2, 1))),
        "sw1T": _cont(np.asarray(inputs["L0_ffn_shared_w1"]).T),
        "sw2T": _cont(np.asarray(inputs["L0_ffn_shared_w2"]).T),
        "sw3T": _cont(np.asarray(inputs["L0_ffn_shared_w3"]).T),
        "wqT": None, "wkT": None,
        "wvT": _cont(np.asarray(inputs["L1_attn_Wv"]).T),
        "woT": _cont(np.asarray(inputs["L1_attn_Wo"]).T),
        "W1T": _cont(np.asarray(inputs["L1_ffn_W1"]).T),
        "W3T": _cont(np.asarray(inputs["L1_ffn_W3"]).T),
        "W2T": _cont(np.asarray(inputs["L1_ffn_W2"]).T),
    }
    def pair_cols(nheads):
        idx = []
        for p in range(nheads // 2):
            h0, h1 = 2 * p, 2 * p + 1
            idx += [np.arange(h0 * HD, h0 * HD + 64), np.arange(h1 * HD, h1 * HD + 64),
                    np.arange(h0 * HD + 64, (h0 + 1) * HD), np.arange(h1 * HD + 64, (h1 + 1) * HD)]
        return np.concatenate(idx)

    shared["wqT"] = _cont(np.asarray(inputs["L1_attn_Wq"]).T[:, pair_cols(NH)])
    shared["wkT"] = _cont(np.asarray(inputs["L1_attn_Wk"]).T[:, pair_cols(NKV)])
    qwT_full = np.asarray(inputs["L0_attn_q_w"]).T
    idx_nope = np.concatenate([np.arange(h * QHD, h * QHD + DQK) for h in range(NH)])
    shared["qwT"] = _cont(qwT_full[:, idx_nope])
    idx_pe = []
    for p in range(NH // 2):
        h0, h1 = 2 * p, 2 * p + 1
        idx_pe += [np.arange(h0 * QHD + 128, h0 * QHD + 160),
                   np.arange(h1 * QHD + 128, h1 * QHD + 160),
                   np.arange(h0 * QHD + 160, h0 * QHD + 192),
                   np.arange(h1 * QHD + 160, h1 * QHD + 192)]
    shared["qwTpe"] = _cont(qwT_full[:, np.concatenate(idx_pe)])
    inv0 = (1.0 / (10000.0 ** (np.arange(0, DPE, 2, dtype=f32) / DPE))).astype(f32)
    inv1 = (1.0 / (10000.0 ** (np.arange(0, HD, 2, dtype=f32) / HD))).astype(f32)
    percore = []
    for c in range(8):
        b, h = c // 2, c % 2
        perm = np.concatenate([np.arange(2048) + h * 2048, np.arange(2048) + (1 - h) * 2048])
        pos = perm.astype(f32)
        fr0 = pos[:, None] * inv0[None, :]
        c0, s0 = np.cos(fr0).astype(f32), np.sin(fr0).astype(f32)
        fr1 = pos[:, None] * inv1[None, :]
        emb = np.concatenate([fr1, fr1], axis=1)
        c1, s1 = np.cos(emb).astype(f32), np.sin(emb).astype(f32)
        d = {
            "xT": _cont(x[b].T[:, perm].astype(f32)),
            "r0k": _cont(np.concatenate([c0.T, s0.T], axis=1)),
            "r0q": _cont(np.concatenate([
                np.concatenate([c0.T, c0.T], axis=0) * SC0,
                np.concatenate([s0.T, s0.T], axis=0) * SC0], axis=1)),
            "r1k": _cont(np.concatenate([
                np.concatenate([c1.T[0:64], c1.T[0:64]], axis=0),
                np.concatenate([s1.T[0:64], s1.T[0:64]], axis=0),
                np.concatenate([c1.T[64:128], c1.T[64:128]], axis=0),
                np.concatenate([s1.T[64:128], s1.T[64:128]], axis=0)], axis=1)),
            "r1q": _cont(np.concatenate([
                np.concatenate([c1.T[0:64, 0:2048], c1.T[0:64, 0:2048]], axis=0),
                np.concatenate([s1.T[0:64, 0:2048], s1.T[0:64, 0:2048]], axis=0),
                np.concatenate([c1.T[64:128, 0:2048], c1.T[64:128, 0:2048]], axis=0),
                np.concatenate([s1.T[64:128, 0:2048], s1.T[64:128, 0:2048]], axis=0)], axis=1) * SC1),
        }
        percore.append(d)
    return shared, percore


LAUNCH_PLAN = [list(ALL_PHASES)]


def run_launches(inputs, launch_plan=None, extra_outputs=()):
    launch_plan = launch_plan or LAUNCH_PLAN
    shared, percore = host_prep(inputs)
    carry = [dict() for _ in range(8)]
    for phases in launch_plan:
        kb = build_program(phases)
        in_maps = []
        for c in range(8):
            m = {}
            for name in kb.in_names:
                if name in shared:
                    m[name] = shared[name]
                elif name in percore[c]:
                    m[name] = percore[c][name]
                else:
                    m[name] = carry[c][name]
            in_maps.append(m)
        res = run_bass_kernel_spmd(kb.nc, in_maps, core_ids=list(range(8)))
        for c in range(8):
            for name in kb.out_names:
                carry[c][name] = res.results[c][name]
    return carry


def kernel(**inputs):
    carry = run_launches(inputs)
    out = np.zeros((B, S, D), np.float32)
    for c in range(8):
        b, h = c // 2, c % 2
        out[b, h * 2048:(h + 1) * 2048, :] = carry[c]["OUT"].T
    return out


# version 13 (final)
# speedup vs baseline: 1.0565x; 1.0060x over previous
import numpy as np
import ml_dtypes
import concourse.bass as bass
import concourse.mybir as mybir
from concourse.bass_utils import run_bass_kernel_spmd

F32 = mybir.dt.float32
BF16 = mybir.dt.bfloat16
AF = mybir.ActivationFunctionType
ALU = mybir.AluOpType
AX = mybir.AxisListType
BF = ml_dtypes.bfloat16

B, S, D = 4, 4096, 2048
NH, DQK, DPE, DV = 16, 128, 64, 128
QHD = DQK + DPE
LATD = 512
NE, HID = 8, 1024
NKV, HD = 4, 128
FF = 8192
TT = 512
NTT = S // TT
NTO = 2048 // TT
EPS = 1e-5
SC0 = float(1.0 / np.sqrt(QHD))
SC1 = float(1.0 / np.sqrt(HD))
NEGBIG = -1.0e30

ALL_PHASES = ("p0", "p2", "p3", "p4", "p4b", "p5", "p9", "p7", "p8", "p10")

DRAM_SPECS = {
    "Q0":  ([NH * QHD, S], BF16, "p0", ("p2",)),
    "LAT": ([LATD + DPE, S], BF16, "p0", ("p2",)),
    "QS0": ([1, 128], F32, "p0", ("p2",)),
    "CT0": ([D, S], BF16, "p2", ("p3",)),
    "X1":  ([D, S], BF16, "p3", ("p4", "p4b")),
    "WG8": ([8, S], BF16, "p3", ("p4",)),
    "ACC": ([D, S], F32, "p4", ("p4b",)),
    "X2":  ([D, S], BF16, "p4b", ("p5", "p8")),
    "Q1":  ([D, 2048], BF16, "p5", ("p7",)),
    "K1":  ([NKV * HD, S], BF16, "p5", ("p7",)),
    "V1":  ([S, NKV * HD], BF16, "p5", ("p7",)),
    "QS1": ([1, 96], F32, "p5", ("p7",)),
    "CT1": ([D, 2048], BF16, "p7", ("p8",)),
    "X3":  ([D, 2048], BF16, "p8", ("p10",)),
    "FS":  ([FF, D], BF16, "p9", ()),
    "W1B": ([D, FF], BF16, "p9", ("p10",)),
    "W3B": ([D, FF], BF16, "p9", ("p10",)),
    "W2X": ([16, 64, 128, 128], BF16, "p9", ("p10",)),
    "OUT": ([D, 2048], F32, "p10", ()),
}

IN_SPECS = {
    "xT":   ([D, S], F32, ("p0", "p3")),
    "qwT":  ([D, NH * DQK], F32, ("p0",)),
    "qwTpe": ([D, NH * DPE], F32, ("p0",)),
    "kvdT": ([D, LATD + DPE], F32, ("p0",)),
    "lng":  ([128, 4], F32, ("p0",)),
    "lnb":  ([128, 4], F32, ("p0",)),
    "r0k":  ([32, 2 * S], F32, ("p0",)),
    "r0q":  ([64, 2 * S], F32, ("p0",)),
    "kvuT": ([LATD, NH * (DQK + DV)], F32, ("p2",)),
    "owT":  ([D, D], F32, ("p3",)),
    "gT":   ([D, 8], F32, ("p3",)),
    "ebc":  ([128, 8], F32, ("p3",)),
    "w1T":  ([NE, D, HID], F32, ("p4",)),
    "w2T":  ([NE, HID, D], F32, ("p4",)),
    "w3T":  ([NE, D, HID], F32, ("p4",)),
    "sw1T": ([D, HID], F32, ("p4",)),
    "sw2T": ([HID, D], F32, ("p4",)),
    "sw3T": ([D, HID], F32, ("p4",)),
    "wqT":  ([D, D], F32, ("p5",)),
    "wkT":  ([D, NKV * HD], F32, ("p5",)),
    "wvT":  ([D, NKV * HD], F32, ("p5",)),
    "r1k":  ([128, 4 * S], F32, ("p5",)),
    "r1q":  ([128, 4 * 2048], F32, ("p5",)),
    "woT":  ([D, D], F32, ("p8",)),
    "W2T":  ([FF, D], F32, ("p9",)),
    "W1T":  ([D, FF], F32, ("p9",)),
    "W3T":  ([D, FF], F32, ("p9",)),
}


def _hadamard(n):
    h = np.array([[1.0]], dtype=np.float32)
    while h.shape[0] < n:
        h = np.block([[h, h], [h, -h]])
    return h


class Sem:
    __slots__ = ("h", "v")

    def __init__(self, nc, name):
        self.h = nc.alloc_semaphore(name=name)
        self.v = 0


class Tok:
    __slots__ = ("sem", "val")

    def __init__(self, sem, val):
        self.sem = sem
        self.val = val


class Sched:

    ENGINES = ("gpsimd", "sync", "tensor", "scalar", "vector")

    def __init__(self, kb):
        self.kb = kb
        self.qs = {e: [] for e in self.ENGINES}
        self.seq = 0

    def op(self, engine, fn, wait=(), inc=None, n=1, ni=1):
        tok = None
        if inc is not None:
            inc.v += n * ni
            tok = Tok(inc, inc.v)
        waits = [(w.sem, w.val) for w in wait if w is not None]
        self.qs[engine].append((fn, waits, inc, n, ni, self.seq))
        self.seq += 1
        return tok

    def merge(self, other):
        ta, tb = max(self.seq, 1), max(other.seq, 1)
        for e in self.ENGINES:
            merged = sorted(
                [(op[-1] / ta, 0, k) for k, op in enumerate(self.qs[e])]
                + [(op[-1] / tb, 1, k) for k, op in enumerate(other.qs[e])])
            self.qs[e] = [(self.qs[e] if ph == 0 else other.qs[e])[k]
                          for _, ph, k in merged]

    def emit(self, blk):
        for name in self.ENGINES:
            ops = self.qs[name]
            if not ops:
                continue
            dec = getattr(blk, name)

            def run(eng, ops=ops):
                seen = {}
                for fn, waits, inc, n, ni, _ in ops:
                    for sem, val in waits:
                        if seen.get(id(sem), -1) >= val:
                            continue
                        eng.wait_ge(sem.h, val)
                        seen[id(sem)] = val
                    inst = fn(eng) if fn is not None else None
                    if inc is not None:
                        insts = inst if isinstance(inst, (list, tuple)) else [inst]
                        assert len(insts) == ni and all(i is not None for i in insts)
                        for i in insts:
                            i.then_inc(inc.h, n)

            dec(run)


class KB:

    def __init__(self, phases):
        self.phases = list(phases)
        nc = bass.Bass("TRN2", target_bir_lowering=False)
        self.nc = nc
        pset = set(self.phases)
        self.d = {}
        self.in_names = []
        self.out_names = []
        for name, (shape, dt, consumers) in IN_SPECS.items():
            if pset & set(consumers):
                self.d[name] = nc.dram_tensor(name, shape, dt, kind="ExternalInput").ap()
                self.in_names.append(name)
        for name, (shape, dt, prod, cons) in DRAM_SPECS.items():
            inside_p = prod in pset
            needed = pset & set(cons)
            later = [c for c in cons if c not in pset]
            if inside_p and (name == "OUT" or later):
                self.d[name] = nc.dram_tensor(name, shape, dt, kind="ExternalOutput").ap()
                self.out_names.append(name)
            elif inside_p:
                self.d[name] = nc.dram_tensor(name, shape, dt, kind="Internal").ap()
            elif needed:
                self.d[name] = nc.dram_tensor(name, shape, dt, kind="ExternalInput").ap()
                self.in_names.append(name)
        self.ones_bf = nc.inline_tensor(np.ones((128, 1), dtype=BF), name="c_ones").ap()
        self.ones_row = nc.inline_tensor(np.ones((1, 128), dtype=BF), name="c_onesr").ap()
        sel = np.zeros((8, 8 * 128), dtype=BF)
        for e in range(8):
            sel[e, e * 128:(e + 1) * 128] = 1
        self.sel8 = nc.inline_tensor(sel, name="c_sel8").ap()
        self.ident = nc.inline_tensor(np.eye(128, dtype=BF), name="c_ident").ap()
        self.h64 = nc.inline_tensor(_hadamard(64).astype(BF), name="c_h64").ap()
        self.epsc = nc.inline_tensor(np.full((1, 1), EPS, np.float32), name="c_eps").ap()
        self.h128 = nc.inline_tensor(_hadamard(128).astype(BF), name="c_h128").ap()
        self._sems = {}

    def sem(self, name):
        if name not in self._sems:
            self._sems[name] = Sem(self.nc, name)
        return self._sems[name]


def chunked(dram_ap, cols=None):
    ap = dram_ap if cols is None else dram_ap[:, cols[0]:cols[1]]
    return ap.rearrange("(c p) o -> p c o", p=128)



def emit_hi_lo(v, src_f32, hi_bf, lo_bf):
    v.tensor_copy(out=hi_bf, in_=src_f32)
    v.drain()
    return v.tensor_tensor(out=lo_bf, in0=src_f32, in1=hi_bf, op=ALU.subtract)


def ln_tail(sc, kb, *, nfeat, PMp, PSp, st_f, hl_b, onesr_sb, PBC, mbc_sb, rbc_sb,
            stat_tok, bc_war_tok, MM, EV, AC, eps_ap):
    inv = 1.0 / float(nfeat)

    def stats(v):
        v.tensor_scalar(out=st_f[:, 0], in0=PMp[:], scalar1=inv, scalar2=None, op0=ALU.mult)
        v.tensor_scalar(out=st_f[:, 1], in0=PSp[:], scalar1=inv, scalar2=None, op0=ALU.mult)
        v.drain()
        v.tensor_tensor(out=st_f[:, 2], in0=st_f[:, 0], in1=st_f[:, 0], op=ALU.mult)
        v.drain()
        return v.tensor_tensor(out=st_f[:, 3], in0=st_f[:, 1], in1=st_f[:, 2], op=ALU.subtract)

    t_stats = sc.op("vector", stats, wait=[stat_tok], inc=EV)
    t_sqrt = sc.op("scalar",
                   lambda s: s.activation(out=st_f[:, 1], in_=st_f[:, 3], func=AF.Sqrt, bias=eps_ap),
                   wait=[t_stats], inc=AC)
    t_rec = sc.op("vector", lambda v: v.reciprocal(out=st_f[:, 2], in_=st_f[:, 1]),
                  wait=[t_sqrt], inc=EV)

    def hilo(v):
        emit_hi_lo(v, st_f[:, 0], hl_b[:, 0], hl_b[:, 1])
        v.drain()
        return emit_hi_lo(v, st_f[:, 2], hl_b[:, 2], hl_b[:, 3])

    t_hl = sc.op("vector", hilo, wait=[t_rec], inc=EV)

    def bc1(t):
        t.matmul(PBC[:], onesr_sb[:], hl_b[:, 0], start=True, stop=False)
        return t.matmul(PBC[:], onesr_sb[:], hl_b[:, 1], start=False, stop=True)

    t_b1 = sc.op("tensor", bc1, wait=[t_hl, bc_war_tok], inc=MM)
    t_e1 = sc.op("vector", lambda v: v.tensor_copy(out=mbc_sb[:], in_=PBC[:]),
                 wait=[t_b1], inc=EV)

    def bc2(t):
        t.matmul(PBC[:], onesr_sb[:], hl_b[:, 2], start=True, stop=False)
        return t.matmul(PBC[:], onesr_sb[:], hl_b[:, 3], start=False, stop=True)

    t_b2 = sc.op("tensor", bc2, wait=[t_hl, t_e1], inc=MM)
    t_e2 = sc.op("vector", lambda v: v.tensor_copy(out=rbc_sb[:], in_=PBC[:]),
                 wait=[t_b2], inc=EV)
    return t_e2, t_stats



def p0(kb, stack):
    nc = kb.nc
    sc = Sched(kb)
    MM, EV, AC = (kb.sem(x) for x in ("p0mm", "p0ev", "p0ac"))
    LDW, LDX, LDT = (kb.sem(x) for x in ("p0ldw", "p0ldx", "p0ldt"))
    STL, STQ, STS = (kb.sem(x) for x in ("p0stl", "p0stq", "p0sts"))

    ctx = stack.enter_context
    qw_sb = ctx(nc.sbuf_tensor("p0_qw", [128, 16, NH * DQK], BF16))
    qwp_sb = ctx(nc.sbuf_tensor("p0_qwp", [128, 16, NH * DPE], BF16))
    kvd_sb = ctx(nc.sbuf_tensor("p0_kvd", [128, 16, LATD + DPE], BF16))
    lng_sb = ctx(nc.sbuf_tensor("p0_lng", [128, 4], F32))
    lnb_sb = ctx(nc.sbuf_tensor("p0_lnb", [128, 4], F32))
    ones_sb = ctx(nc.sbuf_tensor("p0_ones", [128, 1], BF16))
    eps_sb = ctx(nc.sbuf_tensor("p0_eps", [1, 1], F32))
    onesr_sb = ctx(nc.sbuf_tensor("p0_onesr", [1, 128], BF16))
    rtk_sb = ctx(nc.sbuf_tensor("p0_rtk", [32, 2, TT], BF16))
    rtq_sb = ctx(nc.sbuf_tensor("p0_rtq", [64, 2, TT], BF16))
    xt_sb = ctx(nc.sbuf_tensor("p0_xt", [128, 16, TT], BF16))
    lat_f = ctx(nc.sbuf_tensor("p0_latf", [128, 4, TT], F32))
    lat_b = ctx(nc.sbuf_tensor("p0_latb", [128, 4, TT], BF16))
    sq_b = ctx(nc.sbuf_tensor("p0_sqb", [128, 4, TT], BF16))
    st_f = ctx(nc.sbuf_tensor("p0_stf", [1, 4, TT], F32))
    hl_b = ctx(nc.sbuf_tensor("p0_hlb", [1, 4, TT], BF16))
    mbc_sb = ctx(nc.sbuf_tensor("p0_mbc", [128, TT], F32))
    rbc_sb = ctx(nc.sbuf_tensor("p0_rbc", [128, TT], F32))
    t1_f = ctx(nc.sbuf_tensor("p0_t1f", [128, TT], F32))
    latout = ctx(nc.sbuf_tensor("p0_latout", [128, 4, TT], BF16))
    peout = ctx(nc.sbuf_tensor("p0_peout", [32, 2, TT], BF16))
    pesc = ctx(nc.sbuf_tensor("p0_pesc", [64, 2, TT], F32))
    qn_b = ctx(nc.sbuf_tensor("p0_qnb", [128, 2, TT], BF16))
    qp_b = ctx(nc.sbuf_tensor("p0_qpb", [64, 2, TT], BF16))
    qsq_b = ctx(nc.sbuf_tensor("p0_qsqb", [128, 2, TT], BF16))
    qpsq_b = ctx(nc.sbuf_tensor("p0_qpsqb", [64, 2, TT], BF16))
    qs_f = ctx(nc.sbuf_tensor("p0_qsf", [1, 128], F32))
    PA = ctx(nc.psum_tensor("p0_PA", [128, TT], F32))
    PB = ctx(nc.psum_tensor("p0_PB", [128, TT], F32))
    PPA = ctx(nc.psum_tensor("p0_PPA", [64, TT], F32))
    PPB = ctx(nc.psum_tensor("p0_PPB", [64, TT], F32))
    PM = ctx(nc.psum_tensor("p0_PM", [1, TT], F32))
    PS = ctx(nc.psum_tensor("p0_PS", [1, TT], F32))
    PQ2 = ctx(nc.psum_tensor("p0_PQ2", [1, TT], F32))
    PBC = ctx(nc.psum_tensor("p0_PBC", [128, TT], F32))

    for dst, srcd in ((qw_sb, chunked(kb.d["qwT"])), (qwp_sb, chunked(kb.d["qwTpe"])),
                      (kvd_sb, chunked(kb.d["kvdT"])),
                      (lng_sb, kb.d["lng"]), (lnb_sb, kb.d["lnb"]),
                      (ones_sb, kb.ones_bf), (onesr_sb, kb.ones_row),
                      (eps_sb, kb.epsc)):
        sc.op("gpsimd", lambda g, dst=dst, srcd=srcd: g.dma_start(out=dst[:], in_=srcd),
              inc=LDW, n=16)
    t_w = Tok(LDW, LDW.v)

    prev = {
        "evA": None, "evB": None, "evPA": None, "evPB": None,
        "bc_done": None, "apply": None, "qnorm": None,
        "q_store": None, "lat_store": None, "pm_free": None,
        "tab_used": None, "tile_done": None,
    }

    for tt in range(NTT):
        cs = (tt * TT, (tt + 1) * TT)
        x_tok = sc.op("gpsimd",
                      lambda g, cs=cs: g.dma_start(out=xt_sb[:], in_=chunked(kb.d["xT"], cs)),
                      wait=[t_w, prev["tile_done"]], inc=LDX, n=16)
        t_tab = sc.op("gpsimd",
                      lambda g, cs=cs: [
                          g.dma_start(out=rtk_sb[:, 0], in_=kb.d["r0k"][:, cs[0]:cs[1]]),
                          g.dma_start(out=rtk_sb[:, 1], in_=kb.d["r0k"][:, S + cs[0]:S + cs[1]]),
                          g.dma_start(out=rtq_sb[:, 0], in_=kb.d["r0q"][:, cs[0]:cs[1]]),
                          g.dma_start(out=rtq_sb[:, 1], in_=kb.d["r0q"][:, S + cs[0]:S + cs[1]])],
                      wait=[prev["tab_used"], prev["tile_done"]], inc=LDT, n=16, ni=4)
        xb = xt_sb

        ev_lat = [None] * 4
        for oc in range(4):
            tgt = PA if oc % 2 == 0 else PB
            warp = prev["evA"] if oc % 2 == 0 else prev["evB"]

            def mms(t, tgt=tgt, oc=oc):
                for fc in range(16):
                    r = t.matmul(tgt[:], kvd_sb[:, fc, oc * 128:(oc + 1) * 128],
                                 xb[:, fc], start=(fc == 0), stop=(fc == 15))
                return r

            t_mm = sc.op("tensor", mms, wait=[x_tok, warp], inc=MM)

            def evac(v, oc=oc, tgt=tgt):
                v.tensor_copy(out=lat_f[:, oc], in_=tgt[:])
                return v.tensor_copy(out=lat_b[:, oc], in_=tgt[:])

            tok = sc.op("vector", evac, wait=[t_mm, prev["apply"]], inc=EV)
            ev_lat[oc] = tok
            if oc % 2 == 0:
                prev["evA"] = tok
            else:
                prev["evB"] = tok

        def pemm(t):
            for fc in range(16):
                t.matmul(PPA[0:32, :], kvd_sb[:, fc, 512:544], xb[:, fc],
                         start=(fc == 0), stop=(fc == 15))
            for fc in range(16):
                r = t.matmul(PPB[0:32, :], kvd_sb[:, fc, 544:576], xb[:, fc],
                             start=(fc == 0), stop=(fc == 15))
            return r

        t_pemm = sc.op("tensor", pemm, wait=[x_tok, prev["evPA"], prev["evPB"]], inc=MM)

        def ropepe(v):
            a = PPA[0:32, :]
            b = PPB[0:32, :]
            c = rtk_sb[:, 0]
            s_ = rtk_sb[:, 1]
            v.tensor_tensor(out=pesc[0:32, 0], in0=a, in1=c, op=ALU.mult)
            v.tensor_tensor(out=pesc[0:32, 1], in0=b, in1=s_, op=ALU.mult)
            v.drain()
            v.tensor_tensor(out=peout[:, 0], in0=pesc[0:32, 0], in1=pesc[0:32, 1],
                            op=ALU.subtract)
            v.drain()
            v.tensor_tensor(out=pesc[0:32, 0], in0=a, in1=s_, op=ALU.mult)
            v.tensor_tensor(out=pesc[0:32, 1], in0=b, in1=c, op=ALU.mult)
            v.drain()
            return v.tensor_tensor(out=peout[:, 1], in0=pesc[0:32, 1], in1=pesc[0:32, 0],
                                   op=ALU.add)

        t_rope = sc.op("vector", ropepe, wait=[t_pemm, t_tab, prev["lat_store"]], inc=EV)
        prev["evPA"] = t_rope
        prev["evPB"] = t_rope

        def sqs(s):
            for c in range(4):
                r = s.activation(out=sq_b[:, c], in_=lat_b[:, c], func=AF.Square)
            return r

        t_sq = sc.op("scalar", sqs, wait=ev_lat, inc=AC)

        def statmm(t):
            for c in range(4):
                t.matmul(PM[:], ones_sb[:], lat_b[:, c], start=(c == 0), stop=(c == 3))
            for c in range(4):
                r = t.matmul(PS[:], ones_sb[:], sq_b[:, c], start=(c == 0), stop=(c == 3))
            return r

        t_stat = sc.op("tensor", statmm, wait=ev_lat + [t_sq, prev["pm_free"]], inc=MM)

        t_bc, t_statsdone = ln_tail(
            sc, kb, nfeat=LATD, PMp=PM, PSp=PS, st_f=st_f, hl_b=hl_b,
            onesr_sb=onesr_sb, PBC=PBC, mbc_sb=mbc_sb, rbc_sb=rbc_sb,
            stat_tok=t_stat, bc_war_tok=prev["apply"], MM=MM, EV=EV, AC=AC,
            eps_ap=eps_sb[:, :])
        prev["pm_free"] = t_statsdone
        prev["bc_done"] = t_bc

        def apply(v):
            for oc in range(4):
                v.tensor_tensor(out=t1_f[:], in0=lat_f[:, oc], in1=mbc_sb[:], op=ALU.subtract)
                v.drain()
                v.tensor_tensor(out=t1_f[:], in0=t1_f[:], in1=rbc_sb[:], op=ALU.mult)
                v.drain()
                r = v.tensor_scalar(out=latout[:, oc], in0=t1_f[:],
                                    scalar1=lng_sb[:, oc:oc + 1],
                                    scalar2=lnb_sb[:, oc:oc + 1],
                                    op0=ALU.mult, op1=ALU.add)
                v.drain()
            return r

        t_apply = sc.op("vector", apply, wait=[t_bc, prev["lat_store"]] + ev_lat, inc=EV)
        prev["apply"] = t_apply

        def latst(e, cs=cs):
            return [
                e.dma_start(out=kb.d["LAT"][0:LATD, cs[0]:cs[1]].rearrange("(c p) t -> p c t", p=128),
                            in_=latout[:]),
                e.dma_start(out=kb.d["LAT"][LATD:LATD + 32, cs[0]:cs[1]], in_=peout[:, 0]),
                e.dma_start(out=kb.d["LAT"][LATD + 32:LATD + 64, cs[0]:cs[1]], in_=peout[:, 1])]

        prev["lat_store"] = sc.op("sync", latst, wait=[t_apply, t_rope], inc=STL, n=16, ni=3)

        for p in range(NH // 2):
            h0, h1 = 2 * p, 2 * p + 1

            def qmm(t, p=p, h0=h0, h1=h1):
                for fc in range(16):
                    t.matmul(PA[:], qw_sb[:, fc, h0 * DQK:(h0 + 1) * DQK], xb[:, fc],
                             start=(fc == 0), stop=(fc == 15))
                for fc in range(16):
                    t.matmul(PB[:], qw_sb[:, fc, h1 * DQK:(h1 + 1) * DQK], xb[:, fc],
                             start=(fc == 0), stop=(fc == 15))
                for fc in range(16):
                    t.matmul(PPA[:], qwp_sb[:, fc, p * 128:p * 128 + 64], xb[:, fc],
                             start=(fc == 0), stop=(fc == 15))
                for fc in range(16):
                    r = t.matmul(PPB[:], qwp_sb[:, fc, p * 128 + 64:p * 128 + 128], xb[:, fc],
                                 start=(fc == 0), stop=(fc == 15))
                return r

            t_qmm = sc.op("tensor", qmm,
                          wait=[x_tok, prev["evA"], prev["evB"], prev["evPA"], prev["evPB"],
                                prev["apply"] if p == 0 else None],
                          inc=MM)

            def qevac(s):
                s.activation(out=qn_b[:, 0], in_=PA[:], func=AF.Copy, scale=SC0)
                s.activation(out=qsq_b[:, 0], in_=PA[:], func=AF.Square, scale=SC0)
                s.activation(out=qn_b[:, 1], in_=PB[:], func=AF.Copy, scale=SC0)
                return s.activation(out=qsq_b[:, 1], in_=PB[:], func=AF.Square, scale=SC0)

            t_qe = sc.op("scalar", qevac, wait=[t_qmm, prev["q_store"]], inc=AC)
            prev["evA"] = t_qe
            prev["evB"] = t_qe

            def qrope(v):
                c = rtq_sb[:, 0]
                s_ = rtq_sb[:, 1]
                v.tensor_tensor(out=pesc[:, 0], in0=PPA[:], in1=c, op=ALU.mult)
                v.tensor_tensor(out=pesc[:, 1], in0=PPB[:], in1=s_, op=ALU.mult)
                v.drain()
                v.tensor_tensor(out=qp_b[:, 0], in0=pesc[:, 0], in1=pesc[:, 1],
                                op=ALU.subtract)
                v.drain()
                v.tensor_tensor(out=pesc[:, 0], in0=PPA[:], in1=s_, op=ALU.mult)
                v.tensor_tensor(out=pesc[:, 1], in0=PPB[:], in1=c, op=ALU.mult)
                v.drain()
                v.tensor_tensor(out=qp_b[:, 1], in0=pesc[:, 1], in1=pesc[:, 0], op=ALU.add)
                v.drain()
                v.tensor_tensor(out=qpsq_b[:, 0], in0=qp_b[:, 0], in1=qp_b[:, 0], op=ALU.mult)
                return v.tensor_tensor(out=qpsq_b[:, 1], in0=qp_b[:, 1], in1=qp_b[:, 1],
                                       op=ALU.mult)

            t_qr = sc.op("vector", qrope, wait=[t_qmm, t_tab, prev["q_store"], t_rope], inc=EV)
            prev["evPA"] = t_qr
            prev["evPB"] = t_qr

            for hh, base in ((h0, 0), (h1, 32)):
                def qnmm(t, base=base, hh=hh):
                    t.matmul(PQ2[:], ones_sb[:], qsq_b[:, 0 if base == 0 else 1],
                             start=True, stop=False)
                    t.matmul(PQ2[:], ones_sb[base:base + 32, :], qpsq_b[base:base + 32, 0],
                             start=False, stop=False)
                    return t.matmul(PQ2[:], ones_sb[base:base + 32, :], qpsq_b[base:base + 32, 1],
                                    start=False, stop=True)

                t_qn = sc.op("tensor", qnmm, wait=[t_qe, t_qr, prev["qnorm"]], inc=MM)
                slot = hh * 8 + tt
                t_qx = sc.op("vector",
                             lambda v, slot=slot: v.tensor_reduce(
                                 out=qs_f[:, slot:slot + 1], in_=PQ2[:], axis=AX.X, op=ALU.max),
                             wait=[t_qn], inc=EV)
                prev["qnorm"] = t_qx

            def qst(e, h0=h0, h1=h1, cs=cs):
                return [
                    e.dma_start(out=kb.d["Q0"][h0 * QHD:h0 * QHD + 128, cs[0]:cs[1]], in_=qn_b[:, 0]),
                    e.dma_start(out=kb.d["Q0"][h1 * QHD:h1 * QHD + 128, cs[0]:cs[1]], in_=qn_b[:, 1]),
                    e.dma_start(out=kb.d["Q0"][h0 * QHD + 128:h0 * QHD + 160, cs[0]:cs[1]],
                                in_=qp_b[0:32, 0]),
                    e.dma_start(out=kb.d["Q0"][h0 * QHD + 160:h0 * QHD + 192, cs[0]:cs[1]],
                                in_=qp_b[0:32, 1]),
                    e.dma_start(out=kb.d["Q0"][h1 * QHD + 128:h1 * QHD + 160, cs[0]:cs[1]],
                                in_=qp_b[32:64, 0]),
                    e.dma_start(out=kb.d["Q0"][h1 * QHD + 160:h1 * QHD + 192, cs[0]:cs[1]],
                                in_=qp_b[32:64, 1])]

            t_qs = sc.op("sync", qst, wait=[t_qe, t_qr], inc=STQ, n=16, ni=6)
            prev["q_store"] = t_qs
        prev["tile_done"] = prev["qnorm"]
        prev["tab_used"] = prev["evPA"]

    sc.op("sync", lambda e: e.dma_start(out=kb.d["QS0"], in_=qs_f[:]),
          wait=[prev["qnorm"]], inc=STS, n=16)
    sc.op("sync", None, wait=[Tok(STL, STL.v), Tok(STQ, STQ.v), Tok(STS, STS.v)])

    with nc.Block() as blk:
        sc.emit(blk)



P2_NG = 4
P2_NQ = NTT
P2_SECT = 3
P2_SUB = 2


def p2(kb, stack, sc=None, precast=False):
    nc = kb.nc
    own = sc is None
    if own:
        sc = Sched(kb)
    t_cw = None
    pc_jobs = []
    if precast:
        CW = kb.sem("p9cw")
        for w in ("W1T", "W3T"):
            for ch in range(16):
                pc_jobs.append((kb.d[w.replace("T", "B")], kb.d[w], ch))

        def pc_emit(njobs):
            for _ in range(njobs):
                if not pc_jobs:
                    return
                dst, srcd, ch = pc_jobs.pop(0)
                sc.op("gpsimd",
                      lambda g, dst=dst, srcd=srcd, ch=ch: g.dma_start(
                          out=dst[ch * 128:(ch + 1) * 128, :],
                          in_=srcd[ch * 128:(ch + 1) * 128, :]),
                      inc=CW, n=16)
    MM, EV, AC = (kb.sem(x) for x in ("p2mm", "p2ev", "p2ac"))
    LDC, LDK = (kb.sem(x) for x in ("p2ldc", "p2ldk"))
    LQ = [kb.sem("p2lqa"), kb.sem("p2lqb")]
    STC = kb.sem("p2stc")
    ctx = stack.enter_context
    lat_sb = ctx(nc.sbuf_tensor("p2_lat", [128, 4, S], BF16))
    pe_sb = ctx(nc.sbuf_tensor("p2_pe", [64, S], BF16))
    qs_sb = ctx(nc.sbuf_tensor("p2_qs", [1, 128], F32))
    pex_f = ctx(nc.sbuf_tensor("p2_pex", [1, 9], F32))
    pesq_b = ctx(nc.sbuf_tensor("p2_pesq", [64, TT], BF16))
    kvuK_sb = ctx(nc.sbuf_tensor("p2_kvuK", [128, 4, 4, 128], BF16))
    kvuV_sb = ctx(nc.sbuf_tensor("p2_kvuV", [128, 4, 4, 128], BF16))
    K_sb = ctx(nc.sbuf_tensor("p2_K", [128, 4, S], BF16))
    V_sb = ctx(nc.sbuf_tensor("p2_V", [128, 32, 512], BF16))
    ksq_b = ctx(nc.sbuf_tensor("p2_ksq", [128, TT], BF16))
    k2_f = ctx(nc.sbuf_tensor("p2_k2", [1, 4, 8], F32))
    k2t_f = ctx(nc.sbuf_tensor("p2_k2t", [1, TT], F32))
    bs_f = ctx(nc.sbuf_tensor("p2_bs", [1, 4, 4], F32))
    bs_b = ctx(nc.sbuf_tensor("p2_bsb", [1, 4], BF16))
    bias_sb = ctx(nc.sbuf_tensor("p2_bias", [128, 4], F32))
    qn_sb = ctx(nc.sbuf_tensor("p2_qn", [128, 2, TT], BF16))
    qp_sb = ctx(nc.sbuf_tensor("p2_qp", [64, 2, TT], BF16))
    P_sb = ctx(nc.sbuf_tensor("p2_P", [128, 2, 8, TT], BF16))
    su_f = ctx(nc.sbuf_tensor("p2_su", [1, 2, TT], F32))
    hl_b = ctx(nc.sbuf_tensor("p2_hl", [1, 2, TT], BF16))
    bc_f = ctx(nc.sbuf_tensor("p2_bc", [128, TT], F32))
    ct_b = ctx(nc.sbuf_tensor("p2_ct", [128, TT], BF16))
    ones_sb = ctx(nc.sbuf_tensor("p2_ones", [128, 1], BF16))
    onesr_sb = ctx(nc.sbuf_tensor("p2_onesr", [1, 128], BF16))
    PSC = ctx(nc.psum_tensor("p2_PSC", [128, 4, TT], F32))
    PZ = ctx(nc.psum_tensor("p2_PZ", [128, TT], F32))
    PW = ctx(nc.psum_tensor("p2_PW", [1, TT], F32))
    PBC = ctx(nc.psum_tensor("p2_PBC", [128, TT], F32))
    PBI = ctx(nc.psum_tensor("p2_PBI", [128, 4], F32))

    sc.op("gpsimd", lambda g: g.dma_start(out=ones_sb[:], in_=kb.ones_bf), inc=LDC, n=16)
    sc.op("gpsimd", lambda g: g.dma_start(out=onesr_sb[:], in_=kb.ones_row), inc=LDC, n=16)
    sc.op("gpsimd", lambda g: g.dma_start(out=qs_sb[:], in_=kb.d["QS0"]), inc=LDC, n=16)
    sc.op("gpsimd",
          lambda g: g.dma_start(out=lat_sb[:],
                                in_=kb.d["LAT"][0:LATD, :].rearrange("(c p) t -> p c t", p=128)),
          inc=LDC, n=16)
    sc.op("gpsimd", lambda g: g.dma_start(out=pe_sb[:], in_=kb.d["LAT"][LATD:LATD + 64, :]),
          inc=LDC, n=16)
    t_res = Tok(LDC, LDC.v)

    t_prev = None
    for seg in range(NTT):
        ss = (seg * TT, (seg + 1) * TT)
        t_a = sc.op("scalar",
                    lambda s, ss=ss: s.activation(out=pesq_b[:], in_=pe_sb[:, ss[0]:ss[1]],
                                                  func=AF.Square),
                    wait=[t_res, t_prev], inc=AC)
        t_m = sc.op("tensor",
                    lambda t: t.matmul(PW[:], ones_sb[0:64, :], pesq_b[:], start=True, stop=True),
                    wait=[t_a, t_prev], inc=MM)
        t_prev = sc.op("vector",
                       lambda v, seg=seg: v.tensor_reduce(out=pex_f[:, seg:seg + 1], in_=PW[:],
                                                          axis=AX.X, op=ALU.max),
                       wait=[t_m], inc=EV)
    t_pe2 = sc.op("vector",
                  lambda v: v.tensor_reduce(out=pex_f[:, 8:9], in_=pex_f[:, 0:8],
                                            axis=AX.X, op=ALU.max),
                  wait=[t_prev], inc=EV)

    prev = {
        "grp_done": None,
        "kv_loads": None,
        "expA": [None], "expB": None,
        "ctx_done": None,
        "qload": [None, None],
        "qtok": [None, None],
        "cmm2": None, "cmm3": None,
        "pending_epi": None,
        "st": None,
    }

    for g in range(P2_NG):
        heads = [4 * g + i for i in range(4)]
        def kvld(gp, heads=heads):
            r = []
            for i, h in enumerate(heads):
                r.append(gp.dma_start(
                    out=kvuK_sb[:, :, i, :],
                    in_=kb.d["kvuT"][:, h * 256:h * 256 + 128].rearrange("(c p) o -> p c o", p=128)))
                r.append(gp.dma_start(
                    out=kvuV_sb[:, :, i, :],
                    in_=kb.d["kvuT"][:, h * 256 + 128:h * 256 + 256].rearrange("(c p) o -> p c o", p=128)))
            return r

        t_kv = sc.op("gpsimd", kvld, wait=[prev["grp_done"]], inc=LDK, n=16, ni=8)

        if P2_SECT < 1:
            continue
        kwar = {0: prev["grp_done"], 1: prev["grp_done"]}
        kprev_k2 = None
        kidx = 0
        for i in range(4 if P2_SUB >= 1 else 0):
            for seg in range(NTT):
                ss = (seg * TT, (seg + 1) * TT)
                par2 = kidx % 2
                tgt = PSC[:, par2]

                def kmm(t, i=i, ss=ss, tgt=tgt):
                    for lc in range(4):
                        r = t.matmul(tgt, kvuK_sb[:, lc, i, :], lat_sb[:, lc, ss[0]:ss[1]],
                                     start=(lc == 0), stop=(lc == 3))
                    return r

                t_kmm = sc.op("tensor", kmm,
                              wait=[t_kv, t_res, kwar[par2]],
                              inc=MM)

                def kev_f(v, i=i, ss=ss, tgt=tgt):
                    return v.tensor_copy(out=K_sb[:, i, ss[0]:ss[1]], in_=tgt)

                t_kev = sc.op("vector", kev_f, wait=[t_kmm], inc=EV)

                if P2_SUB < 2:
                    kwar[par2] = t_kev
                    kprev_k2 = t_kev
                    kidx += 1
                    continue
                t_ksq = sc.op("scalar",
                              lambda s, tgt=tgt: s.activation(out=ksq_b[:], in_=tgt,
                                                              func=AF.Square),
                              wait=[t_kmm, t_kev, kprev_k2], inc=AC)
                t_k2m = sc.op("tensor",
                              lambda t: t.matmul(PW[:], ones_sb[:], ksq_b[:],
                                                 start=True, stop=True),
                              wait=[t_ksq, t_pe2, kprev_k2, prev["ctx_done"]], inc=MM)

                def k2fin(v, i=i, seg=seg):
                    return v.tensor_reduce(out=k2_f[:, i, seg:seg + 1], in_=PW[:],
                                           axis=AX.X, op=ALU.max)

                t_k2 = sc.op("vector", k2fin, wait=[t_k2m], inc=EV)
                kwar[par2] = t_k2
                kprev_k2 = t_k2
                kidx += 1
        t_kdone = kprev_k2

        if P2_SECT < 2:
            prev["grp_done"] = t_kdone
            continue
        vev = []
        for kt in range(32):
            def vmm(t, kt=kt):
                for lc in range(4):
                    r = t.matmul(PZ[:], lat_sb[:, lc, kt * 128:(kt + 1) * 128],
                                 kvuV_sb[:, lc].rearrange("p a b -> p (a b)"),
                                 start=(lc == 0), stop=(lc == 3))
                return r

            t_vmm = sc.op("tensor", vmm,
                          wait=[t_kv, t_res, prev["ctx_done"],
                                vev[-1] if vev else prev["grp_done"]],
                          inc=MM)
            t_vev = sc.op("vector",
                          lambda v, kt=kt: v.tensor_copy(out=V_sb[:, kt, :], in_=PZ[:]),
                          wait=[t_vmm], inc=EV)
            vev.append(t_vev)
        t_vdone = vev[-1]

        def bmax(v, heads=heads):
            for i, h in enumerate(heads):
                v.tensor_reduce(out=bs_f[:, 0, i:i + 1], in_=qs_sb[:, h * 8:h * 8 + 8],
                                axis=AX.X, op=ALU.max)
                v.tensor_reduce(out=bs_f[:, 1, i:i + 1], in_=k2_f[:, i, :],
                                axis=AX.X, op=ALU.max)
            v.drain()
            v.tensor_scalar(out=bs_f[:, 1], in0=bs_f[:, 1], scalar1=pex_f[:, 8:9],
                            scalar2=None, op0=ALU.add)
            v.drain()
            return v.tensor_tensor(out=bs_f[:, 2], in0=bs_f[:, 0], in1=bs_f[:, 1],
                                   op=ALU.mult)

        t_bm = sc.op("vector", bmax, wait=[t_kdone, t_res, t_pe2], inc=EV)
        t_bsq = sc.op("scalar",
                      lambda s: s.activation(out=bs_f[:, 3], in_=bs_f[:, 2], func=AF.Sqrt),
                      wait=[t_bm], inc=AC)
        t_bneg = sc.op("vector",
                       lambda v: v.tensor_scalar(out=bs_b[:], in0=bs_f[:, 3],
                                                 scalar1=-1.0, scalar2=None, op0=ALU.mult),
                       wait=[t_bsq], inc=EV)

        def bimm(t):
            for i in range(4):
                r = t.matmul(PBI[:, i:i + 1], onesr_sb[:], bs_b[:, i:i + 1],
                             start=True, stop=True)
            return r

        t_bimm = sc.op("tensor", bimm, wait=[t_bneg, prev["grp_done"]], inc=MM)
        t_bias = sc.op("vector", lambda v: v.tensor_copy(out=bias_sb[:], in_=PBI[:]),
                       wait=[t_bimm], inc=EV)

        if P2_SECT < 3:
            prev["grp_done"] = t_bias
            continue
        units = [(i, qt) for i in range(4) for qt in range(P2_NQ)]

        def emit_qld(u, heads=heads):
            i_, qt_ = units[u]
            h_ = heads[i_]
            par_ = u % 2
            qs2 = (qt_ * TT, (qt_ + 1) * TT)

            def qld(gp, h_=h_, qs2=qs2, par_=par_):
                return [
                    gp.dma_start(out=qn_sb[:, par_],
                                 in_=kb.d["Q0"][h_ * QHD:h_ * QHD + 128, qs2[0]:qs2[1]]),
                    gp.dma_start(out=qp_sb[:, par_],
                                 in_=kb.d["Q0"][h_ * QHD + 128:h_ * QHD + QHD,
                                                qs2[0]:qs2[1]])]

            prev["qtok"][par_] = sc.op("gpsimd", qld, wait=[prev["qload"][par_]],
                                       inc=LQ[par_], n=16, ni=2)

        for u, (i, qt) in enumerate(units):
            h = heads[i]
            qs_ = (qt * TT, (qt + 1) * TT)
            par = u % 2
            if True:
                if u == 0:
                    emit_qld(0)
                    emit_qld(1)
                elif u + 1 < len(units):
                    emit_qld(u + 1)
                t_ql = prev["qtok"][par]

                t_exp4 = [None] * 4
                t_cmm4 = [None] * 4

                def emit_cmm(qq):
                    def cmm8(t, i=i, qq=qq):
                        for jj in range(8):
                            kt = qq * 8 + jj
                            t.matmul(PZ[:], V_sb[:, kt, i * 128:(i + 1) * 128],
                                     P_sb[:, qq % 2, jj],
                                     start=(kt == 0), stop=(kt == 31))
                            r = t.matmul(PW[:], ones_sb[:], P_sb[:, qq % 2, jj],
                                         start=(kt == 0), stop=(kt == 31))
                        return r

                    t_cmm4[qq] = sc.op("tensor", cmm8,
                                       wait=[t_exp4[qq],
                                             prev["ctx_done"] if qq == 0 else None],
                                       inc=MM)

                pairtok = prev.setdefault("evroll", [None, None])
                for q in range(4):
                    qp_ = q % 2
                    t_e = None
                    for rr in range(4):
                        pr = rr % 2

                        def smm(t, i=i, q=q, rr=rr, pr=pr, par=par):
                            for u in range(2):
                                kt = q * 8 + 2 * rr + u
                                t.matmul(PSC[:, 2 * pr + u],
                                         K_sb[:, i, kt * 128:(kt + 1) * 128],
                                         qn_sb[:, par], start=True, stop=False)
                                r = t.matmul(PSC[:, 2 * pr + u],
                                             pe_sb[:, kt * 128:(kt + 1) * 128],
                                             qp_sb[:, par], start=False, stop=True)
                            return r

                        t_s = sc.op("tensor", smm,
                                    wait=[t_ql, t_kdone, t_vdone, pairtok[pr]],
                                    inc=MM)

                        def expf(s, i=i, qp_=qp_, rr=rr, pr=pr):
                            return s.activation(
                                out=P_sb[:, qp_, 2 * rr:2 * rr + 2].rearrange(
                                    "p a t -> p (a t)"),
                                in_=PSC[:, 2 * pr:2 * pr + 2].rearrange("p a t -> p (a t)"),
                                func=AF.Exp, bias=bias_sb[:, i:i + 1])

                        t_e = sc.op("scalar", expf,
                                    wait=[t_s, t_bias,
                                          (t_cmm4[q - 2] if q >= 2 else
                                           prev["cmm2" if q == 0 else "cmm3"])
                                          if rr == 0 else None],
                                    inc=AC)
                        pairtok[pr] = t_e
                        if q == 0 and rr == 1 and prev.get("pending_epi"):
                            prev["pending_epi"]()
                            prev["pending_epi"] = None
                    t_exp4[q] = t_e
                    if q >= 1:
                        emit_cmm(q - 1)
                emit_cmm(3)
                t_last_cmm = t_cmm4[3]
                t_ctx = [t_last_cmm] * 32
                prev["qload"][par] = t_s
                prev["cmm2"], prev["cmm3"] = t_cmm4[2], t_cmm4[3]

                def epi1(v):
                    v.tensor_copy(out=su_f[:, 0], in_=PW[:])
                    v.drain()
                    v.reciprocal(out=su_f[:, 1], in_=su_f[:, 0])
                    v.drain()
                    return emit_hi_lo(v, su_f[:, 1], hl_b[:, 0], hl_b[:, 1])

                t_ep1 = sc.op("vector", epi1, wait=[t_ctx[31]], inc=EV)

                def flush_epi(t_ep1=t_ep1, h=h, qs_=qs_):
                    def epbc(t):
                        t.matmul(PBC[:], onesr_sb[:], hl_b[:, 0], start=True, stop=False)
                        return t.matmul(PBC[:], onesr_sb[:], hl_b[:, 1],
                                        start=False, stop=True)

                    t_eb = sc.op("tensor", epbc, wait=[t_ep1], inc=MM)

                    def epi2(v):
                        v.tensor_copy(out=bc_f[:], in_=PBC[:])
                        v.drain()
                        return v.tensor_tensor(out=ct_b[:], in0=PZ[:], in1=bc_f[:],
                                               op=ALU.mult)

                    t_ep2 = sc.op("vector", epi2, wait=[t_eb, prev["st"]], inc=EV)
                    prev["ctx_done"] = t_ep2
                    t_st = sc.op("sync",
                                 lambda e, h=h, qs_=qs_: e.dma_start(
                                     out=kb.d["CT0"][h * 128:(h + 1) * 128,
                                                     qs_[0]:qs_[1]],
                                     in_=ct_b[:]),
                                 wait=[t_ep2, prev["st"]], inc=STC, n=16)
                    prev["st"] = t_st

                prev["pending_epi"] = flush_epi
        if prev.get("pending_epi"):
            prev["pending_epi"]()
            prev["pending_epi"] = None
        prev["grp_done"] = prev["ctx_done"]
        if precast:
            pc_emit(8)

    if precast:
        pc_emit(99)
        t_cw = Tok(CW, CW.v)
    sc.op("sync", None, wait=[Tok(STC, STC.v), t_cw])
    if own:
        with nc.Block() as blk:
            sc.emit(blk)
    return sc



def p3(kb, stack):
    nc = kb.nc
    sc = Sched(kb)
    MM, EV, AC = (kb.sem(x) for x in ("p3mm", "p3ev", "p3ac"))
    LDW, LDC, LDX = (kb.sem(x) for x in ("p3ldw", "p3ldc", "p3ldx"))
    STX, STW = (kb.sem(x) for x in ("p3stx", "p3stw"))
    ctx = stack.enter_context
    ow_sb = ctx(nc.sbuf_tensor("p3_ow", [128, 16, D], BF16))
    gt_sb = ctx(nc.sbuf_tensor("p3_gt", [128, 16, 8], BF16))
    ebc_sb = ctx(nc.sbuf_tensor("p3_ebc", [128, 8], F32))
    ident_sb = ctx(nc.sbuf_tensor("p3_id", [128, 128], BF16))
    ones_sb = ctx(nc.sbuf_tensor("p3_ones", [128, 1], BF16))
    onesr_sb = ctx(nc.sbuf_tensor("p3_onesr", [1, 128], BF16))
    eps_sb = ctx(nc.sbuf_tensor("p3_eps", [1, 1], F32))
    ct_sb = ctx(nc.sbuf_tensor("p3_ct", [128, 16, TT], BF16))
    xf_sb = ctx(nc.sbuf_tensor("p3_xf", [128, 16, TT], F32))
    r_f = ctx(nc.sbuf_tensor("p3_rf", [128, 16, TT], F32))
    rb_b = ctx(nc.sbuf_tensor("p3_rb", [128, 2, TT], BF16))
    sq_b = ctx(nc.sbuf_tensor("p3_sq", [128, 2, TT], BF16))
    st_f = ctx(nc.sbuf_tensor("p3_st", [1, 4, TT], F32))
    hl_b = ctx(nc.sbuf_tensor("p3_hl", [1, 4, TT], BF16))
    mbc_sb = ctx(nc.sbuf_tensor("p3_mbc", [128, TT], F32))
    rbc_sb = ctx(nc.sbuf_tensor("p3_rbc", [128, TT], F32))
    t1_f = ctx(nc.sbuf_tensor("p3_t1", [128, TT], F32))
    x1b_sb = ctx(nc.sbuf_tensor("p3_x1b", [128, 16, TT], BF16))
    gs_f = ctx(nc.sbuf_tensor("p3_gs", [128, 6, 8], F32))
    wgt_b = ctx(nc.sbuf_tensor("p3_wgt", [128, 8], BF16))
    wgtT_b = ctx(nc.sbuf_tensor("p3_wgtT", [8, 128], BF16))
    PA = ctx(nc.psum_tensor("p3_PA", [128, TT], F32))
    PB = ctx(nc.psum_tensor("p3_PB", [128, TT], F32))
    PM = ctx(nc.psum_tensor("p3_PM", [1, TT], F32))
    PS = ctx(nc.psum_tensor("p3_PS", [1, TT], F32))
    PBC = ctx(nc.psum_tensor("p3_PBC", [128, TT], F32))
    PG = ctx(nc.psum_tensor("p3_PG", [128, 8], F32))
    PT = ctx(nc.psum_tensor("p3_PT", [8, 128], BF16))

    for dst, srcd in ((ow_sb, chunked(kb.d["owT"])), (gt_sb, chunked(kb.d["gT"])),
                      (ebc_sb, kb.d["ebc"]), (ident_sb, kb.ident),
                      (ones_sb, kb.ones_bf), (onesr_sb, kb.ones_row),
                      (eps_sb, kb.epsc)):
        sc.op("gpsimd", lambda g, dst=dst, srcd=srcd: g.dma_start(out=dst[:], in_=srcd),
              inc=LDW, n=16)
    t_w = Tok(LDW, LDW.v)

    prev = {
        "mmA": None, "mmB": None, "apply": None, "pm_free": None,
        "x1_store": None, "wg_store": None, "stat": [None, None],
        "gmm": None, "tr": None, "top": None, "ct_used": None, "xf_used": None,
    }

    for tt in range(NTT):
        cs = (tt * TT, (tt + 1) * TT)
        t_ct = sc.op("gpsimd",
                     lambda g, cs=cs: g.dma_start(out=ct_sb[:], in_=chunked(kb.d["CT0"], cs)),
                     wait=[prev["ct_used"]], inc=LDC, n=16)
        t_xf = sc.op("gpsimd",
                     lambda g, cs=cs: g.dma_start(out=xf_sb[:], in_=chunked(kb.d["xT"], cs)),
                     wait=[prev["xf_used"]], inc=LDX, n=16)

        t_radd = [None] * 16
        t_stat = None
        for oc in range(16):
            tgt = PA if oc % 2 == 0 else PB
            warp = prev["mmA"] if oc % 2 == 0 else prev["mmB"]

            def omm(t, oc=oc, tgt=tgt):
                for cc in range(16):
                    r = t.matmul(tgt[:], ow_sb[:, cc, oc * 128:(oc + 1) * 128],
                                 ct_sb[:, cc], start=(cc == 0), stop=(cc == 15))
                return r

            t_mm = sc.op("tensor", omm, wait=[t_ct, t_w, warp], inc=MM)

            t_radd[oc] = sc.op("vector",
                               lambda v, oc=oc, tgt=tgt: v.tensor_tensor(
                                   out=r_f[:, oc], in0=tgt[:], in1=xf_sb[:, oc], op=ALU.add),
                               wait=[t_mm, t_xf, prev["apply"]], inc=EV)
            if oc % 2 == 0:
                prev["mmA"] = t_radd[oc]
            else:
                prev["mmB"] = t_radd[oc]

            par = oc % 2
            t_rb = sc.op("vector",
                         lambda v, oc=oc, par=par: v.tensor_copy(out=rb_b[:, par], in_=r_f[:, oc]),
                         wait=[t_radd[oc], prev["stat"][par]], inc=EV)
            t_sq = sc.op("scalar",
                         lambda s, oc=oc, par=par: s.activation(out=sq_b[:, par], in_=r_f[:, oc],
                                                                func=AF.Square),
                         wait=[t_radd[oc], prev["stat"][par]], inc=AC)

            def smm(t, oc=oc, par=par):
                t.matmul(PM[:], ones_sb[:], rb_b[:, par], start=(oc == 0), stop=(oc == 15))
                return t.matmul(PS[:], ones_sb[:], sq_b[:, par], start=(oc == 0), stop=(oc == 15))

            t_stat = sc.op("tensor", smm, wait=[t_rb, t_sq, prev["pm_free"] if oc == 0 else None],
                           inc=MM)
            prev["stat"][par] = t_stat

        t_bc, t_statsdone = ln_tail(
            sc, kb, nfeat=D, PMp=PM, PSp=PS, st_f=st_f, hl_b=hl_b,
            onesr_sb=onesr_sb, PBC=PBC, mbc_sb=mbc_sb, rbc_sb=rbc_sb,
            stat_tok=t_stat, bc_war_tok=prev["apply"], MM=MM, EV=EV, AC=AC,
            eps_ap=eps_sb[:, :])
        prev["pm_free"] = t_statsdone

        def apply(v):
            for oc in range(16):
                v.tensor_tensor(out=t1_f[:], in0=r_f[:, oc], in1=mbc_sb[:], op=ALU.subtract)
                v.drain()
                r = v.tensor_tensor(out=x1b_sb[:, oc], in0=t1_f[:], in1=rbc_sb[:], op=ALU.mult)
                v.drain()
            return r

        t_apply = sc.op("vector", apply, wait=[t_bc, prev["x1_store"]] + t_radd, inc=EV)
        prev["apply"] = t_apply
        prev["xf_used"] = t_apply

        t_x1s = sc.op("sync",
                      lambda e, cs=cs: e.dma_start(
                          out=chunked(kb.d["X1"], cs), in_=x1b_sb[:]),
                      wait=[t_apply], inc=STX, n=16)
        prev["x1_store"] = t_x1s

        for ts_ in range(4):
            col = (ts_ * 128, (ts_ + 1) * 128)

            def gmm(t, col=col):
                for cc in range(16):
                    r = t.matmul(PG[:], x1b_sb[:, cc, col[0]:col[1]], gt_sb[:, cc],
                                 start=(cc == 0), stop=(cc == 15))
                return r

            t_gmm = sc.op("tensor", gmm, wait=[t_apply, prev["gmm"]], inc=MM)
            t_sig = sc.op("scalar",
                          lambda s: s.activation(out=gs_f[:, 0], in_=PG[:], func=AF.Sigmoid),
                          wait=[t_gmm], inc=AC)
            prev["gmm"] = t_sig

            def top2(v):
                v.tensor_tensor(out=gs_f[:, 1], in0=gs_f[:, 0], in1=ebc_sb[:], op=ALU.add)
                v.drain()
                v.tensor_reduce(out=t1_f[:, 0:1], in_=gs_f[:, 1], axis=AX.X, op=ALU.max)
                v.drain()
                v.tensor_scalar(out=gs_f[:, 2], in0=gs_f[:, 1], scalar1=t1_f[:, 0:1],
                                scalar2=None, op0=ALU.is_equal)
                v.drain()
                v.scalar_tensor_tensor(out=gs_f[:, 3], in0=gs_f[:, 2], scalar=NEGBIG,
                                       in1=gs_f[:, 1], op0=ALU.mult, op1=ALU.add)
                v.drain()
                v.tensor_reduce(out=t1_f[:, 0:1], in_=gs_f[:, 3], axis=AX.X, op=ALU.max)
                v.drain()
                v.tensor_scalar(out=gs_f[:, 4], in0=gs_f[:, 3], scalar1=t1_f[:, 0:1],
                                scalar2=None, op0=ALU.is_equal)
                v.drain()
                v.tensor_tensor(out=gs_f[:, 5], in0=gs_f[:, 2], in1=gs_f[:, 4], op=ALU.add)
                v.drain()
                return v.tensor_tensor(out=wgt_b[:], in0=gs_f[:, 0], in1=gs_f[:, 5], op=ALU.mult)

            t_top = sc.op("vector", top2, wait=[t_sig, t_w, prev["tr"]], inc=EV)
            prev["top"] = t_top

            t_tr = sc.op("tensor",
                         lambda t: t.transpose(PT[:], wgt_b[:], ident_sb[:]),
                         wait=[t_top, prev["tr"]], inc=MM)
            t_wc = sc.op("vector", lambda v: v.tensor_copy(out=wgtT_b[:], in_=PT[:]),
                         wait=[t_tr, prev["wg_store"]], inc=EV)
            prev["tr"] = t_wc
            t_ws = sc.op("sync",
                         lambda e, cs=cs, col=col: e.dma_start(
                             out=kb.d["WG8"][:, cs[0] + col[0]:cs[0] + col[1]], in_=wgtT_b[:]),
                         wait=[t_wc], inc=STW, n=16)
            prev["wg_store"] = t_ws
        prev["ct_used"] = Tok(MM, prev["gmm"].val if False else MM.v)

    sc.op("sync", None, wait=[Tok(STX, STX.v), Tok(STW, STW.v)])
    with nc.Block() as blk:
        sc.emit(blk)


def chunked3(ap3, e, cp=128):
    return ap3[e:e + 1].rearrange("a (c p) o -> p (a c) o", p=cp)



def p4(kb, stack):
    nc = kb.nc
    sc = Sched(kb)
    MM, EV, AC = (kb.sem(x) for x in ("p4mm", "p4ev", "p4ac"))
    LDW, LD2, LDX, LDG = (kb.sem(x) for x in ("p4ldw", "p4ld2", "p4ldx", "p4ldg"))
    DAC = kb.sem("p4dac")
    DA2 = kb.sem("p4da2")
    ctx = stack.enter_context
    w13_sb = ctx(nc.sbuf_tensor("p4_w13", [128, 16, 2 * HID], BF16))
    w2_sb = ctx(nc.sbuf_tensor("p4_w2", [128, 8, D], BF16))
    hT_sb = ctx(nc.sbuf_tensor("p4_hT", [128, 8, 2048], BF16))
    x1t_sb = ctx(nc.sbuf_tensor("p4_x1t", [128, 16, TT], BF16))
    sg2_f = ctx(nc.sbuf_tensor("p4_sg2", [128, 4, TT], F32))
    wg8_sb = ctx(nc.sbuf_tensor("p4_wg8", [8, S], BF16))
    sel_sb = ctx(nc.sbuf_tensor("p4_sel", [8, 8 * 128], BF16))
    wbc_sb = ctx(nc.sbuf_tensor("p4_wbc", [128, TT], F32))
    sg_f = ctx(nc.sbuf_tensor("p4_sg", [128, 4, TT], F32))
    oe_f = ctx(nc.sbuf_tensor("p4_oe", [128, 2, 4, TT], F32))
    PG = ctx(nc.psum_tensor("p4_PG", [128, 4, TT], F32))
    PU = ctx(nc.psum_tensor("p4_PU", [128, 4, TT], F32))

    sc.op("gpsimd", lambda g: g.dma_start(out=wg8_sb[:], in_=kb.d["WG8"]), inc=LDG, n=16)
    sc.op("gpsimd", lambda g: g.dma_start(out=sel_sb[:], in_=kb.sel8), inc=LDG, n=16)
    t_g = Tok(LDG, LDG.v)

    prev = {
        "p1A": None, "p1B": None,
        "w13_used": None, "w2_used": None,
        "x1_used": None, "wbc_used": None,
        "hT_used": None, "sg": [None, None],
        "oe_dma": [None, None], "oe_fill": None,
        "p2evA": None, "p2evB": None,
    }

    experts = [("sh", None)] + [("ex", e) for e in range(NE)]
    for kind, e in experts:
        if kind == "sh":
            def wld(g):
                return [g.dma_start(out=w13_sb[:, :, 0:HID], in_=chunked(kb.d["sw1T"])),
                        g.dma_start(out=w13_sb[:, :, HID:2 * HID], in_=chunked(kb.d["sw3T"]))]

            def w2ld(g):
                return [g.dma_start(out=w2_sb[:], in_=kb.d["sw2T"].rearrange("(c p) o -> p c o", p=128))]
        else:
            def wld(g, e=e):
                return [g.dma_start(out=w13_sb[:, :, 0:HID], in_=chunked3(kb.d["w1T"], e)),
                        g.dma_start(out=w13_sb[:, :, HID:2 * HID], in_=chunked3(kb.d["w3T"], e))]

            def w2ld(g, e=e):
                return [g.dma_start(out=w2_sb[:], in_=chunked3(kb.d["w2T"], e))]

        t_w13 = sc.op("gpsimd", wld, wait=[prev["w13_used"]], inc=LDW, n=16, ni=2)
        t_w2 = sc.op("gpsimd", w2ld, wait=[prev["w2_used"]], inc=LD2, n=16, ni=1)

        for hf in range(2):
            hmuls = []
            for tt4 in range(4):
                tt = hf * 4 + tt4
                cs = (tt * TT, (tt + 1) * TT)
                t_x1 = sc.op("gpsimd",
                             lambda g, cs=cs: g.dma_start(out=x1t_sb[:],
                                                          in_=chunked(kb.d["X1"], cs)),
                             wait=[prev["x1_used"]], inc=LDX, n=16)
                if kind == "ex":
                    def bcmm(t, e=e, cs=cs):
                        return t.matmul(PG[:, 0], sel_sb[:, e * 128:(e + 1) * 128],
                                        wg8_sb[:, cs[0]:cs[1]], start=True, stop=True)

                    t_bc = sc.op("tensor", bcmm,
                                 wait=[t_g, prev["x1_used"], prev["p1A"]], inc=MM)
                    t_bcc = sc.op("vector",
                                  lambda v: v.tensor_copy(out=wbc_sb[:], in_=PG[:, 0]),
                                  wait=[t_bc], inc=EV)
                else:
                    t_bcc = None
                xin = x1t_sb
                x_tok = t_x1

                for q4 in range(2):
                    def gmm4(t, q4=q4, xin=xin):
                        for k in range(4):
                            hc = q4 * 4 + k
                            for fc in range(16):
                                r = t.matmul(PG[:, k], w13_sb[:, fc, hc * 128:(hc + 1) * 128],
                                             xin[:, fc], start=(fc == 0), stop=(fc == 15))
                        return r

                    t_gm = sc.op("tensor", gmm4,
                                 wait=[x_tok, t_w13, prev["p1A"],
                                       t_bcc if (kind == "ex" and q4 == 0) else None],
                                 inc=MM)

                    def umm4(t, q4=q4, xin=xin):
                        for k in range(4):
                            hc = q4 * 4 + k
                            for fc in range(16):
                                r = t.matmul(PU[:, k],
                                             w13_sb[:, fc, HID + hc * 128:HID + (hc + 1) * 128],
                                             xin[:, fc], start=(fc == 0), stop=(fc == 15))
                        return r

                    t_um = sc.op("tensor", umm4, wait=[x_tok, t_w13, prev["p1B"]], inc=MM)
                    wide = lambda ap: ap.rearrange("p a t -> p (a t)")
                    wbc4 = bass.AP(wbc_sb[:].tensor, wbc_sb[:].offset,
                                   [list(wbc_sb[:].ap[0]), [0, 4], [1, TT]])
                    if kind == "ex":
                        def gsc(v, q4=q4):
                            return v.tensor_tensor(out=wide(sg2_f[:]), in0=wide(PG[:]),
                                                   in1=wbc4, op=ALU.mult)

                        t_gs = sc.op("vector", gsc, wait=[t_gm, t_bcc, prev["sg"][0]], inc=EV)
                        prev["p1A"] = t_gs
                        t_si = sc.op("scalar",
                                     lambda s: s.activation(out=wide(sg_f[:]),
                                                            in_=wide(sg2_f[:]), func=AF.Silu),
                                     wait=[t_gs], inc=AC)

                        def hmul4(v, q4=q4, tt4=tt4):
                            v.tensor_tensor(out=wide(sg2_f[:]), in0=wide(sg_f[:]),
                                            in1=wide(PU[:]), op=ALU.mult)
                            v.drain()
                            return v.tensor_tensor(
                                out=hT_sb[:, q4 * 4:(q4 + 1) * 4, tt4 * TT:(tt4 + 1) * TT],
                                in0=wide(sg2_f[:]), in1=wbc4, op=ALU.mult)

                        t_hm = sc.op("vector", hmul4,
                                     wait=[t_si, t_um, prev["hT_used"]], inc=EV)
                    else:
                        t_si = sc.op("scalar",
                                     lambda s: s.activation(out=wide(sg_f[:]),
                                                            in_=wide(PG[:]), func=AF.Silu),
                                     wait=[t_gm, prev["sg"][0]], inc=AC)
                        prev["p1A"] = t_si

                        def hmul4(v, q4=q4, tt4=tt4):
                            return v.tensor_tensor(
                                out=hT_sb[:, q4 * 4:(q4 + 1) * 4, tt4 * TT:(tt4 + 1) * TT],
                                in0=wide(sg_f[:]), in1=wide(PU[:]), op=ALU.mult)

                        t_hm = sc.op("vector", hmul4,
                                     wait=[t_si, t_um, prev["hT_used"]], inc=EV)
                    prev["p1B"] = t_hm
                    prev["sg"][0] = t_hm
                    hmuls.append(t_hm)
                prev["x1_used"] = hmuls[-1]
            t_p1 = hmuls[-1]

            for tt4 in range(4):
                tt = hf * 4 + tt4
                cs = (tt * TT, (tt + 1) * TT)
                for quad in range(4):
                    opar = quad % 2
                    evs = []
                    for j2 in range(2):
                        j0 = 2 * j2
                        oc0 = quad * 4 + j0
                        bk = (j2 % 2) * 2

                        def omm(t, oc0=oc0, tt4=tt4, bk=bk):
                            for k in range(2):
                                for hc in range(8):
                                    r = t.matmul(PG[:, bk + k],
                                                 w2_sb[:, hc, (oc0 + k) * 128:(oc0 + k + 1) * 128],
                                                 hT_sb[:, hc, tt4 * TT:(tt4 + 1) * TT],
                                                 start=(hc == 0), stop=(hc == 7))
                            return r

                        t_om = sc.op("tensor", omm,
                                     wait=[t_p1, t_w2,
                                           prev["p2evA"] if j2 % 2 == 0 else prev["p2evB"]],
                                     inc=MM)

                        def oev(v, j0=j0, bk=bk, opar=opar):
                            return v.tensor_copy(
                                out=oe_f[:, opar, j0:j0 + 2].rearrange("p a t -> p (a t)"),
                                in_=PG[:, bk:bk + 2].rearrange("p a t -> p (a t)"))

                        t_oe = sc.op("vector", oev,
                                     wait=[t_om, prev["oe_dma"][opar] if j2 == 0 else None],
                                     inc=EV)
                        evs.append(t_oe)
                        if j2 % 2 == 0:
                            prev["p2evA"] = t_oe
                        else:
                            prev["p2evB"] = t_oe

                    def accdma(g, cs=cs, quad=quad, kind=kind, opar=opar):
                        dst = kb.d["ACC"][quad * 512:(quad + 1) * 512,
                                          cs[0]:cs[1]].rearrange("(c p) t -> p c t", p=128)
                        if kind == "sh":
                            return g.dma_start(out=dst, in_=oe_f[:, opar])
                        return g.dma_start(out=dst, in_=oe_f[:, opar], accum_op=ALU.add)

                    t_acc = sc.op("gpsimd", accdma, wait=evs,
                                  inc=(DAC if opar == 0 else DA2), n=16)
                    prev["oe_dma"][opar] = t_acc
            prev["p1A"] = evs[-1]
            prev["hT_used"] = evs[-1]
        prev["w13_used"] = t_p1
        prev["w2_used"] = prev["oe_dma"][1]

    sc.op("sync", None, wait=[Tok(DAC, DAC.v), Tok(DA2, DA2.v)])
    with nc.Block() as blk:
        sc.emit(blk)



def p4b(kb, stack):
    nc = kb.nc
    sc = Sched(kb)
    MM, EV, AC = (kb.sem(x) for x in ("p4bmm", "p4bev", "p4bac"))
    LDX, LDA, LDW = (kb.sem(x) for x in ("p4bldx", "p4blda", "p4bldw"))
    STX = kb.sem("p4bstx")
    ctx = stack.enter_context
    ones_sb = ctx(nc.sbuf_tensor("p4b_ones", [128, 1], BF16))
    onesr_sb = ctx(nc.sbuf_tensor("p4b_onesr", [1, 128], BF16))
    eps_sb = ctx(nc.sbuf_tensor("p4b_eps", [1, 1], F32))
    x1t_sb = ctx(nc.sbuf_tensor("p4b_x1t", [128, 16, TT], BF16))
    ac_f = ctx(nc.sbuf_tensor("p4b_ac", [128, 16, TT], F32))
    r_f = ctx(nc.sbuf_tensor("p4b_rf", [128, 16, TT], F32))
    rb_b = ctx(nc.sbuf_tensor("p4b_rb", [128, 2, TT], BF16))
    sq_b = ctx(nc.sbuf_tensor("p4b_sq", [128, 2, TT], BF16))
    st_f = ctx(nc.sbuf_tensor("p4b_st", [1, 4, TT], F32))
    hl_b = ctx(nc.sbuf_tensor("p4b_hl", [1, 4, TT], BF16))
    mbc_sb = ctx(nc.sbuf_tensor("p4b_mbc", [128, TT], F32))
    rbc_sb = ctx(nc.sbuf_tensor("p4b_rbc", [128, TT], F32))
    t1_f = ctx(nc.sbuf_tensor("p4b_t1", [128, TT], F32))
    x2b_sb = ctx(nc.sbuf_tensor("p4b_x2b", [128, 16, TT], BF16))
    PM = ctx(nc.psum_tensor("p4b_PM", [1, TT], F32))
    PS = ctx(nc.psum_tensor("p4b_PS", [1, TT], F32))
    PBC = ctx(nc.psum_tensor("p4b_PBC", [128, TT], F32))

    for dst, srcd in ((ones_sb, kb.ones_bf), (onesr_sb, kb.ones_row), (eps_sb, kb.epsc)):
        sc.op("gpsimd", lambda g, dst=dst, srcd=srcd: g.dma_start(out=dst[:], in_=srcd),
              inc=LDW, n=16)
    t_w = Tok(LDW, LDW.v)

    prev = {"radd": None, "apply": None, "pm_free": None, "x2_store": None,
            "stat": [None, None]}
    for tt in range(NTT):
        cs = (tt * TT, (tt + 1) * TT)
        t_x1 = sc.op("gpsimd",
                     lambda g, cs=cs: g.dma_start(out=x1t_sb[:], in_=chunked(kb.d["X1"], cs)),
                     wait=[prev["radd"]], inc=LDX, n=16)
        t_ac = sc.op("gpsimd",
                     lambda g, cs=cs: g.dma_start(out=ac_f[:], in_=chunked(kb.d["ACC"], cs)),
                     wait=[prev["radd"]], inc=LDA, n=16)

        def radd(v):
            for oc in range(16):
                r = v.tensor_tensor(out=r_f[:, oc], in0=ac_f[:, oc], in1=x1t_sb[:, oc],
                                    op=ALU.add)
            return r

        t_radd = sc.op("vector", radd, wait=[t_x1, t_ac, prev["apply"]], inc=EV)
        prev["radd"] = t_radd

        t_stat = None
        for oc in range(16):
            par = oc % 2
            t_rb = sc.op("vector",
                         lambda v, oc=oc, par=par: v.tensor_copy(out=rb_b[:, par], in_=r_f[:, oc]),
                         wait=[t_radd, prev["stat"][par]], inc=EV)
            t_sq = sc.op("scalar",
                         lambda s, oc=oc, par=par: s.activation(out=sq_b[:, par], in_=r_f[:, oc],
                                                                func=AF.Square),
                         wait=[t_radd, prev["stat"][par]], inc=AC)

            def smm(t, oc=oc, par=par):
                t.matmul(PM[:], ones_sb[:], rb_b[:, par], start=(oc == 0), stop=(oc == 15))
                return t.matmul(PS[:], ones_sb[:], sq_b[:, par], start=(oc == 0), stop=(oc == 15))

            t_stat = sc.op("tensor", smm,
                           wait=[t_rb, t_sq, t_w, prev["pm_free"] if oc == 0 else None], inc=MM)
            prev["stat"][par] = t_stat

        t_bc, t_statsdone = ln_tail(
            sc, kb, nfeat=D, PMp=PM, PSp=PS, st_f=st_f, hl_b=hl_b,
            onesr_sb=onesr_sb, PBC=PBC, mbc_sb=mbc_sb, rbc_sb=rbc_sb,
            stat_tok=t_stat, bc_war_tok=prev["apply"], MM=MM, EV=EV, AC=AC,
            eps_ap=eps_sb[:, :])
        prev["pm_free"] = t_statsdone

        def apply(v):
            for oc in range(16):
                v.tensor_tensor(out=t1_f[:], in0=r_f[:, oc], in1=mbc_sb[:], op=ALU.subtract)
                v.drain()
                r = v.tensor_tensor(out=x2b_sb[:, oc], in0=t1_f[:], in1=rbc_sb[:], op=ALU.mult)
                v.drain()
            return r

        t_apply = sc.op("vector", apply, wait=[t_bc, t_radd, prev["x2_store"]], inc=EV)
        prev["apply"] = t_apply
        t_x2s = sc.op("sync",
                      lambda e, cs=cs: e.dma_start(out=chunked(kb.d["X2"], cs), in_=x2b_sb[:]),
                      wait=[t_apply], inc=STX, n=16)
        prev["x2_store"] = t_x2s

    sc.op("sync", None, wait=[Tok(STX, STX.v)])
    with nc.Block() as blk:
        sc.emit(blk)



def p5(kb, stack, sc=None):
    nc = kb.nc
    own = sc is None
    if own:
        sc = Sched(kb)
    MM, EV, AC = (kb.sem(x) for x in ("p5mm", "p5ev", "p5ac"))
    LDW, LDX, LDT = (kb.sem(x) for x in ("p5ldw", "p5ldx", "p5ldt"))
    STK, STQ, STV, STS = (kb.sem(x) for x in ("p5stk", "p5stq", "p5stv", "p5sts"))
    ctx = stack.enter_context
    wq_sb = ctx(nc.sbuf_tensor("p5_wq", [128, 16, D], BF16))
    wk_sb = ctx(nc.sbuf_tensor("p5_wk", [128, 16, NKV * HD], BF16))
    wv_sb = ctx(nc.sbuf_tensor("p5_wv", [128, 16, NKV * HD], BF16))
    ones_sb = ctx(nc.sbuf_tensor("p5_ones", [128, 1], BF16))
    x2t_sb = ctx(nc.sbuf_tensor("p5_x2t", [128, 16, TT], BF16))
    rtk_sb = ctx(nc.sbuf_tensor("p5_rtk", [128, 4, TT], BF16))
    rtq_sb = ctx(nc.sbuf_tensor("p5_rtq", [128, 4, TT], BF16))
    rsc_f = ctx(nc.sbuf_tensor("p5_rsc", [128, 2, TT], F32))
    olo_b = ctx(nc.sbuf_tensor("p5_olo", [128, TT], BF16))
    ohi_b = ctx(nc.sbuf_tensor("p5_ohi", [128, TT], BF16))
    ksq_b = ctx(nc.sbuf_tensor("p5_ksq", [128, 2, TT], BF16))
    n2_f = ctx(nc.sbuf_tensor("p5_n2", [1, 96], F32))
    vout_b = ctx(nc.sbuf_tensor("p5_vout", [128, 512], BF16))
    PQA = ctx(nc.psum_tensor("p5_PQA", [128, TT], F32))
    PQB = ctx(nc.psum_tensor("p5_PQB", [128, TT], F32))
    PV = ctx(nc.psum_tensor("p5_PV", [128, 512], F32))
    PW = ctx(nc.psum_tensor("p5_PW", [1, TT], F32))

    for dst, srcd in ((wq_sb, chunked(kb.d["wqT"])), (wk_sb, chunked(kb.d["wkT"])),
                      (wv_sb, chunked(kb.d["wvT"])), (ones_sb, kb.ones_bf)):
        sc.op("gpsimd", lambda g, dst=dst, srcd=srcd: g.dma_start(out=dst[:], in_=srcd),
              inc=LDW, n=16)
    t_w = Tok(LDW, LDW.v)

    prev = {"x2_used": None, "tab_used": None, "ropeA": None, "ropeB": None,
            "pv_ev": None, "norm": None, "pst": None, "vst": None}

    def rope_ops(v, cs, tabs):
        v.tensor_tensor(out=rsc_f[:, 0], in0=PQA[:], in1=tabs[:, 0], op=ALU.mult)
        v.tensor_tensor(out=rsc_f[:, 1], in0=PQB[:], in1=tabs[:, 1], op=ALU.mult)
        v.drain()
        v.tensor_tensor(out=olo_b[:], in0=rsc_f[:, 0], in1=rsc_f[:, 1], op=ALU.subtract)
        v.drain()
        v.tensor_tensor(out=rsc_f[:, 0], in0=PQB[:], in1=tabs[:, 2], op=ALU.mult)
        v.tensor_tensor(out=rsc_f[:, 1], in0=PQA[:], in1=tabs[:, 3], op=ALU.mult)
        v.drain()
        v.tensor_tensor(out=ohi_b[:], in0=rsc_f[:, 0], in1=rsc_f[:, 1], op=ALU.add)
        v.drain()
        v.tensor_tensor(out=ksq_b[:, 0], in0=olo_b[:], in1=olo_b[:], op=ALU.mult)
        return v.tensor_tensor(out=ksq_b[:, 1], in0=ohi_b[:], in1=ohi_b[:], op=ALU.mult)

    for tt in range(NTT):
        cs = (tt * TT, (tt + 1) * TT)
        t_x2 = sc.op("gpsimd",
                     lambda g, cs=cs: g.dma_start(out=x2t_sb[:], in_=chunked(kb.d["X2"], cs)),
                     wait=[prev["x2_used"]], inc=LDX, n=16)

        def tabld(g, cs=cs, tt=tt):
            r = [g.dma_start(out=rtk_sb[:, i], in_=kb.d["r1k"][:, i * S + cs[0]:i * S + cs[1]])
                 for i in range(4)]
            if tt < NTO:
                r += [g.dma_start(out=rtq_sb[:, i],
                                  in_=kb.d["r1q"][:, i * 2048 + cs[0]:i * 2048 + cs[1]])
                      for i in range(4)]
            return r

        t_tab = sc.op("gpsimd", tabld, wait=[prev["tab_used"]], inc=LDT, n=16,
                      ni=8 if tt < NTO else 4)

        jobs = [("k", kp) for kp in range(NKV // 2)]
        if tt < NTO:
            jobs += [("q", p) for p in range(NH // 2)]
        for kind, p in jobs:
            wsb = wk_sb if kind == "k" else wq_sb
            tabs = rtk_sb if kind == "k" else rtq_sb

            def pmm(t, wsb=wsb, p=p):
                for fc in range(16):
                    t.matmul(PQA[:], wsb[:, fc, p * 256:p * 256 + 128], x2t_sb[:, fc],
                             start=(fc == 0), stop=(fc == 15))
                for fc in range(16):
                    r = t.matmul(PQB[:], wsb[:, fc, p * 256 + 128:p * 256 + 256], x2t_sb[:, fc],
                                 start=(fc == 0), stop=(fc == 15))
                return r

            t_mm = sc.op("tensor", pmm, wait=[t_x2, t_w, prev["ropeA"], prev["ropeB"]], inc=MM)
            t_rope = sc.op("vector", lambda v, cs=cs, tabs=tabs: rope_ops(v, cs, tabs),
                           wait=[t_mm, t_tab, prev["pst"], prev["norm"]],
                           inc=EV)
            prev["ropeA"] = t_rope
            prev["ropeB"] = t_rope

            for hh, base in ((2 * p, 0), (2 * p + 1, 64)):
                def nmm(t, base=base):
                    t.matmul(PW[:], ones_sb[base:base + 64, :], ksq_b[base:base + 64, 0],
                             start=True, stop=False)
                    return t.matmul(PW[:], ones_sb[base:base + 64, :], ksq_b[base:base + 64, 1],
                                    start=False, stop=True)

                t_nm = sc.op("tensor", nmm, wait=[t_rope, prev["norm"]], inc=MM)
                slot = (64 + hh * 8 + tt) if kind == "k" else (hh * 4 + tt)
                t_nx = sc.op("vector",
                             lambda v, slot=slot: v.tensor_reduce(
                                 out=n2_f[:, slot:slot + 1], in_=PW[:], axis=AX.X, op=ALU.max),
                             wait=[t_nm], inc=EV)
                prev["norm"] = t_nx

            dname = "K1" if kind == "k" else "Q1"
            stsem = STK if kind == "k" else STQ

            def pst(e, p=p, cs=cs, dname=dname):
                dd = kb.d[dname]
                h0, h1 = 2 * p, 2 * p + 1
                return [
                    e.dma_start(out=dd[h0 * HD:h0 * HD + 64, cs[0]:cs[1]], in_=olo_b[0:64, :]),
                    e.dma_start(out=dd[h0 * HD + 64:(h0 + 1) * HD, cs[0]:cs[1]], in_=ohi_b[0:64, :]),
                    e.dma_start(out=dd[h1 * HD:h1 * HD + 64, cs[0]:cs[1]], in_=olo_b[64:128, :]),
                    e.dma_start(out=dd[h1 * HD + 64:(h1 + 1) * HD, cs[0]:cs[1]], in_=ohi_b[64:128, :])]

            t_st = sc.op("sync", pst, wait=[t_rope], inc=stsem, n=16, ni=4)
            prev["pst"] = t_st

        for j in range(4):
            sub = (cs[0] + j * 128, cs[0] + (j + 1) * 128)

            def vmm(t, j=j):
                for fc in range(16):
                    r = t.matmul(PV[:], x2t_sb[:, fc, j * 128:(j + 1) * 128], wv_sb[:, fc],
                                 start=(fc == 0), stop=(fc == 15))
                return r

            t_vm = sc.op("tensor", vmm, wait=[t_x2, t_w, prev["pv_ev"]], inc=MM)
            t_ve = sc.op("vector", lambda v: v.tensor_copy(out=vout_b[:], in_=PV[:]),
                         wait=[t_vm, prev["vst"]], inc=EV)
            prev["pv_ev"] = t_ve
            t_vs = sc.op("sync",
                         lambda e, sub=sub: e.dma_start(out=kb.d["V1"][sub[0]:sub[1], :],
                                                        in_=vout_b[:]),
                         wait=[t_ve], inc=STV, n=16)
            prev["vst"] = t_vs
        prev["x2_used"] = Tok(MM, MM.v)
        prev["tab_used"] = prev["ropeA"]

    t_ns = sc.op("sync", lambda e: e.dma_start(out=kb.d["QS1"], in_=n2_f[:]),
                 wait=[prev["norm"]], inc=STS, n=16)
    sc.op("sync", None, wait=[Tok(STK, STK.v), Tok(STQ, STQ.v), Tok(STV, STV.v),
                              Tok(STS, STS.v)])
    if own:
        with nc.Block() as blk:
            sc.emit(blk)
    return sc



def p7(kb, stack):
    nc = kb.nc
    sc = Sched(kb)
    MM, EV, AC = (kb.sem(x) for x in ("p7mm", "p7ev", "p7ac"))
    LDC = kb.sem("p7ldc")
    LQ = [kb.sem("p7lqa"), kb.sem("p7lqb")]
    STC = kb.sem("p7stc")
    ctx = stack.enter_context
    k_sb = ctx(nc.sbuf_tensor("p7_k", [128, 4, S], BF16))
    v_sb = ctx(nc.sbuf_tensor("p7_v", [128, 32, 512], BF16))
    qs_sb = ctx(nc.sbuf_tensor("p7_qs", [1, 96], F32))
    bs_f = ctx(nc.sbuf_tensor("p7_bs", [1, 4, 16], F32))
    bs_b = ctx(nc.sbuf_tensor("p7_bsb", [1, 16], BF16))
    bias_sb = ctx(nc.sbuf_tensor("p7_bias", [128, 16], F32))
    q_sb = ctx(nc.sbuf_tensor("p7_q", [128, 2, TT], BF16))
    P_sb = ctx(nc.sbuf_tensor("p7_P", [128, 2, 8, TT], BF16))
    su_f = ctx(nc.sbuf_tensor("p7_su", [1, 2, TT], F32))
    hl_b = ctx(nc.sbuf_tensor("p7_hl", [1, 2, TT], BF16))
    bc_f = ctx(nc.sbuf_tensor("p7_bc", [128, TT], F32))
    ct_b = ctx(nc.sbuf_tensor("p7_ct", [128, TT], BF16))
    ones_sb = ctx(nc.sbuf_tensor("p7_ones", [128, 1], BF16))
    onesr_sb = ctx(nc.sbuf_tensor("p7_onesr", [1, 128], BF16))
    PSC = ctx(nc.psum_tensor("p7_PSC", [128, 4, TT], F32))
    PZ = ctx(nc.psum_tensor("p7_PZ", [128, TT], F32))
    PW = ctx(nc.psum_tensor("p7_PW", [1, TT], F32))
    PBC = ctx(nc.psum_tensor("p7_PBC", [128, TT], F32))
    PBI = ctx(nc.psum_tensor("p7_PBI", [128, 16], F32))

    sc.op("gpsimd", lambda g: g.dma_start(out=ones_sb[:], in_=kb.ones_bf), inc=LDC, n=16)
    sc.op("gpsimd", lambda g: g.dma_start(out=onesr_sb[:], in_=kb.ones_row), inc=LDC, n=16)
    sc.op("gpsimd", lambda g: g.dma_start(out=qs_sb[:], in_=kb.d["QS1"]), inc=LDC, n=16)
    sc.op("gpsimd",
          lambda g: g.dma_start(out=k_sb[:], in_=kb.d["K1"].rearrange("(c p) t -> p c t", p=128)),
          inc=LDC, n=16)
    sc.op("gpsimd",
          lambda g: g.dma_start(out=v_sb[:], in_=kb.d["V1"].rearrange("(kt p) o -> p kt o", p=128)),
          inc=LDC, n=16)
    t_res = Tok(LDC, LDC.v)

    def bmax(v):
        for h in range(NH):
            v.tensor_reduce(out=bs_f[:, 0, h:h + 1], in_=qs_sb[:, h * 4:h * 4 + 4],
                            axis=AX.X, op=ALU.max)
            kvh = h // 4
            v.tensor_reduce(out=bs_f[:, 1, h:h + 1], in_=qs_sb[:, 64 + kvh * 8:64 + kvh * 8 + 8],
                            axis=AX.X, op=ALU.max)
        v.drain()
        return v.tensor_tensor(out=bs_f[:, 2], in0=bs_f[:, 0], in1=bs_f[:, 1], op=ALU.mult)

    t_bm = sc.op("vector", bmax, wait=[t_res], inc=EV)
    t_bsq = sc.op("scalar",
                  lambda s: s.activation(out=bs_f[:, 3], in_=bs_f[:, 2], func=AF.Sqrt),
                  wait=[t_bm], inc=AC)
    t_bneg = sc.op("vector",
                   lambda v: v.tensor_scalar(out=bs_b[:], in0=bs_f[:, 3], scalar1=-1.0,
                                             scalar2=None, op0=ALU.mult),
                   wait=[t_bsq], inc=EV)

    def bimm(t):
        for h in range(NH):
            r = t.matmul(PBI[:, h:h + 1], onesr_sb[:], bs_b[:, h:h + 1], start=True, stop=True)
        return r

    t_bimm = sc.op("tensor", bimm, wait=[t_bneg], inc=MM)
    t_bias = sc.op("vector", lambda v: v.tensor_copy(out=bias_sb[:], in_=PBI[:]),
                   wait=[t_bimm], inc=EV)

    prev = {"ctx_done": None, "qload": [None, None], "qtok": [None, None],
            "cmm2": None, "cmm3": None, "pending_epi": None, "st": None}
    units = [(h, qt) for h in range(NH) for qt in range(NTO)]

    def emit_qld(u):
        h_, qt_ = units[u]
        par_ = u % 2
        qs2 = (qt_ * TT, (qt_ + 1) * TT)
        prev["qtok"][par_] = sc.op(
            "gpsimd",
            lambda g, h_=h_, qs2=qs2, par_=par_: g.dma_start(
                out=q_sb[:, par_], in_=kb.d["Q1"][h_ * HD:(h_ + 1) * HD, qs2[0]:qs2[1]]),
            wait=[prev["qload"][par_]], inc=LQ[par_], n=16)

    for uu, (h, qt) in enumerate(units):
        kvh = h // 4
        if True:
            qs_ = (qt * TT, (qt + 1) * TT)
            par = uu % 2
            if uu == 0:
                emit_qld(0)
                emit_qld(1)
            elif uu + 1 < len(units):
                emit_qld(uu + 1)
            t_ql = prev["qtok"][par]

            t_exp4 = [None] * 4
            t_cmm4 = [None] * 4

            def emit_cmm(qq):
                def cmm8(t, kvh=kvh, qq=qq):
                    for jj in range(8):
                        kt = qq * 8 + jj
                        t.matmul(PZ[:], v_sb[:, kt, kvh * 128:(kvh + 1) * 128],
                                 P_sb[:, qq % 2, jj],
                                 start=(kt == 0), stop=(kt == 31))
                        r = t.matmul(PW[:], ones_sb[:], P_sb[:, qq % 2, jj],
                                     start=(kt == 0), stop=(kt == 31))
                    return r

                t_cmm4[qq] = sc.op("tensor", cmm8,
                                   wait=[t_exp4[qq],
                                         prev["ctx_done"] if qq == 0 else None],
                                   inc=MM)

            pairtok = prev.setdefault("evroll", [None, None])
            for q in range(4):
                qp_ = q % 2
                t_e = None
                for rr in range(4):
                    pr = rr % 2

                    def smm(t, kvh=kvh, q=q, rr=rr, pr=pr, par=par):
                        for u in range(2):
                            kt = q * 8 + 2 * rr + u
                            r = t.matmul(PSC[:, 2 * pr + u],
                                         k_sb[:, kvh, kt * 128:(kt + 1) * 128],
                                         q_sb[:, par], start=True, stop=True)
                        return r

                    t_s = sc.op("tensor", smm,
                                wait=[t_ql, t_res, pairtok[pr]],
                                inc=MM)

                    def expf(s, h=h, qp_=qp_, rr=rr, pr=pr):
                        return s.activation(
                            out=P_sb[:, qp_, 2 * rr:2 * rr + 2].rearrange("p a t -> p (a t)"),
                            in_=PSC[:, 2 * pr:2 * pr + 2].rearrange("p a t -> p (a t)"),
                            func=AF.Exp, bias=bias_sb[:, h:h + 1])

                    t_e = sc.op("scalar", expf,
                                wait=[t_s, t_bias,
                                      (t_cmm4[q - 2] if q >= 2 else
                                       prev["cmm2" if q == 0 else "cmm3"])
                                      if rr == 0 else None],
                                inc=AC)
                    pairtok[pr] = t_e
                    if q == 0 and rr == 1 and prev.get("pending_epi"):
                        prev["pending_epi"]()
                        prev["pending_epi"] = None
                t_exp4[q] = t_e
                if q >= 1:
                    emit_cmm(q - 1)
            emit_cmm(3)
            t_ctx = [t_cmm4[3]] * 32
            prev["qload"][par] = t_s
            prev["cmm2"], prev["cmm3"] = t_cmm4[2], t_cmm4[3]

            def epi1(v):
                v.tensor_copy(out=su_f[:, 0], in_=PW[:])
                v.drain()
                v.reciprocal(out=su_f[:, 1], in_=su_f[:, 0])
                v.drain()
                return emit_hi_lo(v, su_f[:, 1], hl_b[:, 0], hl_b[:, 1])

            t_ep1 = sc.op("vector", epi1, wait=[t_ctx[31]], inc=EV)

            def flush_epi(t_ep1=t_ep1, h=h, qs_=qs_):
                def epbc(t):
                    t.matmul(PBC[:], onesr_sb[:], hl_b[:, 0], start=True, stop=False)
                    return t.matmul(PBC[:], onesr_sb[:], hl_b[:, 1], start=False, stop=True)

                t_eb = sc.op("tensor", epbc, wait=[t_ep1], inc=MM)

                def epi2(v):
                    v.tensor_copy(out=bc_f[:], in_=PBC[:])
                    v.drain()
                    return v.tensor_tensor(out=ct_b[:], in0=PZ[:], in1=bc_f[:], op=ALU.mult)

                t_ep2 = sc.op("vector", epi2, wait=[t_eb, prev["st"]], inc=EV)
                prev["ctx_done"] = t_ep2
                t_st = sc.op("sync",
                             lambda e, h=h, qs_=qs_: e.dma_start(
                                 out=kb.d["CT1"][h * 128:(h + 1) * 128, qs_[0]:qs_[1]],
                                 in_=ct_b[:]),
                             wait=[t_ep2, prev["st"]], inc=STC, n=16)
                prev["st"] = t_st

            prev["pending_epi"] = flush_epi

    if prev.get("pending_epi"):
        prev["pending_epi"]()
        prev["pending_epi"] = None
    sc.op("sync", None, wait=[Tok(STC, STC.v)])
    with nc.Block() as blk:
        sc.emit(blk)



def p8(kb, stack):
    nc = kb.nc
    sc = Sched(kb)
    MM, EV, AC = (kb.sem(x) for x in ("p8mm", "p8ev", "p8ac"))
    LDW, LDC, LDX = (kb.sem(x) for x in ("p8ldw", "p8ldc", "p8ldx"))
    STX = kb.sem("p8stx")
    ctx = stack.enter_context
    ow_sb = ctx(nc.sbuf_tensor("p8_ow", [128, 16, D], BF16))
    ones_sb = ctx(nc.sbuf_tensor("p8_ones", [128, 1], BF16))
    onesr_sb = ctx(nc.sbuf_tensor("p8_onesr", [1, 128], BF16))
    eps_sb = ctx(nc.sbuf_tensor("p8_eps", [1, 1], F32))
    ct_sb = ctx(nc.sbuf_tensor("p8_ct", [128, 16, TT], BF16))
    x2t_sb = ctx(nc.sbuf_tensor("p8_x2t", [128, 16, TT], BF16))
    r_f = ctx(nc.sbuf_tensor("p8_rf", [128, 16, TT], F32))
    rb_b = ctx(nc.sbuf_tensor("p8_rb", [128, 2, TT], BF16))
    sq_b = ctx(nc.sbuf_tensor("p8_sq", [128, 2, TT], BF16))
    st_f = ctx(nc.sbuf_tensor("p8_st", [1, 4, TT], F32))
    hl_b = ctx(nc.sbuf_tensor("p8_hl", [1, 4, TT], BF16))
    mbc_sb = ctx(nc.sbuf_tensor("p8_mbc", [128, TT], F32))
    rbc_sb = ctx(nc.sbuf_tensor("p8_rbc", [128, TT], F32))
    t1_f = ctx(nc.sbuf_tensor("p8_t1", [128, TT], F32))
    x3b_sb = ctx(nc.sbuf_tensor("p8_x3b", [128, 16, TT], BF16))
    PA = ctx(nc.psum_tensor("p8_PA", [128, TT], F32))
    PB = ctx(nc.psum_tensor("p8_PB", [128, TT], F32))
    PM = ctx(nc.psum_tensor("p8_PM", [1, TT], F32))
    PS = ctx(nc.psum_tensor("p8_PS", [1, TT], F32))
    PBC = ctx(nc.psum_tensor("p8_PBC", [128, TT], F32))

    for dst, srcd in ((ow_sb, chunked(kb.d["woT"])), (ones_sb, kb.ones_bf),
                      (onesr_sb, kb.ones_row), (eps_sb, kb.epsc)):
        sc.op("gpsimd", lambda g, dst=dst, srcd=srcd: g.dma_start(out=dst[:], in_=srcd),
              inc=LDW, n=16)
    t_w = Tok(LDW, LDW.v)

    prev = {"mmA": None, "mmB": None, "apply": None, "pm_free": None,
            "x3_store": None, "stat": [None, None], "ct_used": None, "x2_used": None}
    for tt in range(NTO):
        cs = (tt * TT, (tt + 1) * TT)
        t_ct = sc.op("gpsimd",
                     lambda g, cs=cs: g.dma_start(out=ct_sb[:], in_=chunked(kb.d["CT1"], cs)),
                     wait=[prev["ct_used"]], inc=LDC, n=16)
        t_x2 = sc.op("gpsimd",
                     lambda g, cs=cs: g.dma_start(out=x2t_sb[:], in_=chunked(kb.d["X2"], cs)),
                     wait=[prev["x2_used"]], inc=LDX, n=16)

        t_radd = [None] * 16
        t_stat = None
        for oc in range(16):
            tgt = PA if oc % 2 == 0 else PB
            warp = prev["mmA"] if oc % 2 == 0 else prev["mmB"]

            def omm(t, oc=oc, tgt=tgt):
                for cc in range(16):
                    r = t.matmul(tgt[:], ow_sb[:, cc, oc * 128:(oc + 1) * 128],
                                 ct_sb[:, cc], start=(cc == 0), stop=(cc == 15))
                return r

            t_mm = sc.op("tensor", omm, wait=[t_ct, t_w, warp], inc=MM)
            t_radd[oc] = sc.op("vector",
                               lambda v, oc=oc, tgt=tgt: v.tensor_tensor(
                                   out=r_f[:, oc], in0=tgt[:], in1=x2t_sb[:, oc], op=ALU.add),
                               wait=[t_mm, t_x2, prev["apply"]], inc=EV)
            if oc % 2 == 0:
                prev["mmA"] = t_radd[oc]
            else:
                prev["mmB"] = t_radd[oc]
            par = oc % 2
            t_rb = sc.op("vector",
                         lambda v, oc=oc, par=par: v.tensor_copy(out=rb_b[:, par], in_=r_f[:, oc]),
                         wait=[t_radd[oc], prev["stat"][par]], inc=EV)
            t_sq = sc.op("scalar",
                         lambda s, oc=oc, par=par: s.activation(out=sq_b[:, par], in_=r_f[:, oc],
                                                                func=AF.Square),
                         wait=[t_radd[oc], prev["stat"][par]], inc=AC)

            def smm(t, oc=oc, par=par):
                t.matmul(PM[:], ones_sb[:], rb_b[:, par], start=(oc == 0), stop=(oc == 15))
                return t.matmul(PS[:], ones_sb[:], sq_b[:, par], start=(oc == 0), stop=(oc == 15))

            t_stat = sc.op("tensor", smm,
                           wait=[t_rb, t_sq, prev["pm_free"] if oc == 0 else None], inc=MM)
            prev["stat"][par] = t_stat

        t_bc, t_statsdone = ln_tail(
            sc, kb, nfeat=D, PMp=PM, PSp=PS, st_f=st_f, hl_b=hl_b,
            onesr_sb=onesr_sb, PBC=PBC, mbc_sb=mbc_sb, rbc_sb=rbc_sb,
            stat_tok=t_stat, bc_war_tok=prev["apply"], MM=MM, EV=EV, AC=AC,
            eps_ap=eps_sb[:, :])
        prev["pm_free"] = t_statsdone

        def apply(v):
            for oc in range(16):
                v.tensor_tensor(out=t1_f[:], in0=r_f[:, oc], in1=mbc_sb[:], op=ALU.subtract)
                v.drain()
                r = v.tensor_tensor(out=x3b_sb[:, oc], in0=t1_f[:], in1=rbc_sb[:], op=ALU.mult)
                v.drain()
            return r

        t_apply = sc.op("vector", apply, wait=[t_bc, prev["x3_store"]] + t_radd, inc=EV)
        prev["apply"] = t_apply
        prev["x2_used"] = t_apply
        prev["ct_used"] = Tok(MM, MM.v)
        t_x3s = sc.op("sync",
                      lambda e, cs=cs: e.dma_start(out=chunked(kb.d["X3"], cs), in_=x3b_sb[:]),
                      wait=[t_apply], inc=STX, n=16)
        prev["x3_store"] = t_x3s

    sc.op("sync", None, wait=[Tok(STX, STX.v)])
    with nc.Block() as blk:
        sc.emit(blk)



def p9(kb, stack, sc=None, precast=True):
    nc = kb.nc
    own = sc is None
    if own:
        sc = Sched(kb)
    MM, EV, AC = (kb.sem(x) for x in ("p9mm", "p9ev", "p9ac"))
    LDH = kb.sem("p9ldh")
    LB = [kb.sem("p9lba"), kb.sem("p9lbb")]
    LA = [kb.sem("p9laa"), kb.sem("p9lab")]
    STB = kb.sem("p9stb")
    SA = [kb.sem("p9saa"), kb.sem("p9sab")]
    ctx = stack.enter_context
    h64_sb = ctx(nc.sbuf_tensor("p9_h64", [64, 64], BF16))
    h128_sb = ctx(nc.sbuf_tensor("p9_h128", [128, 128], BF16))
    w2in = ctx(nc.sbuf_tensor("p9_w2in", [64, 2, D], BF16))
    fout = ctx(nc.sbuf_tensor("p9_fout", [64, D], BF16))
    fsin = ctx(nc.sbuf_tensor("p9_fsin", [128, 2, D], BF16))
    wa_b = ctx(nc.sbuf_tensor("p9_wa", [128, 2, 512], BF16))
    PN = ctx(nc.psum_tensor("p9_PN", [128, 2, 512], F32))

    sc.op("gpsimd", lambda g: g.dma_start(out=h64_sb[:], in_=kb.h64), inc=LDH, n=16)
    sc.op("gpsimd", lambda g: g.dma_start(out=h128_sb[:], in_=kb.h128), inc=LDH, n=16)
    t_h = Tok(LDH, LDH.v)

    prev = {"bmm": [None, None], "bst": None, "bev": None, "ain": None,
            "aev": [None, None], "ast": [None, None], "amm": [None, None]}
    t_cw = None
    if precast:
        CW = kb.sem("p9cw")
        sc.op("gpsimd", lambda g: g.dma_start(out=kb.d["W1B"], in_=kb.d["W1T"]), inc=CW, n=16)
        sc.op("gpsimd", lambda g: g.dma_start(out=kb.d["W3B"], in_=kb.d["W3T"]), inc=CW, n=16)
        t_cw = Tok(CW, CW.v)
    for j in range(128):
        par = j % 2

        def bload(g, j=j, par=par):
            srcv = kb.d["W2T"].rearrange("(a q) o -> q a o", q=128)[j:j + 1].rearrange(
                "z a o -> (z a) o")
            return g.dma_start(out=w2in[:, par], in_=srcv)

        t_bl = sc.op("gpsimd", bload, wait=[prev["bmm"][par]], inc=LB[par], n=16)

        t_be = None
        for seg in range(4):
            sp = seg % 2

            def bmm(t, par=par, seg=seg, sp=sp):
                return t.matmul(PN[0:64, sp], h64_sb[:],
                                w2in[:, par, seg * 512:(seg + 1) * 512], start=True, stop=True)

            t_bm = sc.op("tensor", bmm,
                         wait=[t_bl, t_h, prev["bev"][sp] if isinstance(prev["bev"], list) else prev["bev"]],
                         inc=MM)
            if seg == 3:
                prev["bmm"][par] = t_bm
            t_be = sc.op("vector",
                         lambda v, seg=seg, sp=sp: v.tensor_copy(
                             out=fout[:, seg * 512:(seg + 1) * 512], in_=PN[0:64, sp]),
                         wait=[t_bm, prev["bst"] if seg <= 1 else None], inc=EV)
            if not isinstance(prev["bev"], list):
                prev["bev"] = [None, None]
            prev["bev"][sp] = t_be

        def bstore(e, j=j):
            dstv = kb.d["FS"].rearrange("(a q) o -> q a o", q=128)[j:j + 1].rearrange(
                "z a o -> (z a) o")
            return e.dma_start(out=dstv, in_=fout[:])

        t_bs = sc.op("sync", bstore, wait=[t_be], inc=STB, n=16)
        prev["bst"] = t_bs
    t_ball = Tok(STB, STB.v)

    SCW = float(1.0 / np.sqrt(FF))
    for c in range(64):
        par = c % 2
        t_al = sc.op("gpsimd",
                     lambda g, c=c, par=par: g.dma_start(
                         out=fsin[:, par], in_=kb.d["FS"][c * 128:(c + 1) * 128, :]),
                     wait=[t_ball, prev["amm"][par]], inc=LA[par], n=16)
        for j4 in range(4):
            pj = j4 % 2

            def amm(t, par=par, j4=j4, pj=pj):
                return t.matmul(PN[:, pj], h128_sb[:], fsin[:, par, j4 * 512:(j4 + 1) * 512],
                                start=True, stop=True)

            t_am = sc.op("tensor", amm,
                         wait=[t_al, t_h, prev["aev"][pj], t_ball], inc=MM)
            if j4 == 3:
                prev["amm"][par] = t_am

            t_ae = sc.op("scalar",
                         lambda s, pj=pj: s.mul(wa_b[:, pj], PN[:, pj], SCW),
                         wait=[t_am, prev["ast"][pj]], inc=AC)
            prev["aev"][pj] = t_ae

            def astore(e, c=c, j4=j4, pj=pj):
                dstv = kb.d["W2X"][j4 * 4:(j4 + 1) * 4, c].rearrange("a p o -> p a o")
                return e.dma_start(out=dstv, in_=wa_b[:, pj].rearrange("p (a o) -> p a o", a=4))

            t_as = sc.op("sync", astore, wait=[t_ae], inc=SA[pj], n=16)
            prev["ast"][pj] = t_as

    sc.op("sync", None, wait=[Tok(SA[0], SA[0].v), Tok(SA[1], SA[1].v), t_cw])
    if own:
        with nc.Block() as blk:
            sc.emit(blk)
    return sc



def p10(kb, stack):
    nc = kb.nc
    sc = Sched(kb)
    MM, EV, AC = (kb.sem(x) for x in ("p10mm", "p10ev", "p10ac"))
    LDW, LW3, LD2, LDX, LDC = (kb.sem(x) for x in ("p10ldw", "p10lw3", "p10ld2", "p10ldx", "p10ldc"))
    LW1P = [kb.sem("p10lw1a"), kb.sem("p10lw1b")]
    LW3P = [kb.sem("p10lw3a"), kb.sem("p10lw3b")]
    STO = kb.sem("p10sto")
    ctx = stack.enter_context
    w1g_sb = ctx(nc.sbuf_tensor("p10_w1g", [128, 2, 16, 256], BF16))
    w3g_sb = ctx(nc.sbuf_tensor("p10_w3g", [128, 2, 16, 256], BF16))
    w2x_sb = ctx(nc.sbuf_tensor("p10_w2x", [128, 2, 64, 128], BF16))
    hT_sb = ctx(nc.sbuf_tensor("p10_hT", [128, 64, TT], BF16))
    x3t_sb = ctx(nc.sbuf_tensor("p10_x3t", [128, 16, TT], BF16))
    sg_f = ctx(nc.sbuf_tensor("p10_sg", [128, 1, TT], F32))
    r_f = ctx(nc.sbuf_tensor("p10_rf", [128, 16, TT], F32))
    rb_b = ctx(nc.sbuf_tensor("p10_rb", [128, 2, TT], BF16))
    sq_b = ctx(nc.sbuf_tensor("p10_sq", [128, 2, TT], BF16))
    st_f = ctx(nc.sbuf_tensor("p10_st", [1, 4, TT], F32))
    hl_b = ctx(nc.sbuf_tensor("p10_hl", [1, 4, TT], BF16))
    mbc_sb = ctx(nc.sbuf_tensor("p10_mbc", [128, TT], F32))
    rbc_sb = ctx(nc.sbuf_tensor("p10_rbc", [128, TT], F32))
    t1_f = ctx(nc.sbuf_tensor("p10_t1", [128, TT], F32))
    o_f = ctx(nc.sbuf_tensor("p10_of", [128, 1, TT], F32))
    ones_sb = ctx(nc.sbuf_tensor("p10_ones", [128, 1], BF16))
    onesr_sb = ctx(nc.sbuf_tensor("p10_onesr", [1, 128], BF16))
    eps_sb = ctx(nc.sbuf_tensor("p10_eps", [1, 1], F32))
    PA = ctx(nc.psum_tensor("p10_PA", [128, TT], F32))
    PB = ctx(nc.psum_tensor("p10_PB", [128, TT], F32))
    PC = ctx(nc.psum_tensor("p10_PC", [128, TT], F32))
    PD = ctx(nc.psum_tensor("p10_PD", [128, TT], F32))
    PM = ctx(nc.psum_tensor("p10_PM", [1, TT], F32))
    PS = ctx(nc.psum_tensor("p10_PS", [1, TT], F32))
    PBC = ctx(nc.psum_tensor("p10_PBC", [128, TT], F32))

    for dst, srcd in ((ones_sb, kb.ones_bf), (onesr_sb, kb.ones_row), (eps_sb, kb.epsc)):
        sc.op("gpsimd", lambda g, dst=dst, srcd=srcd: g.dma_start(out=dst[:], in_=srcd),
              inc=LDC, n=16)
    t_w = Tok(LDC, LDC.v)

    prev = {"p1A": [None, None], "p1B": [None, None], "wh_used": [None, None],
            "x3_used": None,
            "hT_used": None, "sg": [None, None], "w2_used": [None, None],
            "radd": [None, None], "apply": None, "pm_free": None,
            "stat": [None, None], "ost": None, "out_ev": [None, None]}
    for tt in range(NTO):
        cs = (tt * TT, (tt + 1) * TT)
        t_x3 = sc.op("gpsimd",
                     lambda g, cs=cs: g.dma_start(out=x3t_sb[:], in_=chunked(kb.d["X3"], cs)),
                     wait=[prev["x3_used"]], inc=LDX, n=16)

        t_hm = None
        for hg in range(16):
            for hf in range(2):
                ph = (hg * 2 + hf) % 2
                c0 = hg * 512 + hf * 256
                t_w1 = sc.op("gpsimd",
                             lambda g, c0=c0, ph=ph: g.dma_start(
                                 out=w1g_sb[:, ph], in_=chunked(kb.d["W1B"], (c0, c0 + 256))),
                             wait=[prev["wh_used"][ph]], inc=LW1P[ph], n=16)
                t_w3 = sc.op("gpsimd",
                             lambda g, c0=c0, ph=ph: g.dma_start(
                                 out=w3g_sb[:, ph], in_=chunked(kb.d["W3B"], (c0, c0 + 256))),
                             wait=[prev["wh_used"][ph]], inc=LW3P[ph], n=16)
                for k in range(2):
                    hc8 = hf * 2 + k
                    hc = hg * 4 + hc8
                    par = hc % 2
                    G = PA if par == 0 else PC
                    U = PB if par == 0 else PD

                    def gmm(t, k=k, ph=ph, G=G):
                        for fc in range(16):
                            r = t.matmul(G[:], w1g_sb[:, ph, fc, k * 128:(k + 1) * 128],
                                         x3t_sb[:, fc], start=(fc == 0), stop=(fc == 15))
                        return r

                    t_gm = sc.op("tensor", gmm, wait=[t_x3, t_w1, prev["p1A"][par]], inc=MM)

                    def umm(t, k=k, ph=ph, U=U):
                        for fc in range(16):
                            r = t.matmul(U[:], w3g_sb[:, ph, fc, k * 128:(k + 1) * 128],
                                         x3t_sb[:, fc], start=(fc == 0), stop=(fc == 15))
                        return r

                    t_um = sc.op("tensor", umm, wait=[t_x3, t_w3, prev["p1B"][par]], inc=MM)
                    t_si = sc.op("scalar",
                                 lambda s, par=par, G=G: s.activation(out=sg_f[:, 0], in_=G[:],
                                                                      func=AF.Silu),
                                 wait=[t_gm, prev["sg"][0]], inc=AC)
                    prev["p1A"][par] = t_si
                    t_hm = sc.op("vector",
                                 lambda v, hc=hc, U=U: v.tensor_tensor(
                                     out=hT_sb[:, hc], in0=sg_f[:, 0], in1=U[:], op=ALU.mult),
                                 wait=[t_si, t_um, prev["hT_used"] if hc <= 1 else None], inc=EV)
                    prev["p1B"][par] = t_hm
                    prev["sg"][0] = t_hm
                prev["wh_used"][ph] = t_um
        t_p1 = prev["p1B"][1]

        t_radds = []
        t_stat = None
        for oc in range(16):
            par = oc % 2
            tgt = PA if par == 0 else PB
            t_w2 = sc.op("gpsimd",
                         lambda g, oc=oc, par=par: g.dma_start(
                             out=w2x_sb[:, par], in_=kb.d["W2X"][oc].rearrange("a p o -> p a o")),
                         wait=[prev["w2_used"][par], t_p1 if oc <= 1 else None],
                         inc=(LD2 if par == 0 else LW3), n=16)

            def omm(t, tgt=tgt, par=par):
                for hc in range(64):
                    r = t.matmul(tgt[:], w2x_sb[:, par, hc, :], hT_sb[:, hc],
                                 start=(hc == 0), stop=(hc == 63))
                return r

            t_om = sc.op("tensor", omm,
                         wait=[t_w2, t_p1, prev["radd"][par]], inc=MM)
            prev["w2_used"][par] = t_om

            t_ra = sc.op("vector",
                         lambda v, oc=oc, tgt=tgt: v.tensor_tensor(
                             out=r_f[:, oc], in0=tgt[:], in1=x3t_sb[:, oc], op=ALU.add),
                         wait=[t_om, prev["apply"]], inc=EV)
            prev["radd"][par] = t_ra
            t_radds.append(t_ra)
            t_rb = sc.op("vector",
                         lambda v, oc=oc, par=par: v.tensor_copy(out=rb_b[:, par], in_=r_f[:, oc]),
                         wait=[t_ra, prev["stat"][par]], inc=EV)
            t_sq = sc.op("scalar",
                         lambda s, oc=oc, par=par: s.activation(out=sq_b[:, par], in_=r_f[:, oc],
                                                                func=AF.Square),
                         wait=[t_ra, prev["stat"][par]], inc=AC)

            def smm(t, oc=oc, par=par):
                t.matmul(PM[:], ones_sb[:], rb_b[:, par], start=(oc == 0), stop=(oc == 15))
                return t.matmul(PS[:], ones_sb[:], sq_b[:, par], start=(oc == 0), stop=(oc == 15))

            t_stat = sc.op("tensor", smm,
                           wait=[t_rb, t_sq, t_w, prev["pm_free"] if oc == 0 else None], inc=MM)
            prev["stat"][par] = t_stat
        prev["x3_used"] = t_radds[-1]
        prev["hT_used"] = prev["w2_used"][1]
        prev["p1A"] = [prev["radd"][0], prev["p1A"][1]]
        prev["p1B"] = [prev["radd"][1], prev["p1B"][1]]

        t_bc, t_statsdone = ln_tail(
            sc, kb, nfeat=D, PMp=PM, PSp=PS, st_f=st_f, hl_b=hl_b,
            onesr_sb=onesr_sb, PBC=PBC, mbc_sb=mbc_sb, rbc_sb=rbc_sb,
            stat_tok=t_stat, bc_war_tok=prev["apply"], MM=MM, EV=EV, AC=AC,
            eps_ap=eps_sb[:, :])
        prev["pm_free"] = t_statsdone

        for oc in range(16):
            def apply1(v, oc=oc):
                v.tensor_tensor(out=t1_f[:], in0=r_f[:, oc], in1=mbc_sb[:], op=ALU.subtract)
                v.drain()
                r = v.tensor_tensor(out=o_f[:, 0], in0=t1_f[:], in1=rbc_sb[:], op=ALU.mult)
                v.drain()
                return r

            t_ap = sc.op("vector", apply1,
                         wait=[t_bc, t_radds[oc], prev["out_ev"][0]], inc=EV)
            prev["apply"] = t_ap
            t_os = sc.op("sync",
                         lambda e, oc=oc, cs=cs: e.dma_start(
                             out=kb.d["OUT"][oc * 128:(oc + 1) * 128, cs[0]:cs[1]],
                             in_=o_f[:, 0]),
                         wait=[t_ap], inc=STO, n=16)
            prev["out_ev"][0] = t_os

    sc.op("sync", None, wait=[Tok(STO, STO.v)])
    with nc.Block() as blk:
        sc.emit(blk)



PHASE_FNS = {"p0": p0, "p2": p2, "p3": p3, "p4": p4, "p4b": p4b,
             "p5": p5, "p7": p7, "p8": p8, "p9": p9, "p10": p10}

from contextlib import ExitStack


def build_program(phases):
    kb = KB(phases)
    skip = set()
    for ph in phases:
        if ph in skip:
            continue
        if ph == "p5" and "p9" in phases:
            with ExitStack() as stack:
                sc = Sched(kb)
                p5(kb, stack, sc)
                sc9 = Sched(kb)
                p9(kb, stack, sc9, precast=not ("p2" in phases))
                sc.merge(sc9)
                with kb.nc.Block() as blk:
                    sc.emit(blk)
            skip.add("p9")
            continue
        if ph == "p2" and "p9" in phases and "p5" in phases:
            with ExitStack() as stack:
                p2(kb, stack, precast=True)
            continue
        with ExitStack() as stack:
            PHASE_FNS[ph](kb, stack)
    return kb



def _cont(a):
    return np.ascontiguousarray(a)


def host_prep(inputs):
    x = np.asarray(inputs["L0_attn_hidden_states"])
    f32 = np.float32
    shared = {
        "qwT": None,
        "qwTpe": None,
        "kvdT": _cont(np.asarray(inputs["L0_attn_kv_down_w"]).T),
        "lng": _cont(np.asarray(inputs["L0_attn_ln_g"]).reshape(4, 128).T.astype(f32)),
        "lnb": _cont(np.asarray(inputs["L0_attn_ln_b"]).reshape(4, 128).T.astype(f32)),
        "kvuT": _cont(np.asarray(inputs["L0_attn_kv_up_w"]).T),
        "owT": _cont(np.asarray(inputs["L0_attn_out_w"]).T),
        "gT": _cont(np.asarray(inputs["L0_ffn_gate"]).T),
        "ebc": _cont(np.broadcast_to(np.asarray(inputs["L0_ffn_expert_bias"]), (128, 8)).astype(f32)),
        "w1T": _cont(np.transpose(np.asarray(inputs["L0_ffn_w1"]), (0, 2, 1))),
        "w2T": _cont(np.transpose(np.asarray(inputs["L0_ffn_w2"]), (0, 2, 1))),
        "w3T": _cont(np.transpose(np.asarray(inputs["L0_ffn_w3"]), (0, 2, 1))),
        "sw1T": _cont(np.asarray(inputs["L0_ffn_shared_w1"]).T),
        "sw2T": _cont(np.asarray(inputs["L0_ffn_shared_w2"]).T),
        "sw3T": _cont(np.asarray(inputs["L0_ffn_shared_w3"]).T),
        "wqT": None, "wkT": None,
        "wvT": _cont(np.asarray(inputs["L1_attn_Wv"]).T),
        "woT": _cont(np.asarray(inputs["L1_attn_Wo"]).T),
        "W1T": _cont(np.asarray(inputs["L1_ffn_W1"]).T),
        "W3T": _cont(np.asarray(inputs["L1_ffn_W3"]).T),
        "W2T": _cont(np.asarray(inputs["L1_ffn_W2"]).T),
    }
    def pair_cols(nheads):
        idx = []
        for p in range(nheads // 2):
            h0, h1 = 2 * p, 2 * p + 1
            idx += [np.arange(h0 * HD, h0 * HD + 64), np.arange(h1 * HD, h1 * HD + 64),
                    np.arange(h0 * HD + 64, (h0 + 1) * HD), np.arange(h1 * HD + 64, (h1 + 1) * HD)]
        return np.concatenate(idx)

    shared["wqT"] = _cont(np.asarray(inputs["L1_attn_Wq"]).T[:, pair_cols(NH)])
    shared["wkT"] = _cont(np.asarray(inputs["L1_attn_Wk"]).T[:, pair_cols(NKV)])
    qwT_full = np.asarray(inputs["L0_attn_q_w"]).T
    idx_nope = np.concatenate([np.arange(h * QHD, h * QHD + DQK) for h in range(NH)])
    shared["qwT"] = _cont(qwT_full[:, idx_nope])
    idx_pe = []
    for p in range(NH // 2):
        h0, h1 = 2 * p, 2 * p + 1
        idx_pe += [np.arange(h0 * QHD + 128, h0 * QHD + 160),
                   np.arange(h1 * QHD + 128, h1 * QHD + 160),
                   np.arange(h0 * QHD + 160, h0 * QHD + 192),
                   np.arange(h1 * QHD + 160, h1 * QHD + 192)]
    shared["qwTpe"] = _cont(qwT_full[:, np.concatenate(idx_pe)])
    inv0 = (1.0 / (10000.0 ** (np.arange(0, DPE, 2, dtype=f32) / DPE))).astype(f32)
    inv1 = (1.0 / (10000.0 ** (np.arange(0, HD, 2, dtype=f32) / HD))).astype(f32)
    percore = []
    for c in range(8):
        b, h = c // 2, c % 2
        perm = np.concatenate([np.arange(2048) + h * 2048, np.arange(2048) + (1 - h) * 2048])
        pos = perm.astype(f32)
        fr0 = pos[:, None] * inv0[None, :]
        c0, s0 = np.cos(fr0).astype(f32), np.sin(fr0).astype(f32)
        fr1 = pos[:, None] * inv1[None, :]
        emb = np.concatenate([fr1, fr1], axis=1)
        c1, s1 = np.cos(emb).astype(f32), np.sin(emb).astype(f32)
        d = {
            "xT": _cont(x[b].T[:, perm].astype(f32)),
            "r0k": _cont(np.concatenate([c0.T, s0.T], axis=1)),
            "r0q": _cont(np.concatenate([
                np.concatenate([c0.T, c0.T], axis=0) * SC0,
                np.concatenate([s0.T, s0.T], axis=0) * SC0], axis=1)),
            "r1k": _cont(np.concatenate([
                np.concatenate([c1.T[0:64], c1.T[0:64]], axis=0),
                np.concatenate([s1.T[0:64], s1.T[0:64]], axis=0),
                np.concatenate([c1.T[64:128], c1.T[64:128]], axis=0),
                np.concatenate([s1.T[64:128], s1.T[64:128]], axis=0)], axis=1)),
            "r1q": _cont(np.concatenate([
                np.concatenate([c1.T[0:64, 0:2048], c1.T[0:64, 0:2048]], axis=0),
                np.concatenate([s1.T[0:64, 0:2048], s1.T[0:64, 0:2048]], axis=0),
                np.concatenate([c1.T[64:128, 0:2048], c1.T[64:128, 0:2048]], axis=0),
                np.concatenate([s1.T[64:128, 0:2048], s1.T[64:128, 0:2048]], axis=0)], axis=1) * SC1),
        }
        percore.append(d)
    return shared, percore


LAUNCH_PLAN = [list(ALL_PHASES)]


def run_launches(inputs, launch_plan=None, extra_outputs=()):
    launch_plan = launch_plan or LAUNCH_PLAN
    shared, percore = host_prep(inputs)
    carry = [dict() for _ in range(8)]
    for phases in launch_plan:
        kb = build_program(phases)
        in_maps = []
        for c in range(8):
            m = {}
            for name in kb.in_names:
                if name in shared:
                    m[name] = shared[name]
                elif name in percore[c]:
                    m[name] = percore[c][name]
                else:
                    m[name] = carry[c][name]
            in_maps.append(m)
        res = run_bass_kernel_spmd(kb.nc, in_maps, core_ids=list(range(8)))
        for c in range(8):
            for name in kb.out_names:
                carry[c][name] = res.results[c][name]
    return carry


def kernel(**inputs):
    carry = run_launches(inputs)
    out = np.zeros((B, S, D), np.float32)
    for c in range(8):
        b, h = c // 2, c % 2
        out[b, h * 2048:(h + 1) * 2048, :] = carry[c]["OUT"].T
    return out
